# version 39
# baseline (speedup 1.0000x reference)
import numpy as np
import ml_dtypes
import contextlib
import concourse.bass as bass
import concourse.mybir as mybir
from concourse.bass_utils import run_bass_kernel_spmd

F32 = mybir.dt.float32
BF16 = mybir.dt.bfloat16
I32 = mybir.dt.int32
AF = mybir.ActivationFunctionType
ALU = mybir.AluOpType
AX = mybir.AxisListType

T = 4096
D = 1024
NH = 8
DO = 2048
LR = 64
NI = 4096
NTC = T // 128
NDC = D // 128
CM = 12582912.0
ISCL = 0.0625

ENGINES = ("sync", "tensor", "vector", "scalar", "gpsimd")


class Buf:
    __slots__ = ("w", "r")

    def __init__(self):
        self.w = None
        self.r = []

    def handover(self):
        return ([self.w] if self.w is not None else []) + list(self.r)


class Prog:
    def __init__(self):
        self.ops = []
        self.final_waits = []

    def add(self, eng, fn, reads=(), writes=(), extra=(), dma=False):
        deps = []
        for b in writes:
            if b.w is not None:
                deps.append(b.w)
            deps.extend(b.r)
        for b in reads:
            if b.w is not None:
                deps.append(b.w)
        deps.extend(d for d in extra if d is not None)
        i = len(self.ops)
        assert all(d < i for d in deps)
        self.ops.append({"eng": eng, "fn": fn, "deps": sorted(set(deps)),
                         "dma": dma})
        for b in writes:
            b.w = i
            b.r = []
        for b in reads:
            b.r.append(i)
        return i

    def add_final_wait(self, dep):
        self.final_waits.append(dep)

    NSLOT = 12

    def emit(self, nc, block, ctx):
        ops = self.ops
        NS = self.NSLOT
        ekey = [None] * len(ops)
        qidx = {}
        for i, op in enumerate(ops):
            if op["dma"]:
                q = "q_" + op["eng"]
                j = qidx.get(q, 0)
                qidx[q] = j + 1
                ekey[i] = (q, j % NS)
            else:
                ekey[i] = op["eng"]
        for i, op in enumerate(ops):
            op["deps"] = [d for d in op["deps"]
                          if not (ekey[d] == "tensor" and ekey[i] == "tensor")]
        needed = set(i for i, op in enumerate(ops) if op["dma"])
        for op in ops:
            needed.update(op["deps"])
        needed.update(self.final_waits)
        sems = {}
        for e in ENGINES:
            sems[e] = ctx.enter_context(nc.semaphore("s_" + e))
            for s in range(NS):
                k = ("q_" + e, s)
                sems[k] = ctx.enter_context(nc.semaphore(f"sq_{e}_{s}"))
        val = [0] * len(ops)
        cnt = {k: 0 for k in sems}
        for i, op in enumerate(ops):
            if i in needed:
                cnt[ekey[i]] += 16 if op["dma"] else 1
                val[i] = cnt[ekey[i]]
        per_eng = {e: [] for e in ENGINES}
        for i, op in enumerate(ops):
            per_eng[op["eng"]].append(i)

        last_val = dict(cnt)

        def emit_engine(e, eng):
            maxw = {}

            def w(k, v):
                if v > 0 and maxw.get(k, 0) < v:
                    eng.wait_ge(sems[k], v)
                    maxw[k] = v

            for i in per_eng[e]:
                op = ops[i]
                if op["dma"]:
                    w(ekey[i], val[i] - 16)
                for d in op["deps"]:
                    assert val[d] > 0, (i, d)
                    w(ekey[d], val[d])
                inst = op["fn"](eng)
                if i in needed:
                    inst.then_inc(sems[ekey[i]], 16 if op["dma"] else 1)
            for d in self.final_waits:
                w(ekey[d], val[d])
            if e == "sync":
                for k, v in last_val.items():
                    if isinstance(k, tuple):
                        w(k, v)

        @block.sync
        def _(eng):
            emit_engine("sync", eng)

        @block.tensor
        def _(eng):
            emit_engine("tensor", eng)

        @block.vector
        def _(eng):
            emit_engine("vector", eng)

        @block.scalar
        def _(eng):
            emit_engine("scalar", eng)

        @block.gpsimd
        def _(eng):
            emit_engine("gpsimd", eng)


def build_nc(debug=False):
    nc = bass.Bass("TRN2", target_bir_lowering=False,
                   detect_race_conditions=False)

    x_d = nc.dram_tensor("x", [T, D], F32, kind="ExternalInput").ap()
    wq_d = nc.dram_tensor("Wq", [DO, D], F32, kind="ExternalInput").ap()
    wkv_d = nc.dram_tensor("Wkv", [LR, D], F32, kind="ExternalInput").ap()
    wku_d = nc.dram_tensor("Wku", [DO, LR], F32, kind="ExternalInput").ap()
    wvu_d = nc.dram_tensor("Wvu", [DO, LR], F32, kind="ExternalInput").ap()
    wo_d = nc.dram_tensor("Wo", [D, DO], F32, kind="ExternalInput").ap()
    wg_d = nc.dram_tensor("Wg", [NI, D], F32, kind="ExternalInput").ap()
    wu_d = nc.dram_tensor("Wu", [NI, D], F32, kind="ExternalInput").ap()
    wd_d = nc.dram_tensor("Wd", [D, NI], F32, kind="ExternalInput").ap()
    id_d = nc.dram_tensor("ident", [128, 128], BF16, kind="ExternalInput").ap()
    y_d = nc.dram_tensor("y", [T, D], F32, kind="ExternalOutput").ap()

    sk = "ExternalOutput" if debug else "Internal"
    xnqT_d = nc.dram_tensor("xnqT_d", [D, T], BF16, kind=sk).ap()
    aoT_d = nc.dram_tensor("aoT_d", [DO, T], BF16, kind=sk).ap()
    hqT_d = nc.dram_tensor("hqT_d", [NI, T], BF16, kind=sk).ap()
    wqT_d = nc.dram_tensor("wqT_d", [D, DO], BF16, kind=sk).ap()
    woT_d = nc.dram_tensor("woT_d", [DO, D], BF16, kind=sk).ap()
    wgT_d = nc.dram_tensor("wgT_d", [D, NI], BF16, kind=sk).ap()
    wuT_d = nc.dram_tensor("wuT_d", [D, NI], BF16, kind=sk).ap()
    wdT_d = nc.dram_tensor("wdT_d", [NI, D], BF16, kind=sk).ap()
    if debug:
        xnT_dbg = nc.dram_tensor("xnT_dbg", [D, T], BF16, kind="ExternalOutput").ap()
        kvT_dbg = nc.dram_tensor("kvT_dbg", [LR, T], BF16, kind="ExternalOutput").ap()
        sc_dbg = nc.dram_tensor("sc_dbg", [128, 128], F32, kind="ExternalOutput").ap()

    p = Prog()
    with contextlib.ExitStack() as ctx:
        sbuf = lambda n, s, dt: ctx.enter_context(nc.sbuf_tensor(n, s, dt))
        ident = sbuf("identsb", [128, 128], BF16)
        arA = sbuf("arA", [128, 32768], BF16)
        arB = sbuf("arB", [128, 38912], BF16)
        arC = sbuf("arC", [128, 10240], F32)
        rsc = sbuf("rsc", [128, 3072], BF16)
        t32 = sbuf("t32", [128, 1024], I32)
        smf = sbuf("smf", [128, 272], F32)
        ones_t = sbuf("ones_t", [128, 128], F32)
        wkuT = sbuf("wkuT", [LR, DO], BF16)
        wvuT = sbuf("wvuT", [LR, DO], BF16)
        p1b = sbuf("p1b", [128, 4096], BF16)
        psF = ctx.enter_context(nc.psum_tensor("psF", [128, 3584], F32))
        psT0 = ctx.enter_context(nc.psum_tensor("psT0", [128, 1024], BF16))
        bank = lambda i: psF[:, i * 512:(i + 1) * 512]
        trB = psF[:, 3072:3584].bitcast(BF16)
        BKap = lambda t: psT0[:, 0:1024] if t == 0 else trB
        _tb = [0]

        def nb():
            t = _tb[0]
            _tb[0] ^= 1
            return t
        pv_ps = lambda sub, a, b: psF[:, 1536 + sub * 512 + a: 1536 + sub * 512 + b]

        b_ident, b_ones, b_eps = Buf(), Buf(), Buf()
        b_arA = Buf()
        b_qT, b_kT, b_vA = Buf(), Buf(), Buf()
        b_wqst = [Buf(), Buf()]
        b_pT = [Buf(), Buf(), Buf()]
        b_atto = Buf()
        b_aoTh = [Buf(), Buf()]
        b_wuqT = Buf()
        b_wg8 = [Buf() for _ in range(8)]
        b_xnqst = [Buf(), Buf()]
        b_hqneg = [Buf() for _ in range(4)]
        b_woT = Buf()
        b_hqst = [Buf(), Buf()]
        b_aost = [Buf(), Buf()]
        b_xst = [Buf(), Buf()]
        b_xn32 = [Buf(), Buf()]
        b_h32 = [[Buf() for _ in range(8)] for _ in range(2)]
        b_nn, b_ee, b_qq = Buf(), Buf(), Buf()
        b_sil = [Buf(), Buf(), Buf(), Buf()]
        b_t1 = Buf()
        b_x4 = [Buf(), Buf()]
        b_t32 = Buf()
        b_stp = [[Buf() for _ in range(4)] for _ in range(2)]
        b_wstA = [Buf(), Buf(), Buf()]
        b_sxv2 = [Buf(), Buf()]
        b_sgv, b_suv, b_sdv, b_mu = Buf(), Buf(), Buf(), Buf()
        b_scw3 = [Buf(), Buf(), Buf()]
        b_rsx, b_rsh, b_bvec, b_repl, b_acc = Buf(), Buf(), Buf(), Buf(), Buf()
        b_kvT, b_wkvT, b_wkuT, b_wvuT = Buf(), Buf(), Buf(), Buf()
        b_p1r = [Buf(), Buf(), Buf(), Buf()]
        b_bank = [Buf() for _ in range(7)]
        b_psTh = [Buf(), Buf()]
        b_d = {n: Buf() for n in
               ("xnqT", "aoT", "hqT", "wqT", "woT", "wgT", "wuT", "wdT")}

        rsx = smf[:, 0:32]
        rsh = smf[:, 32:64]
        bvec = smf[:, 64:68]
        repl = smf[:, 68:72]
        stp = [[smf[:, o + i: o + i + 1] for i in range(4)] for o in (72, 76)]
        epsv = smf[:, 80:81]
        sxv2 = [smf[:, 81 + i: 82 + i] for i in range(2)]
        sgv = smf[:, 83:84]
        suv = smf[:, 84:85]
        sdv = smf[:, 85:86]
        scw3 = [smf[:, 86 + i: 87 + i] for i in range(3)]
        mu11 = smf[:, 89:90]
        acc32 = smf[:, 96:112]
        sgall = smf[:, 208:240]
        suall = smf[:, 240:272]
        sdall = acc32
        sdall = smf[:, 112:144]
        b_sdall = Buf()
        b_sgall, b_suall = Buf(), Buf()
        accA_sl = lambda s: smf[:, 112 + s: 113 + s]
        b_accA = Buf()
        wstA = [arC[:, 4096 + b * 1024: 4096 + (b + 1) * 1024] for b in (0, 1, 2)]

        xnT_sl = lambda dc, t0, n: arA[:, dc * T + t0: dc * T + t0 + n]
        wgqT_sl = lambda dc, i0, n: arA[:, dc * NI + i0: dc * NI + i0 + n]
        woT_sl = lambda oc, n0, n: arA[:, oc * D + n0: oc * D + n0 + n]
        hq_st = [arA[:, 16384 + b * 4096: 16384 + (b + 1) * 4096] for b in (0, 1)]
        ao_st = [arA[:, 24576 + b * 2048: 24576 + (b + 1) * 2048] for b in (0, 1)]

        qT = arB[:, 0:8192]
        kT = arB[:, 8192:16384]
        vA = arB[:, 16384:24608]
        wqst = [arB[:, 24608 + b * 2048: 24608 + (b + 1) * 2048] for b in (0, 1)]
        pT = [arB[:, 28704 + b * 512: 28704 + (b + 1) * 512] for b in range(3)]
        att_o = arB[:, 30240:31264]
        wkvT = arB[:, 30240:30752]
        aoTh = [arB[:, 31264 + b * 1024: 31264 + (b + 1) * 1024] for b in (0, 1)]
        kvT = arB[0:LR, 33312:37408]
        wuqT_sl = lambda dc, i0, n: arB[:, dc * NI + i0: dc * NI + i0 + n]
        wdqT_sl = lambda ic, n0, n: arB[:, ic * D + n0: ic * D + n0 + n]
        xnq_st = [arB[:, 32768 + b * 1024: 32768 + (b + 1) * 1024] for b in (0, 1)]
        hq_neg = arB[:, 34816:38912]

        xst = [arC[:, b * 1024:(b + 1) * 1024] for b in (0, 1)]
        xn32p = [arC[:, 2048 + b * 1024: 2048 + (b + 1) * 1024] for b in (0, 1)]
        h32p = [arC[:, b * 4096:(b + 1) * 4096] for b in (0, 1)]
        x4st = [arC[:, 2048 + b * 1024: 2048 + (b + 1) * 1024] for b in (0, 1)]
        out32 = [arC[:, b * 1024:(b + 1) * 1024] for b in (0, 1)]
        nn32 = rsc[:, 0:1024]
        ee32 = rsc[:, 1024:2048]
        qq32 = rsc[:, 2048:3072]
        t1_32 = arC[:, 4096:5120]
        sil32 = [arC[:, 8192 + b * 512: 8192 + (b + 1) * 512] for b in range(4)]

        p.add("sync", lambda e: e.dma_start(out=ident[:], in_=id_d),
              writes=[b_ident], dma=True)
        p.add("vector", lambda e: e.memset(epsv, 1e-5), writes=[b_eps])
        p.add("vector", lambda e: e.memset(ones_t[:, :], 1.0), writes=[b_ones])
        p.add("gpsimd", lambda e: e.dma_start(out=p1b[0:LR, 0:D], in_=wkv_d),
              writes=[b_p1r[0], b_p1r[1], b_p1r[2], b_p1r[3]], dma=True)
        for dc in range(NDC):
            p.add("tensor", lambda e, dc=dc: e.transpose(
                psT0[:, dc * 128: dc * 128 + LR],
                p1b[0:LR, dc * 128:(dc + 1) * 128], ident[0:LR, 0:LR]),
                reads=[b_p1r[dc // 2], b_ident], writes=[b_psTh[0]])
        for dc in range(NDC):
            p.add("vector", lambda e, dc=dc: e.tensor_copy(
                wkvT[:, dc * LR:(dc + 1) * LR], psT0[:, dc * 128: dc * 128 + LR]),
                reads=[b_psTh[0]], writes=[b_wkvT])

        lodB = [arB[:, 8192 + b * 2048: 8192 + (b + 1) * 2048] for b in (0, 1, 2)]
        stgB = [arB[:, 16384 + b * 1024: 16384 + (b + 1) * 1024] for b in range(4)]
        _sg = [0]
        b_ldB = [Buf(), Buf(), Buf()]
        b_stB = [Buf(), Buf(), Buf(), Buf()]
        prep_tiles = []
        for r in range(DO // 128):
            prep_tiles.append(("up", wku_d, wkuT, b_wkuT, r, 0))
        for r in range(DO // 128):
            prep_tiles.append(("up", wvu_d, wvuT, b_wvuT, r, 0))
        for r in range(DO // 128):
            prep_tiles.append(("pl", wq_d, wqT_d, b_d["wqT"], r, 1024))
        for r in range(D // 128):
            prep_tiles.append(("pl", wo_d, woT_d, b_d["woT"], r, 2048))
        _pc = [0]
        _pd = [0]

        def emit_prep_dma():
            if _pd[0] >= len(prep_tiles):
                return
            i = _pd[0]
            _pd[0] += 1
            kind, srct, dst, b_dst, r, cols = prep_tiles[i]
            par = i % 3
            n = LR if kind == "up" else cols
            p.add("gpsimd", lambda e, r=r, srct=srct, par=par, n=n: e.dma_start(
                out=lodB[par][:, 0:n], in_=srct[r * 128:(r + 1) * 128, :]),
                writes=[b_ldB[par]], dma=True)

        def emit_prep_comp():
            if _pc[0] >= _pd[0] or _pc[0] >= len(prep_tiles):
                return
            i = _pc[0]
            _pc[0] += 1
            kind, srct, dst, b_dst, r, cols = prep_tiles[i]
            par = i % 3
            if kind == "up":
                tb = nb()
                p.add("tensor", lambda e, par=par, tb=tb: e.transpose(
                    BKap(tb)[0:LR, 0:128], lodB[par][:, 0:LR],
                    ident[:]),
                    reads=[b_ldB[par], b_ident], writes=[b_psTh[tb]])
                p.add("vector", lambda e, r=r, dst=dst, tb=tb: e.tensor_copy(
                    dst[:, r * 128:(r + 1) * 128],
                    BKap(tb)[0:LR, 0:128]),
                    reads=[b_psTh[tb]], writes=[b_dst])
                return
            for g in range(cols // 1024):
                tb = nb()
                for dc in range(8):
                    cc = g * 8 + dc
                    p.add("tensor", lambda e, cc=cc, dc=dc, par=par, tb=tb: e.transpose(
                        BKap(tb)[:, dc * 128:(dc + 1) * 128],
                        lodB[par][:, cc * 128:(cc + 1) * 128], ident[:]),
                        reads=[b_ldB[par], b_ident], writes=[b_psTh[tb]])
                sg = _sg[0] % 4
                _sg[0] += 1
                p.add("vector", lambda e, sg=sg, tb=tb: e.tensor_copy(
                    stgB[sg][:, :], BKap(tb)[:, :]),
                    reads=[b_psTh[tb]], writes=[b_stB[sg]])
                p.add("sync", lambda e, r=r, g=g, dst=dst, sg=sg: e.dma_start(
                    out=dst.rearrange("(cc pp) o -> pp cc o", pp=128)[:, g * 8:(g + 1) * 8, r * 128:(r + 1) * 128],
                    in_=stgB[sg][:, :].rearrange("pp (cc t) -> pp cc t", t=128)),
                    reads=[b_stB[sg]], writes=[b_dst], dma=True)

        def emit_wqst(h):
            wb = h % 2
            p.add("scalar" if h > 0 else "sync", lambda e, h=h, wb=wb: e.dma_start(
                out=wqst[wb][:, :].rearrange("pp (dc o) -> pp dc o", o=256),
                in_=wqT_d.rearrange("(dc pp) o -> pp dc o", pp=128)[:, :, h * 256:(h + 1) * 256]),
                reads=[b_d["wqT"]], writes=[b_wqst[wb]], dma=True)

        def emit_qgrp(h, oc, ts_):
            wb = h % 2
            i6 = oc * 8 + ts_
            g = (1 + i6 % 5) if h == 0 else i6 % 6
            vec = (i6 % 2 == 0) if h == 0 else True
            for dc in range(NDC):
                p.add("tensor", lambda e, oc=oc, ts_=ts_, dc=dc, g=g, wb=wb: e.matmul(
                    bank(g),
                    wqst[wb][:, dc * 256 + oc * 128: dc * 256 + (oc + 1) * 128],
                    xnT_sl(dc, ts_ * 512, 512),
                    start=(dc == 0), stop=(dc == NDC - 1)),
                    reads=[b_wqst[wb], b_arA], writes=[b_bank[g]])
            p.add("vector" if vec else "scalar",
                  (lambda e, oc=oc, ts_=ts_, g=g: e.tensor_copy(
                      qT[:, oc * T + ts_ * 512: oc * T + (ts_ + 1) * 512], bank(g)))
                  if vec else
                  (lambda e, oc=oc, ts_=ts_, g=g: e.activation(
                      qT[:, oc * T + ts_ * 512: oc * T + (ts_ + 1) * 512], bank(g),
                      AF.Identity, bias=0.0, scale=1.0)),
                  reads=[b_bank[g]], writes=[b_qT])

        def emit_kgrp(h, oc, ts_, first=False):
            i6 = oc * 8 + ts_
            g = (1 + i6 % 5) if h == 0 else i6 % 6
            vec = (i6 % 2 == 0) if h == 0 else True
            p.add("tensor", lambda e, oc=oc, ts_=ts_, g=g, h=h: e.matmul(
                bank(g), wkuT[:, h * 256 + oc * 128: h * 256 + (oc + 1) * 128],
                kvT[:, ts_ * 512:(ts_ + 1) * 512], start=True, stop=True),
                reads=[b_wkuT, b_kvT], writes=[b_bank[g]])
            p.add("vector" if vec else "scalar",
                  (lambda e, oc=oc, ts_=ts_, g=g: e.tensor_copy(
                      kT[:, oc * T + ts_ * 512: oc * T + (ts_ + 1) * 512], bank(g)))
                  if vec else
                  (lambda e, oc=oc, ts_=ts_, g=g: e.activation(
                      kT[:, oc * T + ts_ * 512: oc * T + (ts_ + 1) * 512], bank(g),
                      AF.Identity, bias=0.0, scale=1.0)),
                  reads=[b_bank[g]], writes=[b_kT],
                  extra=((b_ldB[0].handover() + b_ldB[1].handover() +
                          b_ldB[2].handover())
                         if (first or (h == 0 and oc == 1 and ts_ <= 3))
                         else ()))

        def emit_vmemset(h):
            p.add("vector", lambda e: e.memset(vA[:, :], 1.0), writes=[b_vA],
                  extra=((b_stB[0].handover() + b_stB[1].handover() +
                          b_stB[2].handover() + b_stB[3].handover())
                         if h == 0 else ()))

        def emit_vchunk(h, tk):
            g = (1 + tk % 5) if h == 0 else tk % 6
            vec = (tk % 2 == 0) if h == 0 else True
            p.add("tensor", lambda e, tk=tk, g=g, h=h: e.matmul(
                bank(g)[:, 0:256], kvT[:, tk * 128:(tk + 1) * 128],
                wvuT[:, h * 256:(h + 1) * 256], start=True, stop=True),
                reads=[b_kvT, b_wvuT], writes=[b_bank[g]])
            p.add("vector" if vec else "scalar",
                  (lambda e, tk=tk, g=g: e.tensor_copy(
                      vA[:, tk * 257: tk * 257 + 256], bank(g)[:, 0:256]))
                  if vec else
                  (lambda e, tk=tk, g=g: e.activation(
                      vA[:, tk * 257: tk * 257 + 256], bank(g)[:, 0:256],
                      AF.Identity, bias=0.0, scale=1.0)),
                  reads=[b_bank[g]], writes=[b_vA])

        def round5(src_ap, b_src, dst_ap, b_dst, sx, b_sx):
            p.add("vector", lambda e: e.tensor_scalar(
                t32[:, :].bitcast(F32), src_ap, sx, CM, op0=ALU.mult, op1=ALU.add),
                reads=[b_src, b_sx], writes=[b_t32])
            p.add("vector", lambda e: e.tensor_scalar(
                nn32[:, :], t32[:, :].bitcast(F32), -CM, None, op0=ALU.add),
                reads=[b_t32], writes=[b_nn])
            p.add("vector", lambda e: e.scalar_tensor_tensor(
                ee32[:, :], src_ap, sx, nn32[:, :], op0=ALU.mult, op1=ALU.subtract),
                reads=[b_src, b_sx, b_nn], writes=[b_ee])
            p.add("vector", lambda e: e.scalar_tensor_tensor(
                qq32[:, :], ee32[:, :], 0.5, nn32[:, :], op0=ALU.is_gt, op1=ALU.add),
                reads=[b_ee, b_nn], writes=[b_qq])
            p.add("vector", lambda e: e.scalar_tensor_tensor(
                dst_ap, ee32[:, :], -0.5, qq32[:, :], op0=ALU.is_lt, op1=ALU.subtract),
                reads=[b_ee, b_qq], writes=[b_dst])

        def round5_2(src_ap, b_srcs, dst_ap, b_dst, sx, b_sx, eng12="vector"):
            p.add(eng12, lambda e: e.tensor_scalar(
                t32[:, :].bitcast(F32), src_ap, sx, CM, op0=ALU.mult, op1=ALU.add),
                reads=b_srcs + [b_sx], writes=[b_t32])
            p.add(eng12, lambda e: e.tensor_scalar(
                nn32[:, :], t32[:, :].bitcast(F32), -CM, None, op0=ALU.add),
                reads=[b_t32], writes=[b_nn])
            p.add("vector", lambda e: e.scalar_tensor_tensor(
                ee32[:, :], src_ap, sx, nn32[:, :], op0=ALU.mult, op1=ALU.subtract),
                reads=b_srcs + [b_sx, b_nn], writes=[b_ee])
            p.add("vector", lambda e: e.scalar_tensor_tensor(
                qq32[:, :], ee32[:, :], 0.5, nn32[:, :], op0=ALU.is_gt, op1=ALU.add),
                reads=[b_ee, b_nn], writes=[b_qq])
            p.add("vector", lambda e: e.scalar_tensor_tensor(
                dst_ap, ee32[:, :], -0.5, qq32[:, :], op0=ALU.is_lt, op1=ALU.subtract),
                reads=[b_ee, b_qq], writes=[b_dst])

        def actscale(src_small, b_src, st, b_st, sx, b_sx):
            p.add("vector", lambda e: e.tensor_scalar(
                st[1], src_small, 1e-5, None, op0=ALU.max),
                reads=[b_src], writes=[b_st[1]])
            p.add("vector", lambda e: e.reciprocal(st[1], st[1]),
                  reads=[b_st[1]], writes=[b_st[1]])
            p.add("vector", lambda e: e.tensor_scalar(
                sx, st[1], 127.0, None, op0=ALU.mult),
                reads=[b_st[1]], writes=[b_sx])

        passA = []
        for wsrc, rows, cols, base in ((wg_d, NI, D, 0), (wu_d, NI, D, 32),
                                       (wd_d, D, NI, 64)):
            ncc = cols // 1024
            for i in range(rows // 128 * ncc):
                r, cb = divmod(i, ncc)
                passA.append((wsrc, r, cb, base + i))

        _pa_d = [0]
        _pa_c = [0]

        def emit_passA_dma():
            if _pa_d[0] >= len(passA):
                return
            j = _pa_d[0]
            _pa_d[0] += 1
            wsrc, r, cb, slot = passA[j]
            ab = j % 3
            p.add("scalar", lambda e, wsrc=wsrc, r=r, cb=cb, ab=ab: e.dma_start(
                out=wstA[ab][:, :], in_=wsrc[r * 128:(r + 1) * 128, cb * 1024:(cb + 1) * 1024]),
                writes=[b_wstA[ab]], dma=True)

        def emit_passA_abs():
            if _pa_c[0] >= _pa_d[0] or _pa_c[0] >= len(passA):
                return
            j = _pa_c[0]
            _pa_c[0] += 1
            wsrc, r, cb, slot = passA[j]
            ab = j % 3
            p.add("scalar", lambda e, slot=slot, ab=ab: e.activation(
                wstA[ab][:, :], wstA[ab][:, :], AF.Abs, accum_out=accA_sl(slot)),
                reads=[b_wstA[ab]], writes=[b_wstA[ab], b_accA])

        def emit_xnq_out(tc2):
            bq = tc2 % 2
            rq2 = b_p1r[bq]
            rs2 = b_p1r[2 + bq]
            tb = nb()
            for dc in range(NDC):
                p.add("tensor", lambda e, dc=dc, bq=bq, tb=tb: e.transpose(
                    BKap(tb)[:, dc * 128:(dc + 1) * 128],
                    p1b[:, bq * 1024 + dc * 128: bq * 1024 + (dc + 1) * 128],
                    ident[:]),
                    reads=[rq2, b_ident], writes=[b_psTh[tb]])
            p.add("scalar", lambda e, bq=bq, tb=tb: e.activation(
                p1b[:, (2 + bq) * 1024:(3 + bq) * 1024], BKap(tb)[:, :],
                AF.Identity, bias=0.0, scale=1.0),
                reads=[b_psTh[tb]], writes=[rs2])
            p.add("sync", lambda e, tc2=tc2, bq=bq: e.dma_start(
                out=xnqT_d.rearrange("(dc pp) t -> pp dc t", pp=128)[:, :, tc2 * 128:(tc2 + 1) * 128],
                in_=p1b[:, (2 + bq) * 1024:(3 + bq) * 1024].rearrange("pp (dc t) -> pp dc t", t=128)),
                reads=[rs2], writes=[b_d["xnqT"]], dma=True)

        def emit_kv(tc2):
            for dc in range(NDC):
                p.add("tensor", lambda e, dc=dc, tc2=tc2: e.matmul(
                    bank(0)[0:LR, 0:128], wkvT[:, dc * LR:(dc + 1) * LR],
                    xnT_sl(dc, tc2 * 128, 128),
                    start=(dc == 0), stop=(dc == NDC - 1)),
                    reads=[b_wkvT, b_arA], writes=[b_bank[0]])
            p.add("scalar", lambda e, tc2=tc2: e.activation(
                kvT[:, tc2 * 128:(tc2 + 1) * 128], bank(0)[0:LR, 0:128],
                AF.Identity, bias=0.0, scale=1.0),
                reads=[b_bank[0]], writes=[b_kvT])

        pre_q = [(oc_, ts__) for ts__ in range(8) for oc_ in range(2)]
        pre_k = [(oc_, ts__) for ts__ in range(8) for oc_ in range(2)]
        pre_v = list(range(NTC))
        for _ in range(3):
            emit_passA_dma()
        for _ in range(3):
            emit_prep_dma()
        for tc in range(NTC):
            bb = tc % 2
            xn32 = xn32p[bb]
            b_xn = b_xn32[bb]
            st, b_st = stp[bb], b_stp[bb]
            sxv, b_sxv = sxv2[bb], b_sxv2[bb]
            rq = b_p1r[bb]
            rs = b_p1r[2 + bb]
            if tc == 20:
                emit_wqst(0)
                emit_vmemset(0)
            if tc >= 21:
                for _ in range(2):
                    if pre_q and 4 * pre_q[0][1] + 3 < tc:
                        oc_, ts__ = pre_q.pop(0)
                        emit_qgrp(0, oc_, ts__)
                for _ in range(2):
                    if pre_k and 4 * pre_k[0][1] + 4 < tc:
                        oc_, ts__ = pre_k.pop(0)
                        emit_kgrp(0, oc_, ts__, first=(oc_ == 0 and ts__ == 0))
                for _ in range(4):
                    if pre_v and pre_v[0] + 1 < tc:
                        emit_vchunk(0, pre_v.pop(0))
            p.add("sync", lambda e, tc=tc, bb=bb: e.dma_start(
                out=xst[bb][:, :], in_=x_d[tc * 128:(tc + 1) * 128, :]),
                writes=[b_xst[bb]], dma=True)
            p.add("vector", lambda e, bb=bb, st=st: e.reduce_sum(
                st[0], xst[bb][:, :], axis=AX.X),
                reads=[b_xst[bb]], writes=[b_st[0]])
            p.add("scalar", lambda e, bb=bb, xn32=xn32, st=st: e.activation(
                xn32[:, :], xst[bb][:, :], AF.Square, accum_out=st[1]),
                reads=[b_xst[bb]], writes=[b_xn, b_st[1]])
            p.add("vector", lambda e, st=st: e.tensor_scalar(
                st[0], st[0], 1.0 / D, None, op0=ALU.mult),
                reads=[b_st[0]], writes=[b_st[0]])
            p.add("vector", lambda e, st=st: e.tensor_scalar(
                st[1], st[1], 1.0 / D, None, op0=ALU.mult),
                reads=[b_st[1]], writes=[b_st[1]])
            p.add("vector", lambda e, st=st: e.scalar_tensor_tensor(
                st[2], st[0], st[0], st[1], op0=ALU.mult, op1=ALU.subtract),
                reads=[b_st[0], b_st[1]], writes=[b_st[2]])
            p.add("scalar", lambda e, st=st: e.activation(
                st[3], st[2], AF.Sqrt, bias=epsv, scale=-1.0),
                reads=[b_st[2], b_eps], writes=[b_st[3]])
            p.add("vector", lambda e, st=st: e.reciprocal(st[3], st[3]),
                  reads=[b_st[3]], writes=[b_st[3]])
            p.add("vector", lambda e, st=st: e.tensor_scalar(
                st[2], st[3], st[0], -1.0, op0=ALU.mult, op1=ALU.mult),
                reads=[b_st[3], b_st[0]], writes=[b_st[2]])
            p.add("scalar", lambda e, bb=bb, st=st: e.activation(
                p1b[:, (2 + bb) * 1024:(3 + bb) * 1024], xst[bb][:, :],
                AF.Identity, bias=st[2], scale=st[3]),
                reads=[b_xst[bb], b_st[2], b_st[3]], writes=[rs])
            p.add("scalar", lambda e, bb=bb, xn32=xn32, st=st: e.activation(
                xn32[:, :], xst[bb][:, :], AF.Identity, bias=st[2], scale=st[3]),
                reads=[b_xst[bb], b_st[2], b_st[3]], writes=[b_xn])
            for _ in range(3):
                emit_passA_abs()
            for _ in range(3):
                emit_prep_comp()
            if tc >= 1:
                emit_kv(tc - 1)
            p.add("vector", lambda e, xn32=xn32, st=st: e.tensor_reduce(
                st[0], xn32[:, :], axis=AX.X, op=ALU.max, apply_absolute_value=True),
                reads=[b_xn], writes=[b_st[0]])
            p.add("vector", lambda e, tc=tc, st=st: e.tensor_scalar(
                rsx[:, tc:tc + 1], st[0], 1e-5, 1.0 / 127.0, op0=ALU.max, op1=ALU.mult),
                reads=[b_st[0]], writes=[b_rsx])
            actscale(st[0], b_st[0], st, b_st, sxv, b_sxv)
            tb = nb()
            for dc in range(NDC):
                p.add("tensor", lambda e, dc=dc, bb=bb, tb=tb: e.transpose(
                    BKap(tb)[:, dc * 128:(dc + 1) * 128],
                    p1b[:, (2 + bb) * 1024 + dc * 128: (2 + bb) * 1024 + (dc + 1) * 128],
                    ident[:]),
                    reads=[rs, b_ident], writes=[b_psTh[tb]])
            p.add("vector", lambda e, tc=tc, tb=tb: e.tensor_copy(
                arA[:, :].rearrange("pp (dc t) -> pp dc t", t=T)[:, :, tc * 128:(tc + 1) * 128],
                BKap(tb)[:, :].rearrange("pp (dc t) -> pp dc t", t=128)),
                reads=[b_psTh[tb]], writes=[b_arA])
            if tc >= 1:
                emit_xnq_out(tc - 1)
            round5(xn32[:, :], b_xn, p1b[:, bb * 1024:(bb + 1) * 1024], rq, sxv, b_sxv)
            for _ in range(3):
                emit_passA_dma()
            for _ in range(3):
                emit_prep_dma()
        emit_xnq_out(NTC - 1)
        emit_kv(NTC - 1)
        while _pa_c[0] < len(passA):
            emit_passA_abs()
            emit_passA_dma()
        while _pc[0] < len(prep_tiles):
            emit_prep_comp()
            emit_prep_dma()
        for oc_, ts__ in pre_q:
            emit_qgrp(0, oc_, ts__)
        for oc_, ts__ in pre_k:
            emit_kgrp(0, oc_, ts__, first=(oc_ == 0 and ts__ == 0))
        for tk_ in pre_v:
            emit_vchunk(0, tk_)
        if debug:
            p.add_final_wait(p.add("sync", lambda e: e.dma_start(
                out=xnT_dbg.rearrange("(dc pp) t -> pp dc t", pp=128),
                in_=arA[:, :].rearrange("pp (dc t) -> pp dc t", t=T)),
                reads=[b_arA], dma=True))
            p.add_final_wait(p.add("sync", lambda e: e.dma_start(
                out=kvT_dbg, in_=kvT[:, :]), reads=[b_kvT], dma=True))

        for bslot, (rows, cols, base) in enumerate(((NI, D, 0), (NI, D, 32), (D, NI, 64))):
            st = stp[0]
            p.add("vector", lambda e, base=base, st=st: e.reduce_sum(
                st[0], smf[:, 112 + base: 112 + base + 32], axis=AX.X),
                reads=[b_accA], writes=[b_stp[0][0]])
            p.add("tensor", lambda e, st=st: e.matmul(
                bank(5)[0:1, 0:1], ones_t[:, 0:1], st[0], start=True, stop=True),
                reads=[b_ones, b_stp[0][0]], writes=[b_bank[5]])
            p.add("vector", lambda e, rows=rows, cols=cols: e.tensor_scalar(
                mu11[0:1, :], bank(5)[0:1, 0:1], 1.0 / (rows * cols), 1e-5,
                op0=ALU.mult, op1=ALU.max),
                reads=[b_bank[5]], writes=[b_mu])
            p.add("tensor", lambda e: e.matmul(
                bank(5)[:, 0:1], ones_t[0:1, :], mu11[0:1, :], start=True, stop=True),
                reads=[b_ones, b_mu], writes=[b_bank[5]])
            p.add("vector", lambda e, bslot=bslot: e.tensor_copy(
                bvec[:, bslot:bslot + 1], bank(5)[:, 0:1]),
                reads=[b_bank[5]], writes=[b_bvec])
            p.add("vector", lambda e, bslot=bslot: e.reciprocal(
                scw3[bslot], bvec[:, bslot:bslot + 1]),
                reads=[b_bvec], writes=[b_scw3[bslot]])

        passB = []
        for bslot, (wsrc, rows, cols, dst, bd) in enumerate((
                (wg_d, NI, D, wgT_d, b_d["wgT"]), (wu_d, NI, D, wuT_d, b_d["wuT"]),
                (wd_d, D, NI, wdT_d, b_d["wdT"]))):
            ncc = cols // 1024
            for i in range(rows // 128 * ncc):
                r, cb = divmod(i, ncc)
                passB.append((wsrc, r, cb, dst, bd, bslot))

        def emit_passB1(j):
            wsrc, r, cb, dst, bd, bslot = passB[j]
            bb = j % 2
            ro = b_p1r[bb]
            p.add("gpsimd", lambda e, r=r, cb=cb, bb=bb, wsrc=wsrc: e.dma_start(
                out=xst[bb][:, :], in_=wsrc[r * 128:(r + 1) * 128, cb * 1024:(cb + 1) * 1024]),
                writes=[b_xst[bb]], dma=True)
            p.add("gpsimd", lambda e, bb=bb, bslot=bslot: e.tensor_scalar(
                nn32[:, :], xst[bb][:, :], scw3[bslot], -0.5, op0=ALU.mult, op1=ALU.is_lt),
                reads=[b_xst[bb], b_scw3[bslot]], writes=[b_nn])
            p.add("gpsimd", lambda e, bb=bb, bslot=bslot: e.tensor_scalar(
                ee32[:, :], xst[bb][:, :], scw3[bslot], 0.5, op0=ALU.mult, op1=ALU.is_gt),
                reads=[b_xst[bb], b_scw3[bslot]], writes=[b_ee])
            p.add("gpsimd", lambda e, bb=bb: e.tensor_tensor(
                p1b[:, bb * 1024:(bb + 1) * 1024], nn32[:, :], ee32[:, :],
                op=ALU.subtract),
                reads=[b_nn, b_ee], writes=[ro])

        trB_guard = b_psTh[1].handover()
        for h in range(NH):
            if h > 0:
                emit_wqst(h)
                for oc in range(2):
                    for ts_ in range(8):
                        emit_qgrp(h, oc, ts_)
                for oc in range(2):
                    for ts_ in range(8):
                        emit_kgrp(h, oc, ts_)
            if h == NH - 1:
                arA_g = b_arA.handover()
                for isl_ in range(8):
                    p.add("sync", lambda e, isl_=isl_: e.dma_start(
                        out=arA[:, :].rearrange("pp (dc i) -> pp dc i", i=NI)[:, :, isl_ * 512:(isl_ + 1) * 512],
                        in_=wgT_d.rearrange("(dc pp) i -> pp dc i", pp=128)[:, :, isl_ * 512:(isl_ + 1) * 512]),
                        reads=[b_d["wgT"]], writes=[b_wg8[isl_]],
                        extra=arA_g, dma=True)
            if h > 0:
                emit_vmemset(h)
                for tk in range(NTC):
                    emit_vchunk(h, tk)
            def emit_qk(bi, c):
                g = c % 3
                for d2 in range(2):
                    p.add("tensor", lambda e, bi=bi, c=c, g=g, d2=d2: e.matmul(
                        bank(g), kT[:, d2 * T + c * 128: d2 * T + (c + 1) * 128],
                        qT[:, d2 * T + bi * 512: d2 * T + (bi + 1) * 512],
                        start=(d2 == 0), stop=(d2 == 1)),
                        reads=[b_kT, b_qT], writes=[b_bank[g]])
                p.add("scalar", lambda e, g=g, c=c: e.activation(
                    pT[c % 3][:, :], bank(g), AF.Exp, bias=0.0, scale=ISCL),
                    reads=[b_bank[g]], writes=[b_pT[c % 3]])

            def emit_pv(c):
                for sub in range(4):
                    p.add("tensor", lambda e, c=c, sub=sub: e.matmul(
                        pv_ps(sub, 0, 257), pT[c % 3][:, sub * 128:(sub + 1) * 128],
                        vA[:, c * 257:(c + 1) * 257],
                        start=(c == 0), stop=(c == NTC - 1)),
                        reads=[b_pT[c % 3], b_vA], writes=[b_bank[3 + sub]],
                        extra=(trB_guard if h == 0 and c == 0 and sub == 3 else ()))

            def finalize_recips():
                for sub in range(4):
                    p.add("vector", lambda e, sub=sub: e.reciprocal(
                        repl[:, sub:sub + 1], pv_ps(sub, 256, 257)),
                        reads=[b_bank[3 + sub]], writes=[b_repl])

            def finalize_atto(bi):
                for sub in range(4):
                    p.add("scalar", lambda e, sub=sub: e.activation(
                        att_o[:, sub * 256:(sub + 1) * 256], pv_ps(sub, 0, 256),
                        AF.Identity, scale=repl[:, sub:sub + 1]),
                        reads=[b_bank[3 + sub], b_repl], writes=[b_atto],
                        extra=(b_wkvT.handover()
                               if h == 0 and bi == 0 and sub == 0 else ()))

            def trans_tasks(bi):
                pb = 0 if (h == NH - 1 and bi == 7) else bi % 2
                for half in range(2):
                    for sub in range(4):
                        yield lambda sub=sub, half=half: p.add(
                            "tensor", lambda e, sub=sub, half=half: e.transpose(
                                psT0[:, (half * 4 + sub) * 128:(half * 4 + sub + 1) * 128],
                                att_o[:, sub * 256 + half * 128: sub * 256 + half * 128 + 128],
                                ident[:]),
                            reads=[b_atto, b_ident], writes=[b_psTh[0]])

                def cpdma(pb=pb, bi=bi):
                    p.add("vector", lambda e, pb=pb: e.tensor_copy(
                        aoTh[pb][:, :], psT0[:, 0:1024]),
                        reads=[b_psTh[0]], writes=[b_aoTh[pb]])
                    p.add("sync", lambda e, h=h, bi=bi, pb=pb: e.dma_start(
                        out=aoT_d[h * 256:(h + 1) * 256, bi * 512:(bi + 1) * 512]
                        .rearrange("(ch pp) t -> pp ch t", pp=128),
                        in_=aoTh[pb][:, :].rearrange("pp (ch t) -> pp ch t", t=512)),
                        reads=[b_aoTh[pb]], writes=[b_d["aoT"]], dma=True)
                yield cpdma
                gi = h * 8 + bi
                for j in range(gi * 3 // 2, min((gi + 1) * 3 // 2, len(passB))):
                    for dc in range(8):
                        yield lambda j=j, dc=dc: _passB2_tr(j, dc)
                    yield lambda j=j: _passB2_fin(j)

            def _passB2_tr(j, dc):
                wsrc, r, cb, dst, bd, bslot = passB[j]
                bb = j % 2
                ro = b_p1r[bb]
                p.add("tensor", lambda e, dc=dc, bb=bb: e.transpose(
                    psT0[:, dc * 128:(dc + 1) * 128],
                    p1b[:, bb * 1024 + dc * 128: bb * 1024 + (dc + 1) * 128], ident[:]),
                    reads=[ro, b_ident], writes=[b_psTh[0]])

            def _passB2_fin(j):
                wsrc, r, cb, dst, bd, bslot = passB[j]
                bb = j % 2
                rsb = b_p1r[2 + bb]
                p.add("vector", lambda e, bb=bb: e.tensor_copy(
                    p1b[:, (2 + bb) * 1024:(3 + bb) * 1024], psT0[:, 0:1024]),
                    reads=[b_psTh[0]], writes=[rsb])
                p.add("gpsimd", lambda e, r=r, cb=cb, dst=dst, bb=bb: e.dma_start(
                    out=dst.rearrange("(cc pp) o -> pp cc o", pp=128)[:, cb * 8:(cb + 1) * 8, r * 128:(r + 1) * 128],
                    in_=p1b[:, (2 + bb) * 1024:(3 + bb) * 1024].rearrange("pp (cc t) -> pp cc t", t=128)),
                    reads=[rsb], writes=[bd], dma=True)

            for bi in range(8):
                tq = list(trans_tasks(bi - 1)) if bi >= 1 else []
                ti = 0
                for c in range(NTC):
                    emit_qk(bi, c)
                    if c == 1 and bi >= 1:
                        finalize_atto(bi - 1)
                    if c >= 4:
                        for _ in range(2):
                            if ti < len(tq):
                                tq[ti]()
                                ti += 1
                    if c >= 2:
                        emit_pv(c - 2)
                while ti < len(tq):
                    tq[ti]()
                    ti += 1
                emit_pv(NTC - 2)
                emit_pv(NTC - 1)
                finalize_recips()
                if bi == 7:
                    finalize_atto(7)
                gi0 = h * 8 + bi
                for j in range(gi0 * 3 // 2, min((gi0 + 1) * 3 // 2, len(passB))):
                    emit_passB1(j)
            for fn in trans_tasks(7):
                fn()
        tok_wu = None
        b_wu8h = [[Buf(), Buf()] for _ in range(8)]
        xnq_guard = b_kvT.handover() + b_aoTh[1].handover()
        guard_wu = (b_qT.handover() + b_kT.handover() + b_vA.handover() +
                    b_wqst[0].handover() + b_wqst[1].handover() +
                    b_pT[0].handover() + b_pT[1].handover() + b_pT[2].handover() +
                    b_kvT.handover() + b_aoTh[0].handover() + b_aoTh[1].handover() +
                    b_atto.handover())
        def emit_wu_loads():
            tok = None
            for isl in range(8):
                tok = p.add("scalar" if isl % 2 == 0 else "sync",
                            lambda e, isl=isl: e.dma_start(
                    out=arB[:, 0:32768].rearrange("pp (dc i) -> pp dc i", i=NI)[:, :, isl * 512:(isl + 1) * 512],
                    in_=wuT_d.rearrange("(dc pp) i -> pp dc i", pp=128)[:, :, isl * 512:(isl + 1) * 512]),
                    reads=[b_d["wuT"]], writes=[b_wu8h[isl][0]],
                    extra=guard_wu, dma=True)
            return tok
        h32_guard = (b_xst[0].handover() + b_xst[1].handover() +
                     b_xn32[0].handover() + b_xn32[1].handover() +
                     b_wstA[0].handover() + b_wstA[1].handover() +
                     b_wstA[2].handover())

        bank6_guard = b_bank[6].handover()
        _p3b1 = [False]

        def emit_trans_group(tc, g):
            tb = g % 2
            ex = ()
            if tb == 1 and not _p3b1[0]:
                _p3b1[0] = True
                ex = bank6_guard
            for dc in range(8):
                ic = g * 8 + dc
                p.add("tensor", lambda e, ic=ic, dc=dc, tb=tb: e.transpose(
                    BKap(tb)[:, dc * 128:(dc + 1) * 128],
                    hq_neg[:, ic * 128:(ic + 1) * 128], ident[:]),
                    reads=[b_hqneg[g], b_ident], writes=[b_psTh[tb]],
                    extra=(ex if dc == 0 else ()))
            p.add("scalar", lambda e, g=g, tb=tb: e.activation(
                p1b[:, g * 1024:(g + 1) * 1024], BKap(tb)[:, :],
                AF.Identity, bias=0.0, scale=1.0),
                reads=[b_psTh[tb]], writes=[b_p1r[g]])

        def emit_hq_dma(tc):
            p.add("sync", lambda e, tc=tc: e.dma_start(
                out=hqT_d.rearrange("(ic pp) t -> pp ic t", pp=128)[:, :, tc * 128:(tc + 1) * 128],
                in_=p1b[:, :].rearrange("pp (ic t) -> pp ic t", t=128)),
                reads=[b_p1r[0], b_p1r[1], b_p1r[2], b_p1r[3]],
                writes=[b_d["hqT"]], dma=True)

        def xnq_load(tc):
            bb = tc % 2
            p.add("sync", lambda e, tc=tc, bb=bb: e.dma_start(
                out=xnq_st[bb][:, :].rearrange("pp (dc t) -> pp dc t", t=128),
                in_=xnqT_d.rearrange("(dc pp) t -> pp dc t", pp=128)[:, :, tc * 128:(tc + 1) * 128]),
                reads=[b_d["xnqT"]], writes=[b_xnqst[bb]],
                extra=(xnq_guard if tc < 2 else ()), dma=True)

        p.add("vector", lambda e: e.tensor_scalar(
            sgall[:, :], rsx[:, :], bvec[:, 0:1], None, op0=ALU.mult),
            reads=[b_rsx, b_bvec], writes=[b_sgall])
        p.add("vector", lambda e: e.tensor_scalar(
            suall[:, :], rsx[:, :], bvec[:, 1:2], None, op0=ALU.mult),
            reads=[b_rsx, b_bvec], writes=[b_suall])
        xnq_load(0)
        xnq_load(1)
        tok_wu = emit_wu_loads()
        for b_ in b_hqneg:
            b_.w = tok_wu
        for tc in range(NTC):
            bb = tc % 2
            h32 = h32p[bb]
            b_hs = b_h32[bb]
            st, b_st = stp[bb], b_stp[bb]
            sxv, b_sxv = sxv2[bb], b_sxv2[bb]
            for isl in range(8):
                gg, gu = isl % 2, 2 + isl % 2
                for dc in range(NDC):
                    p.add("tensor", lambda e, isl=isl, dc=dc, gg=gg, bb=bb: e.matmul(
                        bank(gg), xnq_st[bb][:, dc * 128:(dc + 1) * 128],
                        wgqT_sl(dc, isl * 512, 512),
                        start=(dc == 0), stop=(dc == NDC - 1)),
                        reads=[b_xnqst[bb], b_wg8[isl]], writes=[b_bank[gg]])
                for dc in range(NDC):
                    p.add("tensor", lambda e, isl=isl, dc=dc, gu=gu, bb=bb: e.matmul(
                        bank(gu), xnq_st[bb][:, dc * 128:(dc + 1) * 128],
                        wuqT_sl(dc, isl * 512, 512),
                        start=(dc == 0), stop=(dc == NDC - 1)),
                        reads=[b_xnqst[bb], b_wu8h[isl][0]],
                        writes=[b_bank[gu]])
                p.add("scalar", lambda e, isl=isl, gu=gu, h32=h32, tc=tc: e.activation(
                    h32[:, isl * 512:(isl + 1) * 512], bank(gu), AF.Identity,
                    scale=suall[:, tc:tc + 1]),
                    reads=[b_bank[gu], b_suall], writes=[b_hs[isl]],
                    extra=(h32_guard if tc < 2 and isl == 0 else ()))
                p.add("scalar", lambda e, gg=gg, isl=isl, tc=tc: e.activation(
                    sil32[isl % 4][:, :], bank(gg), AF.Silu, scale=sgall[:, tc:tc + 1]),
                    reads=[b_bank[gg], b_sgall], writes=[b_sil[isl % 4]])
                p.add("gpsimd", lambda e, isl=isl, h32=h32: e.tensor_tensor(
                    h32[:, isl * 512:(isl + 1) * 512], sil32[isl % 4][:, :],
                    h32[:, isl * 512:(isl + 1) * 512], op=ALU.mult),
                    reads=[b_sil[isl % 4]], writes=[b_hs[isl]])
                p.add("vector", lambda e, isl=isl, h32=h32, bb=bb: e.tensor_reduce(
                    acc32[:, bb * 8 + isl: bb * 8 + isl + 1],
                    h32[:, isl * 512:(isl + 1) * 512], axis=AX.X, op=ALU.max,
                    apply_absolute_value=True),
                    reads=[b_hs[isl]], writes=[b_acc])
            p.add("vector", lambda e, st=st, bb=bb: e.tensor_reduce(
                st[0], acc32[:, bb * 8: bb * 8 + 8], axis=AX.X, op=ALU.max),
                reads=[b_acc], writes=[b_st[0]])
            p.add("vector", lambda e, tc=tc, st=st: e.tensor_scalar(
                rsh[:, tc:tc + 1], st[0], 1e-5, 1.0 / 127.0, op0=ALU.max, op1=ALU.mult),
                reads=[b_st[0]], writes=[b_rsh])
            if tc + 2 < NTC:
                xnq_load(tc + 2)
            if tc >= 1:
                for g_ in range(4):
                    emit_trans_group(tc - 1, g_)
                emit_hq_dma(tc - 1)
            actscale(st[0], b_st[0], st, b_st, sxv, b_sxv)
            for sl in range(4):
                round5_2(h32[:, sl * 1024:(sl + 1) * 1024],
                         [b_hs[2 * sl], b_hs[2 * sl + 1]],
                         hq_neg[:, sl * 1024:(sl + 1) * 1024], b_hqneg[sl], sxv, b_sxv)
        for g in range(4):
            emit_trans_group(NTC - 1, g)
        emit_hq_dma(NTC - 1)

        b_wd2 = [[Buf() for _ in range(4)], [Buf() for _ in range(4)]]
        b_wo2 = [Buf(), Buf()]
        guard_wd = (b_xnqst[0].handover() + b_xnqst[1].handover() +
                    [d for b_ in b_hqneg for d in b_.handover()] +
                    [d for pr in b_wu8h for bb_ in pr for d in bb_.handover()])
        guard_wo = (b_arA.handover() +
                    [d for bb_ in b_wg8 for d in bb_.handover()])
        arA_guard = guard_wo
        h32_guard2 = [d for par in range(2) for bb_ in b_h32[par] for d in bb_.handover()]
        qq_guard = b_qq.handover()

        def emit_x_load(tc, bb, eng):
            p.add(eng, lambda e, tc=tc, bb=bb: e.dma_start(
                out=x4st[bb][:, :], in_=x_d[tc * 128:(tc + 1) * 128, :]),
                writes=[b_x4[bb]],
                extra=((h32_guard2 + b_xn32[bb].handover()) if tc < 2 else ()),
                dma=True)

        def emit_hqao_load(tc, bb):
            p.add("sync", lambda e, tc=tc, bb=bb: e.dma_start(
                out=hq_st[bb][:, :].rearrange("pp (ic t) -> pp ic t", t=128),
                in_=hqT_d.rearrange("(ic pp) t -> pp ic t", pp=128)[:, :, tc * 128:(tc + 1) * 128]),
                reads=[b_d["hqT"]], writes=[b_hqst[bb]],
                extra=(arA_guard if tc < 2 else ()), dma=True)
            p.add("sync", lambda e, tc=tc, bb=bb: e.dma_start(
                out=ao_st[bb][:, :].rearrange("pp (oc t) -> pp oc t", t=128),
                in_=aoT_d.rearrange("(oc pp) t -> pp oc t", pp=128)[:, :, tc * 128:(tc + 1) * 128]),
                reads=[b_d["aoT"]], writes=[b_aost[bb]],
                extra=(arA_guard if tc < 2 else ()), dma=True)

        for n_ in range(2):
            for q_ in range(4):
                p.add("gpsimd", lambda e, n_=n_, q_=q_: e.dma_start(
                    out=arB[:, 0:32768].rearrange("pp (ic n) -> pp ic n", n=D)[:, q_ * 8:(q_ + 1) * 8, n_ * 512:(n_ + 1) * 512],
                    in_=wdT_d.rearrange("(ic pp) n -> pp ic n", pp=128)[:, q_ * 8:(q_ + 1) * 8, n_ * 512:(n_ + 1) * 512]),
                    reads=[b_d["wdT"]], writes=[b_wd2[n_][q_]],
                    extra=guard_wd, dma=True)
        for n_ in range(2):
            p.add("scalar", lambda e, n_=n_: e.dma_start(
                out=arA[:, 0:16384].rearrange("pp (oc n) -> pp oc n", n=D)[:, :, n_ * 512:(n_ + 1) * 512],
                in_=woT_d.rearrange("(oc pp) n -> pp oc n", pp=128)[:, :, n_ * 512:(n_ + 1) * 512]),
                reads=[b_d["woT"]], writes=[b_wo2[n_]],
                extra=guard_wo, dma=True)
        p.add("vector", lambda e: e.tensor_scalar(
            sdall[:, :], rsh[:, :], bvec[:, 2:3], None, op0=ALU.mult),
            reads=[b_rsh, b_bvec], writes=[b_sdall], extra=b_accA.handover())
        bkk = lambda i: bank(i) if i < 7 else psT0[:, 0:1024].bitcast(F32)
        b_bkl = lambda i: [b_bank[i]] if i < 7 else [b_psTh[0]]
        trB_guard2 = b_psTh[1].handover()
        dfin = None
        for tc in range(NTC):
            bb = tc % 2
            dbk = (0, 1) if bb == 0 else (4, 5)
            wbk = (2, 3) if bb == 0 else (6, 7)
            emit_hqao_load(tc, bb)
            emit_x_load(tc, bb, "scalar")
            for n_ in range(2):
                for oc in range(16):
                    p.add("tensor", lambda e, oc=oc, n_=n_, bb=bb, wbk=wbk: e.matmul(
                        bkk(wbk[n_]), ao_st[bb][:, oc * 128:(oc + 1) * 128],
                        woT_sl(oc, n_ * 512, 512),
                        start=(oc == 0), stop=(oc == 15)),
                        reads=[b_aost[bb], b_wo2[n_]], writes=b_bkl(wbk[n_]),
                        extra=(trB_guard2 if tc == 1 and n_ == 0 and oc == 0 else ()))
                p.add("vector", lambda e, n_=n_, bb=bb, wbk=wbk: e.scalar_tensor_tensor(
                    out32[bb][:, n_ * 512:(n_ + 1) * 512], bkk(wbk[n_]), 1.0,
                    x4st[bb][:, n_ * 512:(n_ + 1) * 512],
                    op0=ALU.mult, op1=ALU.add),
                    reads=b_bkl(wbk[n_]) + [b_x4[bb]], writes=[b_xst[bb]],
                    extra=(h32_guard2 if tc < 2 and n_ == 0 else ()))
            for n_ in range(2):
                for ic in range(32):
                    p.add("tensor", lambda e, ic=ic, n_=n_, bb=bb, dbk=dbk: e.matmul(
                        bank(dbk[n_]), hq_st[bb][:, ic * 128:(ic + 1) * 128],
                        wdqT_sl(ic, n_ * 512, 512),
                        start=(ic == 0), stop=(ic == 31)),
                        reads=[b_hqst[bb], b_wd2[n_][ic // 8]],
                        writes=[b_bank[dbk[n_]]])
            p.add("scalar", lambda e, tc=tc, bb=bb, dbk=dbk: e.activation(
                t1_32[:, :], psF[:, dbk[0] * 512:(dbk[0] + 2) * 512],
                AF.Identity, scale=sdall[:, tc:tc + 1]),
                reads=[b_bank[dbk[0]], b_bank[dbk[1]], b_sdall], writes=[b_t1],
                extra=((qq_guard + h32_guard2) if tc == 0 else ()))
            p.add("vector", lambda e, bb=bb: e.tensor_tensor(
                out32[bb][:, :], out32[bb][:, :], t1_32[:, :], op=ALU.add),
                reads=[b_xst[bb], b_t1], writes=[b_xst[bb]])
            dfin = p.add("scalar", lambda e, tc=tc, bb=bb: e.dma_start(
                out=y_d[tc * 128:(tc + 1) * 128, :], in_=out32[bb][:, :]),
                reads=[b_xst[bb]], dma=True)
        p.add_final_wait(dfin)
        if debug:
            p.add_final_wait(p.add("sync", lambda e: e.dma_start(
                out=sc_dbg, in_=smf[:, 0:128]),
                reads=[b_rsx, b_rsh, b_bvec], dma=True))

        blk = ctx.enter_context(nc.Block())
        p.emit(nc, blk, ctx)
    return nc


def kernel(L0_attn_x, L0_attn_Wq, L0_attn_Wkv_down, L0_attn_Wk_up,
           L0_attn_Wv_up, L0_attn_Wo, L0_ffn_w_gate, L0_ffn_w_up,
           L0_ffn_w_down):
    nc = build_nc()
    ident = np.eye(128, dtype=ml_dtypes.bfloat16)
    x = np.ascontiguousarray(np.asarray(L0_attn_x, dtype=np.float32))
    common = {
        "Wq": np.ascontiguousarray(np.asarray(L0_attn_Wq, dtype=np.float32)),
        "Wkv": np.ascontiguousarray(np.asarray(L0_attn_Wkv_down, dtype=np.float32)),
        "Wku": np.ascontiguousarray(np.asarray(L0_attn_Wk_up, dtype=np.float32)),
        "Wvu": np.ascontiguousarray(np.asarray(L0_attn_Wv_up, dtype=np.float32)),
        "Wo": np.ascontiguousarray(np.asarray(L0_attn_Wo, dtype=np.float32)),
        "Wg": np.ascontiguousarray(np.asarray(L0_ffn_w_gate, dtype=np.float32)),
        "Wu": np.ascontiguousarray(np.asarray(L0_ffn_w_up, dtype=np.float32)),
        "Wd": np.ascontiguousarray(np.asarray(L0_ffn_w_down, dtype=np.float32)),
        "ident": ident,
    }
    in_maps = [dict(common, x=x[c]) for c in range(8)]
    res = run_bass_kernel_spmd(nc, in_maps, core_ids=list(range(8)))
    return np.stack([np.asarray(r["y"], dtype=np.float32) for r in res.results],
                    axis=0)


# version 40
# speedup vs baseline: 1.3467x; 1.3467x over previous
import numpy as np
import ml_dtypes
import contextlib
import concourse.bass as bass
import concourse.mybir as mybir
from concourse.bass_utils import run_bass_kernel_spmd

F32 = mybir.dt.float32
BF16 = mybir.dt.bfloat16
I32 = mybir.dt.int32
AF = mybir.ActivationFunctionType
ALU = mybir.AluOpType
AX = mybir.AxisListType

T = 4096
D = 1024
NH = 8
DO = 2048
LR = 64
NI = 4096
NTC = T // 128
NDC = D // 128
CM = 12582912.0
ISCL = 0.0625

ENGINES = ("sync", "tensor", "vector", "scalar", "gpsimd")


class Buf:
    __slots__ = ("w", "r")

    def __init__(self):
        self.w = None
        self.r = []

    def handover(self):
        return ([self.w] if self.w is not None else []) + list(self.r)


class Prog:
    def __init__(self):
        self.ops = []
        self.final_waits = []

    def add(self, eng, fn, reads=(), writes=(), extra=(), dma=False):
        deps = []
        for b in writes:
            if b.w is not None:
                deps.append(b.w)
            deps.extend(b.r)
        for b in reads:
            if b.w is not None:
                deps.append(b.w)
        deps.extend(d for d in extra if d is not None)
        i = len(self.ops)
        assert all(d < i for d in deps)
        self.ops.append({"eng": eng, "fn": fn, "deps": sorted(set(deps)),
                         "dma": dma})
        for b in writes:
            b.w = i
            b.r = []
        for b in reads:
            b.r.append(i)
        return i

    def add_final_wait(self, dep):
        self.final_waits.append(dep)

    NSLOT = 12

    def emit(self, nc, block, ctx):
        ops = self.ops
        NS = self.NSLOT
        ekey = [None] * len(ops)
        qidx = {}
        for i, op in enumerate(ops):
            if op["dma"]:
                q = "q_" + op["eng"]
                j = qidx.get(q, 0)
                qidx[q] = j + 1
                ekey[i] = (q, j % NS)
            else:
                ekey[i] = op["eng"]
        for i, op in enumerate(ops):
            op["deps"] = [d for d in op["deps"]
                          if not (ekey[d] == "tensor" and ekey[i] == "tensor")]
        needed = set(i for i, op in enumerate(ops) if op["dma"])
        for op in ops:
            needed.update(op["deps"])
        needed.update(self.final_waits)
        sems = {}
        for e in ENGINES:
            sems[e] = ctx.enter_context(nc.semaphore("s_" + e))
            for s in range(NS):
                k = ("q_" + e, s)
                sems[k] = ctx.enter_context(nc.semaphore(f"sq_{e}_{s}"))
        val = [0] * len(ops)
        cnt = {k: 0 for k in sems}
        for i, op in enumerate(ops):
            if i in needed:
                cnt[ekey[i]] += 16 if op["dma"] else 1
                val[i] = cnt[ekey[i]]
        per_eng = {e: [] for e in ENGINES}
        for i, op in enumerate(ops):
            per_eng[op["eng"]].append(i)

        last_val = dict(cnt)

        def emit_engine(e, eng):
            maxw = {}

            def w(k, v):
                if v > 0 and maxw.get(k, 0) < v:
                    eng.wait_ge(sems[k], v)
                    maxw[k] = v

            for i in per_eng[e]:
                op = ops[i]
                if op["dma"]:
                    w(ekey[i], val[i] - 16)
                for d in op["deps"]:
                    assert val[d] > 0, (i, d)
                    w(ekey[d], val[d])
                inst = op["fn"](eng)
                if i in needed:
                    inst.then_inc(sems[ekey[i]], 16 if op["dma"] else 1)
            for d in self.final_waits:
                w(ekey[d], val[d])
            if e == "sync":
                for k, v in last_val.items():
                    if isinstance(k, tuple):
                        w(k, v)

        @block.sync
        def _(eng):
            emit_engine("sync", eng)

        @block.tensor
        def _(eng):
            emit_engine("tensor", eng)

        @block.vector
        def _(eng):
            emit_engine("vector", eng)

        @block.scalar
        def _(eng):
            emit_engine("scalar", eng)

        @block.gpsimd
        def _(eng):
            emit_engine("gpsimd", eng)


def build_nc(debug=False):
    nc = bass.Bass("TRN2", target_bir_lowering=False,
                   detect_race_conditions=False)

    x_d = nc.dram_tensor("x", [T, D], F32, kind="ExternalInput").ap()
    wq_d = nc.dram_tensor("Wq", [DO, D], F32, kind="ExternalInput").ap()
    wkv_d = nc.dram_tensor("Wkv", [LR, D], F32, kind="ExternalInput").ap()
    wku_d = nc.dram_tensor("Wku", [DO, LR], F32, kind="ExternalInput").ap()
    wvu_d = nc.dram_tensor("Wvu", [DO, LR], F32, kind="ExternalInput").ap()
    wo_d = nc.dram_tensor("Wo", [D, DO], F32, kind="ExternalInput").ap()
    wg_d = nc.dram_tensor("Wg", [NI, D], F32, kind="ExternalInput").ap()
    wu_d = nc.dram_tensor("Wu", [NI, D], F32, kind="ExternalInput").ap()
    wd_d = nc.dram_tensor("Wd", [D, NI], F32, kind="ExternalInput").ap()
    id_d = nc.dram_tensor("ident", [128, 128], BF16, kind="ExternalInput").ap()
    y_d = nc.dram_tensor("y", [T, D], F32, kind="ExternalOutput").ap()

    sk = "ExternalOutput" if debug else "Internal"
    xnqT_d = nc.dram_tensor("xnqT_d", [D, T], BF16, kind=sk).ap()
    aoT_d = nc.dram_tensor("aoT_d", [DO, T], BF16, kind=sk).ap()
    hqT_d = nc.dram_tensor("hqT_d", [NI, T], BF16, kind=sk).ap()
    wqT_d = nc.dram_tensor("wqT_d", [D, DO], BF16, kind=sk).ap()
    woT_d = nc.dram_tensor("woT_d", [DO, D], BF16, kind=sk).ap()
    wgT_d = nc.dram_tensor("wgT_d", [D, NI], BF16, kind=sk).ap()
    wuT_d = nc.dram_tensor("wuT_d", [D, NI], BF16, kind=sk).ap()
    wdT_d = nc.dram_tensor("wdT_d", [NI, D], BF16, kind=sk).ap()
    if debug:
        xnT_dbg = nc.dram_tensor("xnT_dbg", [D, T], BF16, kind="ExternalOutput").ap()
        kvT_dbg = nc.dram_tensor("kvT_dbg", [LR, T], BF16, kind="ExternalOutput").ap()
        sc_dbg = nc.dram_tensor("sc_dbg", [128, 128], F32, kind="ExternalOutput").ap()

    p = Prog()
    with contextlib.ExitStack() as ctx:
        sbuf = lambda n, s, dt: ctx.enter_context(nc.sbuf_tensor(n, s, dt))
        ident = sbuf("identsb", [128, 128], BF16)
        arA = sbuf("arA", [128, 32768], BF16)
        arB = sbuf("arB", [128, 38912], BF16)
        arC = sbuf("arC", [128, 10240], F32)
        rsc = sbuf("rsc", [128, 3072], BF16)
        t32 = sbuf("t32", [128, 1024], I32)
        smf = sbuf("smf", [128, 272], F32)
        ones_t = sbuf("ones_t", [128, 128], F32)
        wkuT = sbuf("wkuT", [LR, DO], BF16)
        wvuT = sbuf("wvuT", [LR, DO], BF16)
        p1b = sbuf("p1b", [128, 4096], BF16)
        psF = ctx.enter_context(nc.psum_tensor("psF", [128, 3584], F32))
        psT0 = ctx.enter_context(nc.psum_tensor("psT0", [128, 1024], BF16))
        bank = lambda i: psF[:, i * 512:(i + 1) * 512]
        trB = psF[:, 3072:3584].bitcast(BF16)
        BKap = lambda t: psT0[:, 0:1024] if t == 0 else trB
        _tb = [0]

        def nb():
            t = _tb[0]
            _tb[0] ^= 1
            return t
        pv_ps = lambda sub, a, b: psF[:, 1536 + sub * 512 + a: 1536 + sub * 512 + b]

        b_ident, b_ones, b_eps = Buf(), Buf(), Buf()
        b_arA = Buf()
        b_qT, b_kT, b_vA = Buf(), Buf(), Buf()
        b_wqst = [Buf(), Buf()]
        b_pT = [Buf(), Buf(), Buf()]
        b_atto = Buf()
        b_aoTh = [Buf(), Buf()]
        b_wuqT = Buf()
        b_wg8 = [Buf() for _ in range(8)]
        b_xnqst = [Buf(), Buf()]
        b_hqneg = [Buf() for _ in range(4)]
        b_woT = Buf()
        b_hqst = [Buf(), Buf()]
        b_aost = [Buf(), Buf()]
        b_xst = [Buf(), Buf()]
        b_xn32 = [Buf(), Buf()]
        b_h32 = [[Buf() for _ in range(8)] for _ in range(2)]
        b_nn, b_ee, b_qq = Buf(), Buf(), Buf()
        b_sil = [Buf(), Buf(), Buf(), Buf()]
        b_t1 = Buf()
        b_x4 = [Buf(), Buf()]
        b_t32 = Buf()
        b_stp = [[Buf() for _ in range(4)] for _ in range(2)]
        b_wstA = [Buf(), Buf(), Buf()]
        b_sxv2 = [Buf(), Buf()]
        b_sgv, b_suv, b_sdv, b_mu = Buf(), Buf(), Buf(), Buf()
        b_scw3 = [Buf(), Buf(), Buf()]
        b_rsx, b_rsh, b_bvec, b_repl, b_acc = Buf(), Buf(), Buf(), Buf(), Buf()
        b_kvT, b_wkvT, b_wkuT, b_wvuT = Buf(), Buf(), Buf(), Buf()
        b_p1r = [Buf(), Buf(), Buf(), Buf()]
        b_bank = [Buf() for _ in range(7)]
        b_psTh = [Buf(), Buf()]
        b_d = {n: Buf() for n in
               ("xnqT", "aoT", "hqT", "wqT", "woT", "wgT", "wuT", "wdT")}

        rsx = smf[:, 0:32]
        rsh = smf[:, 32:64]
        bvec = smf[:, 64:68]
        repl = smf[:, 68:72]
        stp = [[smf[:, o + i: o + i + 1] for i in range(4)] for o in (72, 76)]
        epsv = smf[:, 80:81]
        sxv2 = [smf[:, 81 + i: 82 + i] for i in range(2)]
        sgv = smf[:, 83:84]
        suv = smf[:, 84:85]
        sdv = smf[:, 85:86]
        scw3 = [smf[:, 86 + i: 87 + i] for i in range(3)]
        mu11 = smf[:, 89:90]
        acc32 = smf[:, 96:112]
        sgall = smf[:, 208:240]
        suall = smf[:, 240:272]
        sdall = acc32
        sdall = smf[:, 112:144]
        b_sdall = Buf()
        b_sgall, b_suall = Buf(), Buf()
        accA_sl = lambda s: smf[:, 112 + s: 113 + s]
        b_accA = Buf()
        wstA = [arC[:, 4096 + b * 1024: 4096 + (b + 1) * 1024] for b in (0, 1, 2)]

        xnT_sl = lambda dc, t0, n: arA[:, dc * T + t0: dc * T + t0 + n]
        wgqT_sl = lambda dc, i0, n: arA[:, dc * NI + i0: dc * NI + i0 + n]
        woT_sl = lambda oc, n0, n: arA[:, oc * D + n0: oc * D + n0 + n]
        hq_st = [arA[:, 16384 + b * 4096: 16384 + (b + 1) * 4096] for b in (0, 1)]
        ao_st = [arA[:, 24576 + b * 2048: 24576 + (b + 1) * 2048] for b in (0, 1)]

        qT = arB[:, 0:8192]
        kT = arB[:, 8192:16384]
        vA = arB[:, 16384:24608]
        wqst = [arB[:, 24608 + b * 2048: 24608 + (b + 1) * 2048] for b in (0, 1)]
        pT = [arB[:, 28704 + b * 512: 28704 + (b + 1) * 512] for b in range(3)]
        att_o = arB[:, 30240:31264]
        wkvT = arB[:, 30240:30752]
        aoTh = [arB[:, 31264 + b * 1024: 31264 + (b + 1) * 1024] for b in (0, 1)]
        kvT = arB[0:LR, 33312:37408]
        wuqT_sl = lambda dc, i0, n: arB[:, dc * NI + i0: dc * NI + i0 + n]
        wdqT_sl = lambda ic, n0, n: arB[:, ic * D + n0: ic * D + n0 + n]
        xnq_st = [arB[:, 32768 + b * 1024: 32768 + (b + 1) * 1024] for b in (0, 1)]
        hq_neg = arB[:, 34816:38912]

        xst = [arC[:, b * 1024:(b + 1) * 1024] for b in (0, 1)]
        xn32p = [arC[:, 2048 + b * 1024: 2048 + (b + 1) * 1024] for b in (0, 1)]
        h32p = [arC[:, b * 4096:(b + 1) * 4096] for b in (0, 1)]
        x4st = [arC[:, 2048 + b * 1024: 2048 + (b + 1) * 1024] for b in (0, 1)]
        out32 = [arC[:, b * 1024:(b + 1) * 1024] for b in (0, 1)]
        nn32 = rsc[:, 0:1024]
        ee32 = rsc[:, 1024:2048]
        qq32 = rsc[:, 2048:3072]
        t1_32 = arC[:, 4096:5120]
        sil32 = [arC[:, 8192 + b * 512: 8192 + (b + 1) * 512] for b in range(4)]

        p.add("sync", lambda e: e.dma_start(out=ident[:], in_=id_d),
              writes=[b_ident], dma=True)
        p.add("vector", lambda e: e.memset(epsv, 1e-5), writes=[b_eps])
        p.add("vector", lambda e: e.memset(ones_t[:, :], 1.0), writes=[b_ones])
        p.add("gpsimd", lambda e: e.dma_start(out=p1b[0:LR, 0:D], in_=wkv_d),
              writes=[b_p1r[0], b_p1r[1], b_p1r[2], b_p1r[3]], dma=True)
        for dc in range(NDC):
            p.add("tensor", lambda e, dc=dc: e.transpose(
                psT0[:, dc * 128: dc * 128 + LR],
                p1b[0:LR, dc * 128:(dc + 1) * 128], ident[0:LR, 0:LR]),
                reads=[b_p1r[dc // 2], b_ident], writes=[b_psTh[0]])
        for dc in range(NDC):
            p.add("vector", lambda e, dc=dc: e.tensor_copy(
                wkvT[:, dc * LR:(dc + 1) * LR], psT0[:, dc * 128: dc * 128 + LR]),
                reads=[b_psTh[0]], writes=[b_wkvT])

        lodB = [arB[:, 8192 + b * 2048: 8192 + (b + 1) * 2048] for b in (0, 1, 2)]
        stgB = [arB[:, 16384 + b * 1024: 16384 + (b + 1) * 1024] for b in range(4)]
        _sg = [0]
        b_ldB = [Buf(), Buf(), Buf()]
        b_stB = [Buf(), Buf(), Buf(), Buf()]
        prep_tiles = []
        for r in range(DO // 128):
            prep_tiles.append(("up", wku_d, wkuT, b_wkuT, r, 0))
        for r in range(DO // 128):
            prep_tiles.append(("up", wvu_d, wvuT, b_wvuT, r, 0))
        for r in range(DO // 128):
            prep_tiles.append(("pl", wq_d, wqT_d, b_d["wqT"], r, 1024))
        for r in range(D // 128):
            prep_tiles.append(("pl", wo_d, woT_d, b_d["woT"], r, 2048))
        _pc = [0]
        _pd = [0]

        def emit_prep_dma():
            if _pd[0] >= len(prep_tiles):
                return
            i = _pd[0]
            _pd[0] += 1
            kind, srct, dst, b_dst, r, cols = prep_tiles[i]
            par = i % 3
            n = LR if kind == "up" else cols
            p.add("gpsimd", lambda e, r=r, srct=srct, par=par, n=n: e.dma_start(
                out=lodB[par][:, 0:n], in_=srct[r * 128:(r + 1) * 128, :]),
                writes=[b_ldB[par]], dma=True)

        def emit_prep_comp():
            if _pc[0] >= _pd[0] or _pc[0] >= len(prep_tiles):
                return
            i = _pc[0]
            _pc[0] += 1
            kind, srct, dst, b_dst, r, cols = prep_tiles[i]
            par = i % 3
            if kind == "up":
                tb = nb()
                p.add("tensor", lambda e, par=par, tb=tb: e.transpose(
                    BKap(tb)[0:LR, 0:128], lodB[par][:, 0:LR],
                    ident[:]),
                    reads=[b_ldB[par], b_ident], writes=[b_psTh[tb]])
                p.add("vector", lambda e, r=r, dst=dst, tb=tb: e.tensor_copy(
                    dst[:, r * 128:(r + 1) * 128],
                    BKap(tb)[0:LR, 0:128]),
                    reads=[b_psTh[tb]], writes=[b_dst])
                return
            for g in range(cols // 1024):
                tb = nb()
                for dc in range(8):
                    cc = g * 8 + dc
                    p.add("tensor", lambda e, cc=cc, dc=dc, par=par, tb=tb: e.transpose(
                        BKap(tb)[:, dc * 128:(dc + 1) * 128],
                        lodB[par][:, cc * 128:(cc + 1) * 128], ident[:]),
                        reads=[b_ldB[par], b_ident], writes=[b_psTh[tb]])
                sg = _sg[0] % 4
                _sg[0] += 1
                p.add("vector", lambda e, sg=sg, tb=tb: e.tensor_copy(
                    stgB[sg][:, :], BKap(tb)[:, :]),
                    reads=[b_psTh[tb]], writes=[b_stB[sg]])
                p.add("sync", lambda e, r=r, g=g, dst=dst, sg=sg: e.dma_start(
                    out=dst.rearrange("(cc pp) o -> pp cc o", pp=128)[:, g * 8:(g + 1) * 8, r * 128:(r + 1) * 128],
                    in_=stgB[sg][:, :].rearrange("pp (cc t) -> pp cc t", t=128)),
                    reads=[b_stB[sg]], writes=[b_dst], dma=True)

        def emit_wqst(h):
            wb = h % 2
            p.add("scalar" if h > 0 else "sync", lambda e, h=h, wb=wb: e.dma_start(
                out=wqst[wb][:, :].rearrange("pp (dc o) -> pp dc o", o=256),
                in_=wqT_d.rearrange("(dc pp) o -> pp dc o", pp=128)[:, :, h * 256:(h + 1) * 256]),
                reads=[b_d["wqT"]], writes=[b_wqst[wb]], dma=True)

        def emit_qgrp(h, oc, ts_):
            wb = h % 2
            i6 = oc * 8 + ts_
            g = (1 + i6 % 5) if h == 0 else i6 % 6
            vec = (i6 % 2 == 0) if h == 0 else True
            for dc in range(NDC):
                p.add("tensor", lambda e, oc=oc, ts_=ts_, dc=dc, g=g, wb=wb: e.matmul(
                    bank(g),
                    wqst[wb][:, dc * 256 + oc * 128: dc * 256 + (oc + 1) * 128],
                    xnT_sl(dc, ts_ * 512, 512),
                    start=(dc == 0), stop=(dc == NDC - 1)),
                    reads=[b_wqst[wb], b_arA], writes=[b_bank[g]])
            p.add("vector" if vec else "scalar",
                  (lambda e, oc=oc, ts_=ts_, g=g: e.tensor_copy(
                      qT[:, oc * T + ts_ * 512: oc * T + (ts_ + 1) * 512], bank(g)))
                  if vec else
                  (lambda e, oc=oc, ts_=ts_, g=g: e.activation(
                      qT[:, oc * T + ts_ * 512: oc * T + (ts_ + 1) * 512], bank(g),
                      AF.Identity, bias=0.0, scale=1.0)),
                  reads=[b_bank[g]], writes=[b_qT])

        def emit_kgrp(h, oc, ts_, first=False):
            i6 = oc * 8 + ts_
            g = (1 + i6 % 5) if h == 0 else i6 % 6
            vec = (i6 % 2 == 0)
            p.add("tensor", lambda e, oc=oc, ts_=ts_, g=g, h=h: e.matmul(
                bank(g), wkuT[:, h * 256 + oc * 128: h * 256 + (oc + 1) * 128],
                kvT[:, ts_ * 512:(ts_ + 1) * 512], start=True, stop=True),
                reads=[b_wkuT, b_kvT], writes=[b_bank[g]])
            p.add("vector" if vec else "scalar",
                  (lambda e, oc=oc, ts_=ts_, g=g: e.tensor_copy(
                      kT[:, oc * T + ts_ * 512: oc * T + (ts_ + 1) * 512], bank(g)))
                  if vec else
                  (lambda e, oc=oc, ts_=ts_, g=g: e.activation(
                      kT[:, oc * T + ts_ * 512: oc * T + (ts_ + 1) * 512], bank(g),
                      AF.Identity, bias=0.0, scale=1.0)),
                  reads=[b_bank[g]], writes=[b_kT],
                  extra=((b_ldB[0].handover() + b_ldB[1].handover() +
                          b_ldB[2].handover())
                         if (first or (h == 0 and oc == 1 and ts_ <= 3))
                         else ()))

        def emit_vmemset(h):
            p.add("vector", lambda e: e.memset(vA[:, :], 1.0), writes=[b_vA],
                  extra=((b_stB[0].handover() + b_stB[1].handover() +
                          b_stB[2].handover() + b_stB[3].handover())
                         if h == 0 else ()))

        def emit_vchunk(h, tk):
            g = (1 + tk % 5) if h == 0 else tk % 6
            vec = (tk % 2 == 0)
            p.add("tensor", lambda e, tk=tk, g=g, h=h: e.matmul(
                bank(g)[:, 0:256], kvT[:, tk * 128:(tk + 1) * 128],
                wvuT[:, h * 256:(h + 1) * 256], start=True, stop=True),
                reads=[b_kvT, b_wvuT], writes=[b_bank[g]])
            p.add("vector" if vec else "scalar",
                  (lambda e, tk=tk, g=g: e.tensor_copy(
                      vA[:, tk * 257: tk * 257 + 256], bank(g)[:, 0:256]))
                  if vec else
                  (lambda e, tk=tk, g=g: e.activation(
                      vA[:, tk * 257: tk * 257 + 256], bank(g)[:, 0:256],
                      AF.Identity, bias=0.0, scale=1.0)),
                  reads=[b_bank[g]], writes=[b_vA])

        def round5(src_ap, b_src, dst_ap, b_dst, sx, b_sx):
            p.add("vector", lambda e: e.tensor_scalar(
                t32[:, :].bitcast(F32), src_ap, sx, CM, op0=ALU.mult, op1=ALU.add),
                reads=[b_src, b_sx], writes=[b_t32])
            p.add("vector", lambda e: e.tensor_scalar(
                nn32[:, :], t32[:, :].bitcast(F32), -CM, None, op0=ALU.add),
                reads=[b_t32], writes=[b_nn])
            p.add("vector", lambda e: e.scalar_tensor_tensor(
                ee32[:, :], src_ap, sx, nn32[:, :], op0=ALU.mult, op1=ALU.subtract),
                reads=[b_src, b_sx, b_nn], writes=[b_ee])
            p.add("vector", lambda e: e.scalar_tensor_tensor(
                qq32[:, :], ee32[:, :], 0.5, nn32[:, :], op0=ALU.is_gt, op1=ALU.add),
                reads=[b_ee, b_nn], writes=[b_qq])
            p.add("vector", lambda e: e.scalar_tensor_tensor(
                dst_ap, ee32[:, :], -0.5, qq32[:, :], op0=ALU.is_lt, op1=ALU.subtract),
                reads=[b_ee, b_qq], writes=[b_dst])

        def round5_2(src_ap, b_srcs, dst_ap, b_dst, sx, b_sx, eng12="vector"):
            p.add(eng12, lambda e: e.tensor_scalar(
                t32[:, :].bitcast(F32), src_ap, sx, CM, op0=ALU.mult, op1=ALU.add),
                reads=b_srcs + [b_sx], writes=[b_t32])
            p.add(eng12, lambda e: e.tensor_scalar(
                nn32[:, :], t32[:, :].bitcast(F32), -CM, None, op0=ALU.add),
                reads=[b_t32], writes=[b_nn])
            p.add("vector", lambda e: e.scalar_tensor_tensor(
                ee32[:, :], src_ap, sx, nn32[:, :], op0=ALU.mult, op1=ALU.subtract),
                reads=b_srcs + [b_sx, b_nn], writes=[b_ee])
            p.add("vector", lambda e: e.scalar_tensor_tensor(
                qq32[:, :], ee32[:, :], 0.5, nn32[:, :], op0=ALU.is_gt, op1=ALU.add),
                reads=[b_ee, b_nn], writes=[b_qq])
            p.add("vector", lambda e: e.scalar_tensor_tensor(
                dst_ap, ee32[:, :], -0.5, qq32[:, :], op0=ALU.is_lt, op1=ALU.subtract),
                reads=[b_ee, b_qq], writes=[b_dst])

        def actscale(src_small, b_src, st, b_st, sx, b_sx):
            p.add("vector", lambda e: e.tensor_scalar(
                st[1], src_small, 1e-5, None, op0=ALU.max),
                reads=[b_src], writes=[b_st[1]])
            p.add("vector", lambda e: e.reciprocal(st[1], st[1]),
                  reads=[b_st[1]], writes=[b_st[1]])
            p.add("vector", lambda e: e.tensor_scalar(
                sx, st[1], 127.0, None, op0=ALU.mult),
                reads=[b_st[1]], writes=[b_sx])

        passA = []
        for wsrc, rows, cols, base in ((wg_d, NI, D, 0), (wu_d, NI, D, 32),
                                       (wd_d, D, NI, 64)):
            ncc = cols // 1024
            for i in range(rows // 128 * ncc):
                r, cb = divmod(i, ncc)
                passA.append((wsrc, r, cb, base + i))

        _pa_d = [0]
        _pa_c = [0]

        def emit_passA_dma():
            if _pa_d[0] >= len(passA):
                return
            j = _pa_d[0]
            _pa_d[0] += 1
            wsrc, r, cb, slot = passA[j]
            ab = j % 3
            p.add("scalar", lambda e, wsrc=wsrc, r=r, cb=cb, ab=ab: e.dma_start(
                out=wstA[ab][:, :], in_=wsrc[r * 128:(r + 1) * 128, cb * 1024:(cb + 1) * 1024]),
                writes=[b_wstA[ab]], dma=True)

        def emit_passA_abs():
            if _pa_c[0] >= _pa_d[0] or _pa_c[0] >= len(passA):
                return
            j = _pa_c[0]
            _pa_c[0] += 1
            wsrc, r, cb, slot = passA[j]
            ab = j % 3
            p.add("scalar", lambda e, slot=slot, ab=ab: e.activation(
                wstA[ab][:, :], wstA[ab][:, :], AF.Abs, accum_out=accA_sl(slot)),
                reads=[b_wstA[ab]], writes=[b_wstA[ab], b_accA])

        def emit_xnq_out(tc2):
            bq = tc2 % 2
            rq2 = b_p1r[bq]
            rs2 = b_p1r[2 + bq]
            tb = nb()
            for dc in range(NDC):
                p.add("tensor", lambda e, dc=dc, bq=bq, tb=tb: e.transpose(
                    BKap(tb)[:, dc * 128:(dc + 1) * 128],
                    p1b[:, bq * 1024 + dc * 128: bq * 1024 + (dc + 1) * 128],
                    ident[:]),
                    reads=[rq2, b_ident], writes=[b_psTh[tb]])
            p.add("scalar", lambda e, bq=bq, tb=tb: e.activation(
                p1b[:, (2 + bq) * 1024:(3 + bq) * 1024], BKap(tb)[:, :],
                AF.Identity, bias=0.0, scale=1.0),
                reads=[b_psTh[tb]], writes=[rs2])
            p.add("sync", lambda e, tc2=tc2, bq=bq: e.dma_start(
                out=xnqT_d.rearrange("(dc pp) t -> pp dc t", pp=128)[:, :, tc2 * 128:(tc2 + 1) * 128],
                in_=p1b[:, (2 + bq) * 1024:(3 + bq) * 1024].rearrange("pp (dc t) -> pp dc t", t=128)),
                reads=[rs2], writes=[b_d["xnqT"]], dma=True)

        def emit_kv(tc2):
            for dc in range(NDC):
                p.add("tensor", lambda e, dc=dc, tc2=tc2: e.matmul(
                    bank(0)[0:LR, 0:128], wkvT[:, dc * LR:(dc + 1) * LR],
                    xnT_sl(dc, tc2 * 128, 128),
                    start=(dc == 0), stop=(dc == NDC - 1)),
                    reads=[b_wkvT, b_arA], writes=[b_bank[0]])
            p.add("scalar", lambda e, tc2=tc2: e.activation(
                kvT[:, tc2 * 128:(tc2 + 1) * 128], bank(0)[0:LR, 0:128],
                AF.Identity, bias=0.0, scale=1.0),
                reads=[b_bank[0]], writes=[b_kvT])

        pre_q = [(oc_, ts__) for ts__ in range(8) for oc_ in range(2)]
        pre_k = [(oc_, ts__) for ts__ in range(8) for oc_ in range(2)]
        pre_v = list(range(NTC))
        for _ in range(3):
            emit_passA_dma()
        for _ in range(3):
            emit_prep_dma()
        for tc in range(NTC):
            bb = tc % 2
            xn32 = xn32p[bb]
            b_xn = b_xn32[bb]
            st, b_st = stp[bb], b_stp[bb]
            sxv, b_sxv = sxv2[bb], b_sxv2[bb]
            rq = b_p1r[bb]
            rs = b_p1r[2 + bb]
            if tc == 20:
                emit_wqst(0)
                emit_vmemset(0)
            if tc >= 21:
                for _ in range(2):
                    if pre_q and 4 * pre_q[0][1] + 3 < tc:
                        oc_, ts__ = pre_q.pop(0)
                        emit_qgrp(0, oc_, ts__)
                for _ in range(2):
                    if pre_k and 4 * pre_k[0][1] + 4 < tc:
                        oc_, ts__ = pre_k.pop(0)
                        emit_kgrp(0, oc_, ts__, first=(oc_ == 0 and ts__ == 0))
                for _ in range(4):
                    if pre_v and pre_v[0] + 1 < tc:
                        emit_vchunk(0, pre_v.pop(0))
            p.add("sync", lambda e, tc=tc, bb=bb: e.dma_start(
                out=xst[bb][:, :], in_=x_d[tc * 128:(tc + 1) * 128, :]),
                writes=[b_xst[bb]], dma=True)
            p.add("vector", lambda e, bb=bb, st=st: e.reduce_sum(
                st[0], xst[bb][:, :], axis=AX.X),
                reads=[b_xst[bb]], writes=[b_st[0]])
            p.add("scalar", lambda e, bb=bb, xn32=xn32, st=st: e.activation(
                xn32[:, :], xst[bb][:, :], AF.Square, accum_out=st[1]),
                reads=[b_xst[bb]], writes=[b_xn, b_st[1]])
            p.add("vector", lambda e, st=st: e.tensor_scalar(
                st[0], st[0], 1.0 / D, None, op0=ALU.mult),
                reads=[b_st[0]], writes=[b_st[0]])
            p.add("vector", lambda e, st=st: e.tensor_scalar(
                st[1], st[1], 1.0 / D, None, op0=ALU.mult),
                reads=[b_st[1]], writes=[b_st[1]])
            p.add("vector", lambda e, st=st: e.scalar_tensor_tensor(
                st[2], st[0], st[0], st[1], op0=ALU.mult, op1=ALU.subtract),
                reads=[b_st[0], b_st[1]], writes=[b_st[2]])
            p.add("scalar", lambda e, st=st: e.activation(
                st[3], st[2], AF.Sqrt, bias=epsv, scale=-1.0),
                reads=[b_st[2], b_eps], writes=[b_st[3]])
            p.add("vector", lambda e, st=st: e.reciprocal(st[3], st[3]),
                  reads=[b_st[3]], writes=[b_st[3]])
            p.add("vector", lambda e, st=st: e.tensor_scalar(
                st[2], st[3], st[0], -1.0, op0=ALU.mult, op1=ALU.mult),
                reads=[b_st[3], b_st[0]], writes=[b_st[2]])
            p.add("scalar", lambda e, bb=bb, st=st: e.activation(
                p1b[:, (2 + bb) * 1024:(3 + bb) * 1024], xst[bb][:, :],
                AF.Identity, bias=st[2], scale=st[3]),
                reads=[b_xst[bb], b_st[2], b_st[3]], writes=[rs])
            p.add("scalar", lambda e, bb=bb, xn32=xn32, st=st: e.activation(
                xn32[:, :], xst[bb][:, :], AF.Identity, bias=st[2], scale=st[3]),
                reads=[b_xst[bb], b_st[2], b_st[3]], writes=[b_xn])
            for _ in range(3):
                emit_passA_abs()
            for _ in range(3):
                emit_prep_comp()
            if tc >= 1:
                emit_kv(tc - 1)
            p.add("vector", lambda e, xn32=xn32, st=st: e.tensor_reduce(
                st[0], xn32[:, :], axis=AX.X, op=ALU.max, apply_absolute_value=True),
                reads=[b_xn], writes=[b_st[0]])
            p.add("vector", lambda e, tc=tc, st=st: e.tensor_scalar(
                rsx[:, tc:tc + 1], st[0], 1e-5, 1.0 / 127.0, op0=ALU.max, op1=ALU.mult),
                reads=[b_st[0]], writes=[b_rsx])
            actscale(st[0], b_st[0], st, b_st, sxv, b_sxv)
            tb = nb()
            for dc in range(NDC):
                p.add("tensor", lambda e, dc=dc, bb=bb, tb=tb: e.transpose(
                    BKap(tb)[:, dc * 128:(dc + 1) * 128],
                    p1b[:, (2 + bb) * 1024 + dc * 128: (2 + bb) * 1024 + (dc + 1) * 128],
                    ident[:]),
                    reads=[rs, b_ident], writes=[b_psTh[tb]])
            p.add("vector", lambda e, tc=tc, tb=tb: e.tensor_copy(
                arA[:, :].rearrange("pp (dc t) -> pp dc t", t=T)[:, :, tc * 128:(tc + 1) * 128],
                BKap(tb)[:, :].rearrange("pp (dc t) -> pp dc t", t=128)),
                reads=[b_psTh[tb]], writes=[b_arA])
            if tc >= 1:
                emit_xnq_out(tc - 1)
            round5(xn32[:, :], b_xn, p1b[:, bb * 1024:(bb + 1) * 1024], rq, sxv, b_sxv)
            for _ in range(3):
                emit_passA_dma()
            for _ in range(3):
                emit_prep_dma()
        emit_xnq_out(NTC - 1)
        emit_kv(NTC - 1)
        while _pa_c[0] < len(passA):
            emit_passA_abs()
            emit_passA_dma()
        while _pc[0] < len(prep_tiles):
            emit_prep_comp()
            emit_prep_dma()
        for oc_, ts__ in pre_q:
            emit_qgrp(0, oc_, ts__)
        for oc_, ts__ in pre_k:
            emit_kgrp(0, oc_, ts__, first=(oc_ == 0 and ts__ == 0))
        for tk_ in pre_v:
            emit_vchunk(0, tk_)
        if debug:
            p.add_final_wait(p.add("sync", lambda e: e.dma_start(
                out=xnT_dbg.rearrange("(dc pp) t -> pp dc t", pp=128),
                in_=arA[:, :].rearrange("pp (dc t) -> pp dc t", t=T)),
                reads=[b_arA], dma=True))
            p.add_final_wait(p.add("sync", lambda e: e.dma_start(
                out=kvT_dbg, in_=kvT[:, :]), reads=[b_kvT], dma=True))

        for bslot, (rows, cols, base) in enumerate(((NI, D, 0), (NI, D, 32), (D, NI, 64))):
            st = stp[0]
            p.add("vector", lambda e, base=base, st=st: e.reduce_sum(
                st[0], smf[:, 112 + base: 112 + base + 32], axis=AX.X),
                reads=[b_accA], writes=[b_stp[0][0]])
            p.add("tensor", lambda e, st=st: e.matmul(
                bank(5)[0:1, 0:1], ones_t[:, 0:1], st[0], start=True, stop=True),
                reads=[b_ones, b_stp[0][0]], writes=[b_bank[5]])
            p.add("vector", lambda e, rows=rows, cols=cols: e.tensor_scalar(
                mu11[0:1, :], bank(5)[0:1, 0:1], 1.0 / (rows * cols), 1e-5,
                op0=ALU.mult, op1=ALU.max),
                reads=[b_bank[5]], writes=[b_mu])
            p.add("tensor", lambda e: e.matmul(
                bank(5)[:, 0:1], ones_t[0:1, :], mu11[0:1, :], start=True, stop=True),
                reads=[b_ones, b_mu], writes=[b_bank[5]])
            p.add("vector", lambda e, bslot=bslot: e.tensor_copy(
                bvec[:, bslot:bslot + 1], bank(5)[:, 0:1]),
                reads=[b_bank[5]], writes=[b_bvec])
            p.add("vector", lambda e, bslot=bslot: e.reciprocal(
                scw3[bslot], bvec[:, bslot:bslot + 1]),
                reads=[b_bvec], writes=[b_scw3[bslot]])

        passB = []
        for bslot, (wsrc, rows, cols, dst, bd) in enumerate((
                (wg_d, NI, D, wgT_d, b_d["wgT"]), (wu_d, NI, D, wuT_d, b_d["wuT"]),
                (wd_d, D, NI, wdT_d, b_d["wdT"]))):
            ncc = cols // 1024
            for i in range(rows // 128 * ncc):
                r, cb = divmod(i, ncc)
                passB.append((wsrc, r, cb, dst, bd, bslot))

        def emit_passB1(j):
            wsrc, r, cb, dst, bd, bslot = passB[j]
            bb = j % 2
            ro = b_p1r[bb]
            p.add("gpsimd", lambda e, r=r, cb=cb, bb=bb, wsrc=wsrc: e.dma_start(
                out=xst[bb][:, :], in_=wsrc[r * 128:(r + 1) * 128, cb * 1024:(cb + 1) * 1024]),
                writes=[b_xst[bb]], dma=True)
            p.add("vector", lambda e, bb=bb, bslot=bslot: e.tensor_scalar(
                nn32[:, :], xst[bb][:, :], scw3[bslot], -0.5, op0=ALU.mult, op1=ALU.is_lt),
                reads=[b_xst[bb], b_scw3[bslot]], writes=[b_nn])
            p.add("vector", lambda e, bb=bb, bslot=bslot: e.tensor_scalar(
                ee32[:, :], xst[bb][:, :], scw3[bslot], 0.5, op0=ALU.mult, op1=ALU.is_gt),
                reads=[b_xst[bb], b_scw3[bslot]], writes=[b_ee])
            p.add("vector", lambda e, bb=bb: e.tensor_tensor(
                p1b[:, bb * 1024:(bb + 1) * 1024], nn32[:, :], ee32[:, :],
                op=ALU.subtract),
                reads=[b_nn, b_ee], writes=[ro])

        trB_guard = b_psTh[1].handover()
        for h in range(NH):
            if h > 0:
                emit_wqst(h)
                for oc in range(2):
                    for ts_ in range(8):
                        emit_qgrp(h, oc, ts_)
                for oc in range(2):
                    for ts_ in range(8):
                        emit_kgrp(h, oc, ts_)
            if h == NH - 1:
                arA_g = b_arA.handover()
                for isl_ in range(8):
                    p.add("sync", lambda e, isl_=isl_: e.dma_start(
                        out=arA[:, :].rearrange("pp (dc i) -> pp dc i", i=NI)[:, :, isl_ * 512:(isl_ + 1) * 512],
                        in_=wgT_d.rearrange("(dc pp) i -> pp dc i", pp=128)[:, :, isl_ * 512:(isl_ + 1) * 512]),
                        reads=[b_d["wgT"]], writes=[b_wg8[isl_]],
                        extra=arA_g, dma=True)
            if h > 0:
                emit_vmemset(h)
                for tk in range(NTC):
                    emit_vchunk(h, tk)
            def emit_qk(bi, c):
                g = c % 3
                for d2 in range(2):
                    p.add("tensor", lambda e, bi=bi, c=c, g=g, d2=d2: e.matmul(
                        bank(g), kT[:, d2 * T + c * 128: d2 * T + (c + 1) * 128],
                        qT[:, d2 * T + bi * 512: d2 * T + (bi + 1) * 512],
                        start=(d2 == 0), stop=(d2 == 1)),
                        reads=[b_kT, b_qT], writes=[b_bank[g]])
                p.add("scalar", lambda e, g=g, c=c: e.activation(
                    pT[c % 3][:, :], bank(g), AF.Exp, bias=0.0, scale=ISCL),
                    reads=[b_bank[g]], writes=[b_pT[c % 3]])

            def emit_pv(c):
                for sub in range(4):
                    p.add("tensor", lambda e, c=c, sub=sub: e.matmul(
                        pv_ps(sub, 0, 257), pT[c % 3][:, sub * 128:(sub + 1) * 128],
                        vA[:, c * 257:(c + 1) * 257],
                        start=(c == 0), stop=(c == NTC - 1)),
                        reads=[b_pT[c % 3], b_vA], writes=[b_bank[3 + sub]],
                        extra=(trB_guard if h == 0 and c == 0 and sub == 3 else ()))

            def finalize_recips():
                for sub in range(4):
                    p.add("vector", lambda e, sub=sub: e.reciprocal(
                        repl[:, sub:sub + 1], pv_ps(sub, 256, 257)),
                        reads=[b_bank[3 + sub]], writes=[b_repl])

            def finalize_atto(bi):
                for sub in range(4):
                    p.add("scalar", lambda e, sub=sub: e.activation(
                        att_o[:, sub * 256:(sub + 1) * 256], pv_ps(sub, 0, 256),
                        AF.Identity, scale=repl[:, sub:sub + 1]),
                        reads=[b_bank[3 + sub], b_repl], writes=[b_atto],
                        extra=(b_wkvT.handover()
                               if h == 0 and bi == 0 and sub == 0 else ()))

            def trans_tasks(bi):
                pb = 0 if (h == NH - 1 and bi == 7) else bi % 2
                for half in range(2):
                    for sub in range(4):
                        yield lambda sub=sub, half=half: p.add(
                            "tensor", lambda e, sub=sub, half=half: e.transpose(
                                psT0[:, (half * 4 + sub) * 128:(half * 4 + sub + 1) * 128],
                                att_o[:, sub * 256 + half * 128: sub * 256 + half * 128 + 128],
                                ident[:]),
                            reads=[b_atto, b_ident], writes=[b_psTh[0]])

                def cpdma(pb=pb, bi=bi):
                    p.add("vector", lambda e, pb=pb: e.tensor_copy(
                        aoTh[pb][:, :], psT0[:, 0:1024]),
                        reads=[b_psTh[0]], writes=[b_aoTh[pb]])
                    p.add("sync", lambda e, h=h, bi=bi, pb=pb: e.dma_start(
                        out=aoT_d[h * 256:(h + 1) * 256, bi * 512:(bi + 1) * 512]
                        .rearrange("(ch pp) t -> pp ch t", pp=128),
                        in_=aoTh[pb][:, :].rearrange("pp (ch t) -> pp ch t", t=512)),
                        reads=[b_aoTh[pb]], writes=[b_d["aoT"]], dma=True)
                yield cpdma
                gi = h * 8 + bi
                for j in range(gi * 3 // 2, min((gi + 1) * 3 // 2, len(passB))):
                    for dc in range(8):
                        yield lambda j=j, dc=dc: _passB2_tr(j, dc)
                    yield lambda j=j: _passB2_fin(j)

            def _passB2_tr(j, dc):
                wsrc, r, cb, dst, bd, bslot = passB[j]
                bb = j % 2
                ro = b_p1r[bb]
                p.add("tensor", lambda e, dc=dc, bb=bb: e.transpose(
                    psT0[:, dc * 128:(dc + 1) * 128],
                    p1b[:, bb * 1024 + dc * 128: bb * 1024 + (dc + 1) * 128], ident[:]),
                    reads=[ro, b_ident], writes=[b_psTh[0]])

            def _passB2_fin(j):
                wsrc, r, cb, dst, bd, bslot = passB[j]
                bb = j % 2
                rsb = b_p1r[2 + bb]
                p.add("vector", lambda e, bb=bb: e.tensor_copy(
                    p1b[:, (2 + bb) * 1024:(3 + bb) * 1024], psT0[:, 0:1024]),
                    reads=[b_psTh[0]], writes=[rsb])
                p.add("gpsimd", lambda e, r=r, cb=cb, dst=dst, bb=bb: e.dma_start(
                    out=dst.rearrange("(cc pp) o -> pp cc o", pp=128)[:, cb * 8:(cb + 1) * 8, r * 128:(r + 1) * 128],
                    in_=p1b[:, (2 + bb) * 1024:(3 + bb) * 1024].rearrange("pp (cc t) -> pp cc t", t=128)),
                    reads=[rsb], writes=[bd], dma=True)

            for bi in range(8):
                tq = list(trans_tasks(bi - 1)) if bi >= 1 else []
                ti = 0
                for c in range(NTC):
                    emit_qk(bi, c)
                    if c == 1 and bi >= 1:
                        finalize_atto(bi - 1)
                    if c >= 4:
                        for _ in range(2):
                            if ti < len(tq):
                                tq[ti]()
                                ti += 1
                    if c >= 2:
                        emit_pv(c - 2)
                while ti < len(tq):
                    tq[ti]()
                    ti += 1
                emit_pv(NTC - 2)
                emit_pv(NTC - 1)
                finalize_recips()
                if bi == 7:
                    finalize_atto(7)
                gi0 = h * 8 + bi
                for j in range(gi0 * 3 // 2, min((gi0 + 1) * 3 // 2, len(passB))):
                    emit_passB1(j)
            for fn in trans_tasks(7):
                fn()
        tok_wu = None
        b_wu8h = [[Buf(), Buf()] for _ in range(8)]
        xnq_guard = b_kvT.handover() + b_aoTh[1].handover()
        guard_wu = (b_qT.handover() + b_kT.handover() + b_vA.handover() +
                    b_wqst[0].handover() + b_wqst[1].handover() +
                    b_pT[0].handover() + b_pT[1].handover() + b_pT[2].handover() +
                    b_kvT.handover() + b_aoTh[0].handover() + b_aoTh[1].handover() +
                    b_atto.handover())
        def emit_wu_loads():
            tok = None
            for isl in range(8):
                tok = p.add("scalar" if isl % 2 == 0 else "sync",
                            lambda e, isl=isl: e.dma_start(
                    out=arB[:, 0:32768].rearrange("pp (dc i) -> pp dc i", i=NI)[:, :, isl * 512:(isl + 1) * 512],
                    in_=wuT_d.rearrange("(dc pp) i -> pp dc i", pp=128)[:, :, isl * 512:(isl + 1) * 512]),
                    reads=[b_d["wuT"]], writes=[b_wu8h[isl][0]],
                    extra=guard_wu, dma=True)
            return tok
        h32_guard = (b_xst[0].handover() + b_xst[1].handover() +
                     b_xn32[0].handover() + b_xn32[1].handover() +
                     b_wstA[0].handover() + b_wstA[1].handover() +
                     b_wstA[2].handover())

        bank6_guard = b_bank[6].handover()
        _p3b1 = [False]

        def emit_trans_group(tc, g):
            tb = g % 2
            ex = ()
            if tb == 1 and not _p3b1[0]:
                _p3b1[0] = True
                ex = bank6_guard
            for dc in range(8):
                ic = g * 8 + dc
                p.add("tensor", lambda e, ic=ic, dc=dc, tb=tb: e.transpose(
                    BKap(tb)[:, dc * 128:(dc + 1) * 128],
                    hq_neg[:, ic * 128:(ic + 1) * 128], ident[:]),
                    reads=[b_hqneg[g], b_ident], writes=[b_psTh[tb]],
                    extra=(ex if dc == 0 else ()))
            p.add("scalar", lambda e, g=g, tb=tb: e.activation(
                p1b[:, g * 1024:(g + 1) * 1024], BKap(tb)[:, :],
                AF.Identity, bias=0.0, scale=1.0),
                reads=[b_psTh[tb]], writes=[b_p1r[g]])

        def emit_hq_dma(tc):
            p.add("sync", lambda e, tc=tc: e.dma_start(
                out=hqT_d.rearrange("(ic pp) t -> pp ic t", pp=128)[:, :, tc * 128:(tc + 1) * 128],
                in_=p1b[:, :].rearrange("pp (ic t) -> pp ic t", t=128)),
                reads=[b_p1r[0], b_p1r[1], b_p1r[2], b_p1r[3]],
                writes=[b_d["hqT"]], dma=True)

        def xnq_load(tc):
            bb = tc % 2
            p.add("sync", lambda e, tc=tc, bb=bb: e.dma_start(
                out=xnq_st[bb][:, :].rearrange("pp (dc t) -> pp dc t", t=128),
                in_=xnqT_d.rearrange("(dc pp) t -> pp dc t", pp=128)[:, :, tc * 128:(tc + 1) * 128]),
                reads=[b_d["xnqT"]], writes=[b_xnqst[bb]],
                extra=(xnq_guard if tc < 2 else ()), dma=True)

        p.add("vector", lambda e: e.tensor_scalar(
            sgall[:, :], rsx[:, :], bvec[:, 0:1], None, op0=ALU.mult),
            reads=[b_rsx, b_bvec], writes=[b_sgall])
        p.add("vector", lambda e: e.tensor_scalar(
            suall[:, :], rsx[:, :], bvec[:, 1:2], None, op0=ALU.mult),
            reads=[b_rsx, b_bvec], writes=[b_suall])
        xnq_load(0)
        xnq_load(1)
        tok_wu = emit_wu_loads()
        for b_ in b_hqneg:
            b_.w = tok_wu
        for tc in range(NTC):
            bb = tc % 2
            h32 = h32p[bb]
            b_hs = b_h32[bb]
            st, b_st = stp[bb], b_stp[bb]
            sxv, b_sxv = sxv2[bb], b_sxv2[bb]
            for isl in range(8):
                gg, gu = isl % 2, 2 + isl % 2
                for dc in range(NDC):
                    p.add("tensor", lambda e, isl=isl, dc=dc, gg=gg, bb=bb: e.matmul(
                        bank(gg), xnq_st[bb][:, dc * 128:(dc + 1) * 128],
                        wgqT_sl(dc, isl * 512, 512),
                        start=(dc == 0), stop=(dc == NDC - 1)),
                        reads=[b_xnqst[bb], b_wg8[isl]], writes=[b_bank[gg]])
                for dc in range(NDC):
                    p.add("tensor", lambda e, isl=isl, dc=dc, gu=gu, bb=bb: e.matmul(
                        bank(gu), xnq_st[bb][:, dc * 128:(dc + 1) * 128],
                        wuqT_sl(dc, isl * 512, 512),
                        start=(dc == 0), stop=(dc == NDC - 1)),
                        reads=[b_xnqst[bb], b_wu8h[isl][0]],
                        writes=[b_bank[gu]])
                p.add("scalar", lambda e, isl=isl, gu=gu, h32=h32, tc=tc: e.activation(
                    h32[:, isl * 512:(isl + 1) * 512], bank(gu), AF.Identity,
                    scale=suall[:, tc:tc + 1]),
                    reads=[b_bank[gu], b_suall], writes=[b_hs[isl]],
                    extra=(h32_guard if tc < 2 and isl == 0 else ()))
                p.add("scalar", lambda e, gg=gg, isl=isl, tc=tc: e.activation(
                    sil32[isl % 4][:, :], bank(gg), AF.Silu, scale=sgall[:, tc:tc + 1]),
                    reads=[b_bank[gg], b_sgall], writes=[b_sil[isl % 4]])
                p.add("gpsimd", lambda e, isl=isl, h32=h32: e.tensor_tensor(
                    h32[:, isl * 512:(isl + 1) * 512], sil32[isl % 4][:, :],
                    h32[:, isl * 512:(isl + 1) * 512], op=ALU.mult),
                    reads=[b_sil[isl % 4]], writes=[b_hs[isl]])
                p.add("vector", lambda e, isl=isl, h32=h32, bb=bb: e.tensor_reduce(
                    acc32[:, bb * 8 + isl: bb * 8 + isl + 1],
                    h32[:, isl * 512:(isl + 1) * 512], axis=AX.X, op=ALU.max,
                    apply_absolute_value=True),
                    reads=[b_hs[isl]], writes=[b_acc])
            p.add("vector", lambda e, st=st, bb=bb: e.tensor_reduce(
                st[0], acc32[:, bb * 8: bb * 8 + 8], axis=AX.X, op=ALU.max),
                reads=[b_acc], writes=[b_st[0]])
            p.add("vector", lambda e, tc=tc, st=st: e.tensor_scalar(
                rsh[:, tc:tc + 1], st[0], 1e-5, 1.0 / 127.0, op0=ALU.max, op1=ALU.mult),
                reads=[b_st[0]], writes=[b_rsh])
            if tc + 2 < NTC:
                xnq_load(tc + 2)
            if tc >= 1:
                for g_ in range(4):
                    emit_trans_group(tc - 1, g_)
                emit_hq_dma(tc - 1)
            actscale(st[0], b_st[0], st, b_st, sxv, b_sxv)
            for sl in range(4):
                round5_2(h32[:, sl * 1024:(sl + 1) * 1024],
                         [b_hs[2 * sl], b_hs[2 * sl + 1]],
                         hq_neg[:, sl * 1024:(sl + 1) * 1024], b_hqneg[sl], sxv, b_sxv)
        for g in range(4):
            emit_trans_group(NTC - 1, g)
        emit_hq_dma(NTC - 1)

        b_wd2 = [[Buf() for _ in range(4)], [Buf() for _ in range(4)]]
        b_wo2 = [Buf(), Buf()]
        guard_wd = (b_xnqst[0].handover() + b_xnqst[1].handover() +
                    [d for b_ in b_hqneg for d in b_.handover()] +
                    [d for pr in b_wu8h for bb_ in pr for d in bb_.handover()])
        guard_wo = (b_arA.handover() +
                    [d for bb_ in b_wg8 for d in bb_.handover()])
        arA_guard = guard_wo
        h32_guard2 = [d for par in range(2) for bb_ in b_h32[par] for d in bb_.handover()]
        qq_guard = b_qq.handover()

        def emit_x_load(tc, bb, eng):
            p.add(eng, lambda e, tc=tc, bb=bb: e.dma_start(
                out=x4st[bb][:, :], in_=x_d[tc * 128:(tc + 1) * 128, :]),
                writes=[b_x4[bb]],
                extra=((h32_guard2 + b_xn32[bb].handover()) if tc < 2 else ()),
                dma=True)

        def emit_hqao_load(tc, bb):
            p.add("sync", lambda e, tc=tc, bb=bb: e.dma_start(
                out=hq_st[bb][:, :].rearrange("pp (ic t) -> pp ic t", t=128),
                in_=hqT_d.rearrange("(ic pp) t -> pp ic t", pp=128)[:, :, tc * 128:(tc + 1) * 128]),
                reads=[b_d["hqT"]], writes=[b_hqst[bb]],
                extra=(arA_guard if tc < 2 else ()), dma=True)
            p.add("sync", lambda e, tc=tc, bb=bb: e.dma_start(
                out=ao_st[bb][:, :].rearrange("pp (oc t) -> pp oc t", t=128),
                in_=aoT_d.rearrange("(oc pp) t -> pp oc t", pp=128)[:, :, tc * 128:(tc + 1) * 128]),
                reads=[b_d["aoT"]], writes=[b_aost[bb]],
                extra=(arA_guard if tc < 2 else ()), dma=True)

        for n_ in range(2):
            p.add("scalar", lambda e, n_=n_: e.dma_start(
                out=arA[:, 0:16384].rearrange("pp (oc n) -> pp oc n", n=D)[:, :, n_ * 512:(n_ + 1) * 512],
                in_=woT_d.rearrange("(oc pp) n -> pp oc n", pp=128)[:, :, n_ * 512:(n_ + 1) * 512]),
                reads=[b_d["woT"]], writes=[b_wo2[n_]],
                extra=guard_wo, dma=True)
        for n_ in range(2):
            for q_ in range(4):
                p.add("gpsimd" if n_ == 0 else "scalar",
                      lambda e, n_=n_, q_=q_: e.dma_start(
                    out=arB[:, 0:32768].rearrange("pp (ic n) -> pp ic n", n=D)[:, q_ * 8:(q_ + 1) * 8, n_ * 512:(n_ + 1) * 512],
                    in_=wdT_d.rearrange("(ic pp) n -> pp ic n", pp=128)[:, q_ * 8:(q_ + 1) * 8, n_ * 512:(n_ + 1) * 512]),
                    reads=[b_d["wdT"]], writes=[b_wd2[n_][q_]],
                    extra=guard_wd, dma=True)
        p.add("vector", lambda e: e.tensor_scalar(
            sdall[:, :], rsh[:, :], bvec[:, 2:3], None, op0=ALU.mult),
            reads=[b_rsh, b_bvec], writes=[b_sdall], extra=b_accA.handover())
        bkk = lambda i: bank(i) if i < 7 else psT0[:, 0:1024].bitcast(F32)
        b_bkl = lambda i: [b_bank[i]] if i < 7 else [b_psTh[0]]
        trB_guard2 = b_psTh[1].handover()
        dfin = None
        for tc in range(NTC):
            bb = tc % 2
            dbk = (0, 1) if bb == 0 else (4, 5)
            wbk = (2, 3) if bb == 0 else (6, 7)
            emit_hqao_load(tc, bb)
            emit_x_load(tc, bb, "sync" if tc < 2 else "scalar")
            for n_ in range(2):
                for oc in range(16):
                    p.add("tensor", lambda e, oc=oc, n_=n_, bb=bb, wbk=wbk: e.matmul(
                        bkk(wbk[n_]), ao_st[bb][:, oc * 128:(oc + 1) * 128],
                        woT_sl(oc, n_ * 512, 512),
                        start=(oc == 0), stop=(oc == 15)),
                        reads=[b_aost[bb], b_wo2[n_]], writes=b_bkl(wbk[n_]),
                        extra=(trB_guard2 if tc == 1 and n_ == 0 and oc == 0 else ()))
                p.add("vector", lambda e, n_=n_, bb=bb, wbk=wbk: e.scalar_tensor_tensor(
                    out32[bb][:, n_ * 512:(n_ + 1) * 512], bkk(wbk[n_]), 1.0,
                    x4st[bb][:, n_ * 512:(n_ + 1) * 512],
                    op0=ALU.mult, op1=ALU.add),
                    reads=b_bkl(wbk[n_]) + [b_x4[bb]], writes=[b_xst[bb]],
                    extra=(h32_guard2 if tc < 2 and n_ == 0 else ()))
            for n_ in range(2):
                for ic in range(32):
                    p.add("tensor", lambda e, ic=ic, n_=n_, bb=bb, dbk=dbk: e.matmul(
                        bank(dbk[n_]), hq_st[bb][:, ic * 128:(ic + 1) * 128],
                        wdqT_sl(ic, n_ * 512, 512),
                        start=(ic == 0), stop=(ic == 31)),
                        reads=[b_hqst[bb], b_wd2[n_][ic // 8]],
                        writes=[b_bank[dbk[n_]]])
            p.add("scalar", lambda e, tc=tc, bb=bb, dbk=dbk: e.activation(
                t1_32[:, :], psF[:, dbk[0] * 512:(dbk[0] + 2) * 512],
                AF.Identity, scale=sdall[:, tc:tc + 1]),
                reads=[b_bank[dbk[0]], b_bank[dbk[1]], b_sdall], writes=[b_t1],
                extra=((qq_guard + h32_guard2) if tc == 0 else ()))
            p.add("vector", lambda e, bb=bb: e.tensor_tensor(
                out32[bb][:, :], out32[bb][:, :], t1_32[:, :], op=ALU.add),
                reads=[b_xst[bb], b_t1], writes=[b_xst[bb]])
            dfin = p.add("scalar", lambda e, tc=tc, bb=bb: e.dma_start(
                out=y_d[tc * 128:(tc + 1) * 128, :], in_=out32[bb][:, :]),
                reads=[b_xst[bb]], dma=True)
        p.add_final_wait(dfin)
        if debug:
            p.add_final_wait(p.add("sync", lambda e: e.dma_start(
                out=sc_dbg, in_=smf[:, 0:128]),
                reads=[b_rsx, b_rsh, b_bvec], dma=True))

        blk = ctx.enter_context(nc.Block())
        p.emit(nc, blk, ctx)
    return nc


def kernel(L0_attn_x, L0_attn_Wq, L0_attn_Wkv_down, L0_attn_Wk_up,
           L0_attn_Wv_up, L0_attn_Wo, L0_ffn_w_gate, L0_ffn_w_up,
           L0_ffn_w_down):
    nc = build_nc()
    ident = np.eye(128, dtype=ml_dtypes.bfloat16)
    x = np.ascontiguousarray(np.asarray(L0_attn_x, dtype=np.float32))
    common = {
        "Wq": np.ascontiguousarray(np.asarray(L0_attn_Wq, dtype=np.float32)),
        "Wkv": np.ascontiguousarray(np.asarray(L0_attn_Wkv_down, dtype=np.float32)),
        "Wku": np.ascontiguousarray(np.asarray(L0_attn_Wk_up, dtype=np.float32)),
        "Wvu": np.ascontiguousarray(np.asarray(L0_attn_Wv_up, dtype=np.float32)),
        "Wo": np.ascontiguousarray(np.asarray(L0_attn_Wo, dtype=np.float32)),
        "Wg": np.ascontiguousarray(np.asarray(L0_ffn_w_gate, dtype=np.float32)),
        "Wu": np.ascontiguousarray(np.asarray(L0_ffn_w_up, dtype=np.float32)),
        "Wd": np.ascontiguousarray(np.asarray(L0_ffn_w_down, dtype=np.float32)),
        "ident": ident,
    }
    in_maps = [dict(common, x=x[c]) for c in range(8)]
    res = run_bass_kernel_spmd(nc, in_maps, core_ids=list(range(8)))
    return np.stack([np.asarray(r["y"], dtype=np.float32) for r in res.results],
                    axis=0)


# version 41
# speedup vs baseline: 1.3606x; 1.0104x over previous
import numpy as np
import ml_dtypes
import contextlib
import concourse.bass as bass
import concourse.mybir as mybir
from concourse.bass_utils import run_bass_kernel_spmd

F32 = mybir.dt.float32
BF16 = mybir.dt.bfloat16
I32 = mybir.dt.int32
AF = mybir.ActivationFunctionType
ALU = mybir.AluOpType
AX = mybir.AxisListType

T = 4096
D = 1024
NH = 8
DO = 2048
LR = 64
NI = 4096
NTC = T // 128
NDC = D // 128
CM = 12582912.0
ISCL = 0.0625

ENGINES = ("sync", "tensor", "vector", "scalar", "gpsimd")


class Buf:
    __slots__ = ("w", "r")

    def __init__(self):
        self.w = None
        self.r = []

    def handover(self):
        return ([self.w] if self.w is not None else []) + list(self.r)


class Prog:
    def __init__(self):
        self.ops = []
        self.final_waits = []

    def add(self, eng, fn, reads=(), writes=(), extra=(), dma=False):
        deps = []
        for b in writes:
            if b.w is not None:
                deps.append(b.w)
            deps.extend(b.r)
        for b in reads:
            if b.w is not None:
                deps.append(b.w)
        deps.extend(d for d in extra if d is not None)
        i = len(self.ops)
        assert all(d < i for d in deps)
        self.ops.append({"eng": eng, "fn": fn, "deps": sorted(set(deps)),
                         "dma": dma})
        for b in writes:
            b.w = i
            b.r = []
        for b in reads:
            b.r.append(i)
        return i

    def add_final_wait(self, dep):
        self.final_waits.append(dep)

    NSLOT = 12

    def emit(self, nc, block, ctx):
        ops = self.ops
        NS = self.NSLOT
        ekey = [None] * len(ops)
        qidx = {}
        for i, op in enumerate(ops):
            if op["dma"]:
                q = "q_" + op["eng"]
                j = qidx.get(q, 0)
                qidx[q] = j + 1
                ekey[i] = (q, j % NS)
            else:
                ekey[i] = op["eng"]
        for i, op in enumerate(ops):
            op["deps"] = [d for d in op["deps"]
                          if not (ekey[d] == "tensor" and ekey[i] == "tensor")]
        needed = set(i for i, op in enumerate(ops) if op["dma"])
        for op in ops:
            needed.update(op["deps"])
        needed.update(self.final_waits)
        sems = {}
        for e in ENGINES:
            sems[e] = ctx.enter_context(nc.semaphore("s_" + e))
            for s in range(NS):
                k = ("q_" + e, s)
                sems[k] = ctx.enter_context(nc.semaphore(f"sq_{e}_{s}"))
        val = [0] * len(ops)
        cnt = {k: 0 for k in sems}
        for i, op in enumerate(ops):
            if i in needed:
                cnt[ekey[i]] += 16 if op["dma"] else 1
                val[i] = cnt[ekey[i]]
        per_eng = {e: [] for e in ENGINES}
        for i, op in enumerate(ops):
            per_eng[op["eng"]].append(i)

        last_val = dict(cnt)

        def emit_engine(e, eng):
            maxw = {}

            def w(k, v):
                if v > 0 and maxw.get(k, 0) < v:
                    eng.wait_ge(sems[k], v)
                    maxw[k] = v

            for i in per_eng[e]:
                op = ops[i]
                if op["dma"]:
                    w(ekey[i], val[i] - 16)
                for d in op["deps"]:
                    assert val[d] > 0, (i, d)
                    w(ekey[d], val[d])
                inst = op["fn"](eng)
                if i in needed:
                    inst.then_inc(sems[ekey[i]], 16 if op["dma"] else 1)
            for d in self.final_waits:
                w(ekey[d], val[d])
            if e == "sync":
                for k, v in last_val.items():
                    if isinstance(k, tuple):
                        w(k, v)

        @block.sync
        def _(eng):
            emit_engine("sync", eng)

        @block.tensor
        def _(eng):
            emit_engine("tensor", eng)

        @block.vector
        def _(eng):
            emit_engine("vector", eng)

        @block.scalar
        def _(eng):
            emit_engine("scalar", eng)

        @block.gpsimd
        def _(eng):
            emit_engine("gpsimd", eng)


def build_nc(debug=False):
    nc = bass.Bass("TRN2", target_bir_lowering=False,
                   detect_race_conditions=False)

    x_d = nc.dram_tensor("x", [T, D], F32, kind="ExternalInput").ap()
    wq_d = nc.dram_tensor("Wq", [DO, D], F32, kind="ExternalInput").ap()
    wkv_d = nc.dram_tensor("Wkv", [LR, D], F32, kind="ExternalInput").ap()
    wku_d = nc.dram_tensor("Wku", [DO, LR], F32, kind="ExternalInput").ap()
    wvu_d = nc.dram_tensor("Wvu", [DO, LR], F32, kind="ExternalInput").ap()
    wo_d = nc.dram_tensor("Wo", [D, DO], F32, kind="ExternalInput").ap()
    wg_d = nc.dram_tensor("Wg", [NI, D], F32, kind="ExternalInput").ap()
    wu_d = nc.dram_tensor("Wu", [NI, D], F32, kind="ExternalInput").ap()
    wd_d = nc.dram_tensor("Wd", [D, NI], F32, kind="ExternalInput").ap()
    id_d = nc.dram_tensor("ident", [128, 128], BF16, kind="ExternalInput").ap()
    y_d = nc.dram_tensor("y", [T, D], F32, kind="ExternalOutput").ap()

    sk = "ExternalOutput" if debug else "Internal"
    xnqT_d = nc.dram_tensor("xnqT_d", [D, T], BF16, kind=sk).ap()
    aoT_d = nc.dram_tensor("aoT_d", [DO, T], BF16, kind=sk).ap()
    hqT_d = nc.dram_tensor("hqT_d", [NI, T], BF16, kind=sk).ap()
    wqT_d = nc.dram_tensor("wqT_d", [D, DO], BF16, kind=sk).ap()
    woT_d = nc.dram_tensor("woT_d", [DO, D], BF16, kind=sk).ap()
    wgT_d = nc.dram_tensor("wgT_d", [D, NI], BF16, kind=sk).ap()
    wuT_d = nc.dram_tensor("wuT_d", [D, NI], BF16, kind=sk).ap()
    wdT_d = nc.dram_tensor("wdT_d", [NI, D], BF16, kind=sk).ap()
    if debug:
        xnT_dbg = nc.dram_tensor("xnT_dbg", [D, T], BF16, kind="ExternalOutput").ap()
        kvT_dbg = nc.dram_tensor("kvT_dbg", [LR, T], BF16, kind="ExternalOutput").ap()
        sc_dbg = nc.dram_tensor("sc_dbg", [128, 128], F32, kind="ExternalOutput").ap()

    p = Prog()
    with contextlib.ExitStack() as ctx:
        sbuf = lambda n, s, dt: ctx.enter_context(nc.sbuf_tensor(n, s, dt))
        ident = sbuf("identsb", [128, 128], BF16)
        arA = sbuf("arA", [128, 32768], BF16)
        arB = sbuf("arB", [128, 38912], BF16)
        arC = sbuf("arC", [128, 10240], F32)
        rsc = sbuf("rsc", [128, 3072], BF16)
        t32 = sbuf("t32", [128, 1024], I32)
        smf = sbuf("smf", [128, 272], F32)
        ones_t = sbuf("ones_t", [128, 128], F32)
        wkuT = sbuf("wkuT", [LR, DO], BF16)
        wvuT = sbuf("wvuT", [LR, DO], BF16)
        p1b = sbuf("p1b", [128, 4096], BF16)
        psF = ctx.enter_context(nc.psum_tensor("psF", [128, 3584], F32))
        psT0 = ctx.enter_context(nc.psum_tensor("psT0", [128, 1024], BF16))
        bank = lambda i: psF[:, i * 512:(i + 1) * 512]
        trB = psF[:, 3072:3584].bitcast(BF16)
        BKap = lambda t: psT0[:, 0:1024] if t == 0 else trB
        _tb = [0]

        def nb():
            t = _tb[0]
            _tb[0] ^= 1
            return t
        pv_ps = lambda sub, a, b: psF[:, 1536 + sub * 512 + a: 1536 + sub * 512 + b]

        b_ident, b_ones, b_eps = Buf(), Buf(), Buf()
        b_arA = Buf()
        b_qT, b_kT, b_vA = Buf(), Buf(), Buf()
        b_wqst = [Buf(), Buf()]
        b_pT = [Buf(), Buf(), Buf()]
        b_atto = Buf()
        b_aoTh = [Buf(), Buf()]
        b_wuqT = Buf()
        b_wg8 = [Buf() for _ in range(8)]
        b_xnqst = [Buf(), Buf()]
        b_hqneg = [Buf() for _ in range(4)]
        b_woT = Buf()
        b_hqst = [Buf(), Buf()]
        b_aost = [Buf(), Buf()]
        b_xst = [Buf(), Buf()]
        b_xn32 = [Buf(), Buf()]
        b_h32 = [[Buf() for _ in range(8)] for _ in range(2)]
        b_nn, b_ee, b_qq = Buf(), Buf(), Buf()
        b_sil = [Buf(), Buf(), Buf(), Buf()]
        b_t1 = Buf()
        b_x4 = [Buf(), Buf()]
        b_t32 = Buf()
        b_stp = [[Buf() for _ in range(4)] for _ in range(2)]
        b_wstA = [Buf(), Buf(), Buf()]
        b_sxv2 = [Buf(), Buf()]
        b_sgv, b_suv, b_sdv, b_mu = Buf(), Buf(), Buf(), Buf()
        b_scw3 = [Buf(), Buf(), Buf()]
        b_rsx, b_rsh, b_bvec, b_repl, b_acc = Buf(), Buf(), Buf(), Buf(), Buf()
        b_kvT, b_wkvT, b_wkuT, b_wvuT = Buf(), Buf(), Buf(), Buf()
        b_p1r = [Buf(), Buf(), Buf(), Buf()]
        b_bank = [Buf() for _ in range(7)]
        b_psTh = [Buf(), Buf()]
        b_d = {n: Buf() for n in
               ("xnqT", "aoT", "hqT", "wqT", "woT", "wgT", "wuT", "wdT")}

        rsx = smf[:, 0:32]
        rsh = smf[:, 32:64]
        bvec = smf[:, 64:68]
        repl = smf[:, 68:72]
        stp = [[smf[:, o + i: o + i + 1] for i in range(4)] for o in (72, 76)]
        epsv = smf[:, 80:81]
        sxv2 = [smf[:, 81 + i: 82 + i] for i in range(2)]
        sgv = smf[:, 83:84]
        suv = smf[:, 84:85]
        sdv = smf[:, 85:86]
        scw3 = [smf[:, 86 + i: 87 + i] for i in range(3)]
        mu11 = smf[:, 89:90]
        acc32 = smf[:, 96:112]
        sgall = smf[:, 208:240]
        suall = smf[:, 240:272]
        sdall = acc32
        sdall = smf[:, 112:144]
        b_sdall = Buf()
        b_sgall, b_suall = Buf(), Buf()
        accA_sl = lambda s: smf[:, 112 + s: 113 + s]
        b_accA = Buf()
        wstA = [arC[:, 4096 + b * 1024: 4096 + (b + 1) * 1024] for b in (0, 1, 2)]

        xnT_sl = lambda dc, t0, n: arA[:, dc * T + t0: dc * T + t0 + n]
        wgqT_sl = lambda dc, i0, n: arA[:, dc * NI + i0: dc * NI + i0 + n]
        woT_sl = lambda oc, n0, n: arA[:, oc * D + n0: oc * D + n0 + n]
        hq_st = [arA[:, 16384 + b * 4096: 16384 + (b + 1) * 4096] for b in (0, 1)]
        ao_st = [arA[:, 24576 + b * 2048: 24576 + (b + 1) * 2048] for b in (0, 1)]

        qT = arB[:, 0:8192]
        kT = arB[:, 8192:16384]
        vA = arB[:, 16384:24608]
        wqst = [arB[:, 24608 + b * 2048: 24608 + (b + 1) * 2048] for b in (0, 1)]
        pT = [arB[:, 28704 + b * 512: 28704 + (b + 1) * 512] for b in range(3)]
        att_o = arB[:, 30240:31264]
        wkvT = arB[:, 30240:30752]
        aoTh = [arB[:, 31264 + b * 1024: 31264 + (b + 1) * 1024] for b in (0, 1)]
        kvT = arB[0:LR, 33312:37408]
        wuqT_sl = lambda dc, i0, n: arB[:, dc * NI + i0: dc * NI + i0 + n]
        wdqT_sl = lambda ic, n0, n: arB[:, ic * D + n0: ic * D + n0 + n]
        xnq_st = [arB[:, 32768 + b * 1024: 32768 + (b + 1) * 1024] for b in (0, 1)]
        hq_neg = arB[:, 34816:38912]

        xst = [arC[:, b * 1024:(b + 1) * 1024] for b in (0, 1)]
        xn32p = [arC[:, 2048 + b * 1024: 2048 + (b + 1) * 1024] for b in (0, 1)]
        h32p = [arC[:, b * 4096:(b + 1) * 4096] for b in (0, 1)]
        x4st = [arC[:, 2048 + b * 1024: 2048 + (b + 1) * 1024] for b in (0, 1)]
        out32 = [arC[:, b * 1024:(b + 1) * 1024] for b in (0, 1)]
        nn32 = rsc[:, 0:1024]
        ee32 = rsc[:, 1024:2048]
        qq32 = rsc[:, 2048:3072]
        t1_32 = arC[:, 4096:5120]
        sil32 = [arC[:, 8192 + b * 512: 8192 + (b + 1) * 512] for b in range(4)]

        p.add("sync", lambda e: e.dma_start(out=ident[:], in_=id_d),
              writes=[b_ident], dma=True)
        p.add("vector", lambda e: e.memset(epsv, 1e-5), writes=[b_eps])
        p.add("vector", lambda e: e.memset(ones_t[:, :], 1.0), writes=[b_ones])
        p.add("gpsimd", lambda e: e.dma_start(out=p1b[0:LR, 0:D], in_=wkv_d),
              writes=[b_p1r[0], b_p1r[1], b_p1r[2], b_p1r[3]], dma=True)
        for dc in range(NDC):
            p.add("tensor", lambda e, dc=dc: e.transpose(
                psT0[:, dc * 128: dc * 128 + LR],
                p1b[0:LR, dc * 128:(dc + 1) * 128], ident[0:LR, 0:LR]),
                reads=[b_p1r[dc // 2], b_ident], writes=[b_psTh[0]])
        for dc in range(NDC):
            p.add("vector", lambda e, dc=dc: e.tensor_copy(
                wkvT[:, dc * LR:(dc + 1) * LR], psT0[:, dc * 128: dc * 128 + LR]),
                reads=[b_psTh[0]], writes=[b_wkvT])

        lodB = [arB[:, 8192 + b * 2048: 8192 + (b + 1) * 2048] for b in (0, 1, 2)]
        stgB = [arB[:, 16384 + b * 1024: 16384 + (b + 1) * 1024] for b in range(4)]
        _sg = [0]
        b_ldB = [Buf(), Buf(), Buf()]
        b_stB = [Buf(), Buf(), Buf(), Buf()]
        prep_tiles = []
        for r in range(DO // 128):
            prep_tiles.append(("up", wku_d, wkuT, b_wkuT, r, 0))
        for r in range(DO // 128):
            prep_tiles.append(("up", wvu_d, wvuT, b_wvuT, r, 0))
        for r in range(DO // 128):
            prep_tiles.append(("pl", wq_d, wqT_d, b_d["wqT"], r, 1024))
        for r in range(D // 128):
            prep_tiles.append(("pl", wo_d, woT_d, b_d["woT"], r, 2048))
        _pc = [0]
        _pd = [0]

        def emit_prep_dma():
            if _pd[0] >= len(prep_tiles):
                return
            i = _pd[0]
            _pd[0] += 1
            kind, srct, dst, b_dst, r, cols = prep_tiles[i]
            par = i % 3
            n = LR if kind == "up" else cols
            p.add("gpsimd", lambda e, r=r, srct=srct, par=par, n=n: e.dma_start(
                out=lodB[par][:, 0:n], in_=srct[r * 128:(r + 1) * 128, :]),
                writes=[b_ldB[par]], dma=True)

        def emit_prep_comp():
            if _pc[0] >= _pd[0] or _pc[0] >= len(prep_tiles):
                return
            i = _pc[0]
            _pc[0] += 1
            kind, srct, dst, b_dst, r, cols = prep_tiles[i]
            par = i % 3
            if kind == "up":
                tb = nb()
                p.add("tensor", lambda e, par=par, tb=tb: e.transpose(
                    BKap(tb)[0:LR, 0:128], lodB[par][:, 0:LR],
                    ident[:]),
                    reads=[b_ldB[par], b_ident], writes=[b_psTh[tb]])
                p.add("vector", lambda e, r=r, dst=dst, tb=tb: e.tensor_copy(
                    dst[:, r * 128:(r + 1) * 128],
                    BKap(tb)[0:LR, 0:128]),
                    reads=[b_psTh[tb]], writes=[b_dst])
                return
            for g in range(cols // 1024):
                tb = nb()
                for dc in range(8):
                    cc = g * 8 + dc
                    p.add("tensor", lambda e, cc=cc, dc=dc, par=par, tb=tb: e.transpose(
                        BKap(tb)[:, dc * 128:(dc + 1) * 128],
                        lodB[par][:, cc * 128:(cc + 1) * 128], ident[:]),
                        reads=[b_ldB[par], b_ident], writes=[b_psTh[tb]])
                sg = _sg[0] % 4
                _sg[0] += 1
                p.add("vector", lambda e, sg=sg, tb=tb: e.tensor_copy(
                    stgB[sg][:, :], BKap(tb)[:, :]),
                    reads=[b_psTh[tb]], writes=[b_stB[sg]])
                p.add("sync", lambda e, r=r, g=g, dst=dst, sg=sg: e.dma_start(
                    out=dst.rearrange("(cc pp) o -> pp cc o", pp=128)[:, g * 8:(g + 1) * 8, r * 128:(r + 1) * 128],
                    in_=stgB[sg][:, :].rearrange("pp (cc t) -> pp cc t", t=128)),
                    reads=[b_stB[sg]], writes=[b_dst], dma=True)

        def emit_wqst(h):
            wb = h % 2
            p.add("scalar" if h > 0 else "sync", lambda e, h=h, wb=wb: e.dma_start(
                out=wqst[wb][:, :].rearrange("pp (dc o) -> pp dc o", o=256),
                in_=wqT_d.rearrange("(dc pp) o -> pp dc o", pp=128)[:, :, h * 256:(h + 1) * 256]),
                reads=[b_d["wqT"]], writes=[b_wqst[wb]], dma=True)

        def emit_qgrp(h, oc, ts_):
            wb = h % 2
            i6 = oc * 8 + ts_
            g = (1 + i6 % 5) if h == 0 else i6 % 6
            vec = (i6 % 2 == 0) if h == 0 else True
            for dc in range(NDC):
                p.add("tensor", lambda e, oc=oc, ts_=ts_, dc=dc, g=g, wb=wb: e.matmul(
                    bank(g),
                    wqst[wb][:, dc * 256 + oc * 128: dc * 256 + (oc + 1) * 128],
                    xnT_sl(dc, ts_ * 512, 512),
                    start=(dc == 0), stop=(dc == NDC - 1)),
                    reads=[b_wqst[wb], b_arA], writes=[b_bank[g]])
            p.add("vector" if vec else "scalar",
                  (lambda e, oc=oc, ts_=ts_, g=g: e.tensor_copy(
                      qT[:, oc * T + ts_ * 512: oc * T + (ts_ + 1) * 512], bank(g)))
                  if vec else
                  (lambda e, oc=oc, ts_=ts_, g=g: e.activation(
                      qT[:, oc * T + ts_ * 512: oc * T + (ts_ + 1) * 512], bank(g),
                      AF.Identity, bias=0.0, scale=1.0)),
                  reads=[b_bank[g]], writes=[b_qT])

        def emit_kgrp(h, oc, ts_, first=False):
            i6 = oc * 8 + ts_
            g = (1 + i6 % 5) if h == 0 else i6 % 6
            vec = (i6 % 2 == 0)
            p.add("tensor", lambda e, oc=oc, ts_=ts_, g=g, h=h: e.matmul(
                bank(g), wkuT[:, h * 256 + oc * 128: h * 256 + (oc + 1) * 128],
                kvT[:, ts_ * 512:(ts_ + 1) * 512], start=True, stop=True),
                reads=[b_wkuT, b_kvT], writes=[b_bank[g]])
            p.add("vector" if vec else "scalar",
                  (lambda e, oc=oc, ts_=ts_, g=g: e.tensor_copy(
                      kT[:, oc * T + ts_ * 512: oc * T + (ts_ + 1) * 512], bank(g)))
                  if vec else
                  (lambda e, oc=oc, ts_=ts_, g=g: e.activation(
                      kT[:, oc * T + ts_ * 512: oc * T + (ts_ + 1) * 512], bank(g),
                      AF.Identity, bias=0.0, scale=1.0)),
                  reads=[b_bank[g]], writes=[b_kT],
                  extra=((b_ldB[0].handover() + b_ldB[1].handover() +
                          b_ldB[2].handover())
                         if (first or (h == 0 and oc == 1 and ts_ <= 3))
                         else ()))

        def emit_vmemset(h):
            p.add("vector", lambda e: e.memset(vA[:, :], 1.0), writes=[b_vA],
                  extra=((b_stB[0].handover() + b_stB[1].handover() +
                          b_stB[2].handover() + b_stB[3].handover())
                         if h == 0 else ()))

        def emit_vchunk(h, tk):
            g = (1 + tk % 5) if h == 0 else tk % 6
            vec = (tk % 2 == 0)
            p.add("tensor", lambda e, tk=tk, g=g, h=h: e.matmul(
                bank(g)[:, 0:256], kvT[:, tk * 128:(tk + 1) * 128],
                wvuT[:, h * 256:(h + 1) * 256], start=True, stop=True),
                reads=[b_kvT, b_wvuT], writes=[b_bank[g]])
            p.add("vector" if vec else "scalar",
                  (lambda e, tk=tk, g=g: e.tensor_copy(
                      vA[:, tk * 257: tk * 257 + 256], bank(g)[:, 0:256]))
                  if vec else
                  (lambda e, tk=tk, g=g: e.activation(
                      vA[:, tk * 257: tk * 257 + 256], bank(g)[:, 0:256],
                      AF.Identity, bias=0.0, scale=1.0)),
                  reads=[b_bank[g]], writes=[b_vA])

        def round5(src_ap, b_src, dst_ap, b_dst, sx, b_sx):
            p.add("vector", lambda e: e.tensor_scalar(
                t32[:, :].bitcast(F32), src_ap, sx, CM, op0=ALU.mult, op1=ALU.add),
                reads=[b_src, b_sx], writes=[b_t32])
            p.add("vector", lambda e: e.tensor_scalar(
                nn32[:, :], t32[:, :].bitcast(F32), -CM, None, op0=ALU.add),
                reads=[b_t32], writes=[b_nn])
            p.add("vector", lambda e: e.scalar_tensor_tensor(
                ee32[:, :], src_ap, sx, nn32[:, :], op0=ALU.mult, op1=ALU.subtract),
                reads=[b_src, b_sx, b_nn], writes=[b_ee])
            p.add("vector", lambda e: e.scalar_tensor_tensor(
                qq32[:, :], ee32[:, :], 0.5, nn32[:, :], op0=ALU.is_gt, op1=ALU.add),
                reads=[b_ee, b_nn], writes=[b_qq])
            p.add("vector", lambda e: e.scalar_tensor_tensor(
                dst_ap, ee32[:, :], -0.5, qq32[:, :], op0=ALU.is_lt, op1=ALU.subtract),
                reads=[b_ee, b_qq], writes=[b_dst])

        def round5_2(src_ap, b_srcs, dst_ap, b_dst, sx, b_sx, eng12="vector"):
            p.add(eng12, lambda e: e.tensor_scalar(
                t32[:, :].bitcast(F32), src_ap, sx, CM, op0=ALU.mult, op1=ALU.add),
                reads=b_srcs + [b_sx], writes=[b_t32])
            p.add(eng12, lambda e: e.tensor_scalar(
                nn32[:, :], t32[:, :].bitcast(F32), -CM, None, op0=ALU.add),
                reads=[b_t32], writes=[b_nn])
            p.add("vector", lambda e: e.scalar_tensor_tensor(
                ee32[:, :], src_ap, sx, nn32[:, :], op0=ALU.mult, op1=ALU.subtract),
                reads=b_srcs + [b_sx, b_nn], writes=[b_ee])
            p.add("vector", lambda e: e.scalar_tensor_tensor(
                qq32[:, :], ee32[:, :], 0.5, nn32[:, :], op0=ALU.is_gt, op1=ALU.add),
                reads=[b_ee, b_nn], writes=[b_qq])
            p.add("vector", lambda e: e.scalar_tensor_tensor(
                dst_ap, ee32[:, :], -0.5, qq32[:, :], op0=ALU.is_lt, op1=ALU.subtract),
                reads=[b_ee, b_qq], writes=[b_dst])

        def actscale(src_small, b_src, st, b_st, sx, b_sx):
            p.add("vector", lambda e: e.tensor_scalar(
                st[1], src_small, 1e-5, None, op0=ALU.max),
                reads=[b_src], writes=[b_st[1]])
            p.add("vector", lambda e: e.reciprocal(st[1], st[1]),
                  reads=[b_st[1]], writes=[b_st[1]])
            p.add("vector", lambda e: e.tensor_scalar(
                sx, st[1], 127.0, None, op0=ALU.mult),
                reads=[b_st[1]], writes=[b_sx])

        passA = []
        for wsrc, rows, cols, base in ((wg_d, NI, D, 0), (wu_d, NI, D, 32),
                                       (wd_d, D, NI, 64)):
            ncc = cols // 1024
            for i in range(rows // 128 * ncc):
                r, cb = divmod(i, ncc)
                passA.append((wsrc, r, cb, base + i))

        _pa_d = [0]
        _pa_c = [0]

        def emit_passA_dma():
            if _pa_d[0] >= len(passA):
                return
            j = _pa_d[0]
            _pa_d[0] += 1
            wsrc, r, cb, slot = passA[j]
            ab = j % 3
            p.add("scalar", lambda e, wsrc=wsrc, r=r, cb=cb, ab=ab: e.dma_start(
                out=wstA[ab][:, :], in_=wsrc[r * 128:(r + 1) * 128, cb * 1024:(cb + 1) * 1024]),
                writes=[b_wstA[ab]], dma=True)

        def emit_passA_abs():
            if _pa_c[0] >= _pa_d[0] or _pa_c[0] >= len(passA):
                return
            j = _pa_c[0]
            _pa_c[0] += 1
            wsrc, r, cb, slot = passA[j]
            ab = j % 3
            p.add("scalar", lambda e, slot=slot, ab=ab: e.activation(
                wstA[ab][:, :], wstA[ab][:, :], AF.Abs, accum_out=accA_sl(slot)),
                reads=[b_wstA[ab]], writes=[b_wstA[ab], b_accA])

        def emit_xnq_out(tc2):
            bq = tc2 % 2
            rq2 = b_p1r[bq]
            rs2 = b_p1r[2 + bq]
            tb = nb()
            for dc in range(NDC):
                p.add("tensor", lambda e, dc=dc, bq=bq, tb=tb: e.transpose(
                    BKap(tb)[:, dc * 128:(dc + 1) * 128],
                    p1b[:, bq * 1024 + dc * 128: bq * 1024 + (dc + 1) * 128],
                    ident[:]),
                    reads=[rq2, b_ident], writes=[b_psTh[tb]])
            p.add("scalar", lambda e, bq=bq, tb=tb: e.activation(
                p1b[:, (2 + bq) * 1024:(3 + bq) * 1024], BKap(tb)[:, :],
                AF.Identity, bias=0.0, scale=1.0),
                reads=[b_psTh[tb]], writes=[rs2])
            p.add("sync", lambda e, tc2=tc2, bq=bq: e.dma_start(
                out=xnqT_d.rearrange("(dc pp) t -> pp dc t", pp=128)[:, :, tc2 * 128:(tc2 + 1) * 128],
                in_=p1b[:, (2 + bq) * 1024:(3 + bq) * 1024].rearrange("pp (dc t) -> pp dc t", t=128)),
                reads=[rs2], writes=[b_d["xnqT"]], dma=True)

        def emit_kv(tc2):
            for dc in range(NDC):
                p.add("tensor", lambda e, dc=dc, tc2=tc2: e.matmul(
                    bank(0)[0:LR, 0:128], wkvT[:, dc * LR:(dc + 1) * LR],
                    xnT_sl(dc, tc2 * 128, 128),
                    start=(dc == 0), stop=(dc == NDC - 1)),
                    reads=[b_wkvT, b_arA], writes=[b_bank[0]])
            p.add("scalar", lambda e, tc2=tc2: e.activation(
                kvT[:, tc2 * 128:(tc2 + 1) * 128], bank(0)[0:LR, 0:128],
                AF.Identity, bias=0.0, scale=1.0),
                reads=[b_bank[0]], writes=[b_kvT])

        pre_q = [(oc_, ts__) for ts__ in range(8) for oc_ in range(2)]
        pre_k = [(oc_, ts__) for ts__ in range(8) for oc_ in range(2)]
        pre_v = list(range(NTC))
        for _ in range(3):
            emit_passA_dma()
        for _ in range(3):
            emit_prep_dma()
        for tc in range(NTC):
            bb = tc % 2
            xn32 = xn32p[bb]
            b_xn = b_xn32[bb]
            st, b_st = stp[bb], b_stp[bb]
            sxv, b_sxv = sxv2[bb], b_sxv2[bb]
            rq = b_p1r[bb]
            rs = b_p1r[2 + bb]
            if tc == 20:
                emit_wqst(0)
                emit_vmemset(0)
            if tc >= 21:
                for _ in range(2):
                    if pre_q and 4 * pre_q[0][1] + 3 < tc:
                        oc_, ts__ = pre_q.pop(0)
                        emit_qgrp(0, oc_, ts__)
                for _ in range(2):
                    if pre_k and 4 * pre_k[0][1] + 4 < tc:
                        oc_, ts__ = pre_k.pop(0)
                        emit_kgrp(0, oc_, ts__, first=(oc_ == 0 and ts__ == 0))
                for _ in range(4):
                    if pre_v and pre_v[0] + 1 < tc:
                        emit_vchunk(0, pre_v.pop(0))
            p.add("sync", lambda e, tc=tc, bb=bb: e.dma_start(
                out=xst[bb][:, :], in_=x_d[tc * 128:(tc + 1) * 128, :]),
                writes=[b_xst[bb]], dma=True)
            p.add("vector", lambda e, bb=bb, st=st: e.reduce_sum(
                st[0], xst[bb][:, :], axis=AX.X),
                reads=[b_xst[bb]], writes=[b_st[0]])
            p.add("scalar", lambda e, bb=bb, xn32=xn32, st=st: e.activation(
                xn32[:, :], xst[bb][:, :], AF.Square, accum_out=st[1]),
                reads=[b_xst[bb]], writes=[b_xn, b_st[1]])
            p.add("vector", lambda e, st=st: e.tensor_scalar(
                st[0], st[0], 1.0 / D, None, op0=ALU.mult),
                reads=[b_st[0]], writes=[b_st[0]])
            p.add("vector", lambda e, st=st: e.tensor_scalar(
                st[1], st[1], 1.0 / D, None, op0=ALU.mult),
                reads=[b_st[1]], writes=[b_st[1]])
            p.add("vector", lambda e, st=st: e.scalar_tensor_tensor(
                st[2], st[0], st[0], st[1], op0=ALU.mult, op1=ALU.subtract),
                reads=[b_st[0], b_st[1]], writes=[b_st[2]])
            p.add("scalar", lambda e, st=st: e.activation(
                st[3], st[2], AF.Sqrt, bias=epsv, scale=-1.0),
                reads=[b_st[2], b_eps], writes=[b_st[3]])
            p.add("vector", lambda e, st=st: e.reciprocal(st[3], st[3]),
                  reads=[b_st[3]], writes=[b_st[3]])
            p.add("vector", lambda e, st=st: e.tensor_scalar(
                st[2], st[3], st[0], -1.0, op0=ALU.mult, op1=ALU.mult),
                reads=[b_st[3], b_st[0]], writes=[b_st[2]])
            p.add("scalar", lambda e, bb=bb, st=st: e.activation(
                p1b[:, (2 + bb) * 1024:(3 + bb) * 1024], xst[bb][:, :],
                AF.Identity, bias=st[2], scale=st[3]),
                reads=[b_xst[bb], b_st[2], b_st[3]], writes=[rs])
            p.add("scalar", lambda e, bb=bb, xn32=xn32, st=st: e.activation(
                xn32[:, :], xst[bb][:, :], AF.Identity, bias=st[2], scale=st[3]),
                reads=[b_xst[bb], b_st[2], b_st[3]], writes=[b_xn])
            for _ in range(3):
                emit_passA_abs()
            for _ in range(3):
                emit_prep_comp()
            if tc >= 1:
                emit_kv(tc - 1)
            tb = nb()
            for dc in range(NDC):
                p.add("tensor", lambda e, dc=dc, bb=bb, tb=tb: e.transpose(
                    BKap(tb)[:, dc * 128:(dc + 1) * 128],
                    p1b[:, (2 + bb) * 1024 + dc * 128: (2 + bb) * 1024 + (dc + 1) * 128],
                    ident[:]),
                    reads=[rs, b_ident], writes=[b_psTh[tb]])
            p.add("vector", lambda e, tc=tc, tb=tb: e.tensor_copy(
                arA[:, :].rearrange("pp (dc t) -> pp dc t", t=T)[:, :, tc * 128:(tc + 1) * 128],
                BKap(tb)[:, :].rearrange("pp (dc t) -> pp dc t", t=128)),
                reads=[b_psTh[tb]], writes=[b_arA])
            if tc >= 1:
                emit_xnq_out(tc - 1)
            p.add("vector", lambda e, xn32=xn32, st=st: e.tensor_reduce(
                st[0], xn32[:, :], axis=AX.X, op=ALU.max, apply_absolute_value=True),
                reads=[b_xn], writes=[b_st[0]])
            p.add("vector", lambda e, tc=tc, st=st: e.tensor_scalar(
                rsx[:, tc:tc + 1], st[0], 1e-5, 1.0 / 127.0, op0=ALU.max, op1=ALU.mult),
                reads=[b_st[0]], writes=[b_rsx])
            actscale(st[0], b_st[0], st, b_st, sxv, b_sxv)
            round5(xn32[:, :], b_xn, p1b[:, bb * 1024:(bb + 1) * 1024], rq, sxv, b_sxv)
            for _ in range(3):
                emit_passA_dma()
            for _ in range(3):
                emit_prep_dma()
        emit_xnq_out(NTC - 1)
        emit_kv(NTC - 1)
        while _pa_c[0] < len(passA):
            emit_passA_abs()
            emit_passA_dma()
        while _pc[0] < len(prep_tiles):
            emit_prep_comp()
            emit_prep_dma()
        for oc_, ts__ in pre_q:
            emit_qgrp(0, oc_, ts__)
        for oc_, ts__ in pre_k:
            emit_kgrp(0, oc_, ts__, first=(oc_ == 0 and ts__ == 0))
        for tk_ in pre_v:
            emit_vchunk(0, tk_)
        if debug:
            p.add_final_wait(p.add("sync", lambda e: e.dma_start(
                out=xnT_dbg.rearrange("(dc pp) t -> pp dc t", pp=128),
                in_=arA[:, :].rearrange("pp (dc t) -> pp dc t", t=T)),
                reads=[b_arA], dma=True))
            p.add_final_wait(p.add("sync", lambda e: e.dma_start(
                out=kvT_dbg, in_=kvT[:, :]), reads=[b_kvT], dma=True))

        for bslot, (rows, cols, base) in enumerate(((NI, D, 0), (NI, D, 32), (D, NI, 64))):
            st = stp[0]
            p.add("vector", lambda e, base=base, st=st: e.reduce_sum(
                st[0], smf[:, 112 + base: 112 + base + 32], axis=AX.X),
                reads=[b_accA], writes=[b_stp[0][0]])
            p.add("tensor", lambda e, st=st: e.matmul(
                bank(5)[0:1, 0:1], ones_t[:, 0:1], st[0], start=True, stop=True),
                reads=[b_ones, b_stp[0][0]], writes=[b_bank[5]])
            p.add("vector", lambda e, rows=rows, cols=cols: e.tensor_scalar(
                mu11[0:1, :], bank(5)[0:1, 0:1], 1.0 / (rows * cols), 1e-5,
                op0=ALU.mult, op1=ALU.max),
                reads=[b_bank[5]], writes=[b_mu])
            p.add("tensor", lambda e: e.matmul(
                bank(5)[:, 0:1], ones_t[0:1, :], mu11[0:1, :], start=True, stop=True),
                reads=[b_ones, b_mu], writes=[b_bank[5]])
            p.add("vector", lambda e, bslot=bslot: e.tensor_copy(
                bvec[:, bslot:bslot + 1], bank(5)[:, 0:1]),
                reads=[b_bank[5]], writes=[b_bvec])
            p.add("vector", lambda e, bslot=bslot: e.reciprocal(
                scw3[bslot], bvec[:, bslot:bslot + 1]),
                reads=[b_bvec], writes=[b_scw3[bslot]])

        passB = []
        for bslot, (wsrc, rows, cols, dst, bd) in enumerate((
                (wg_d, NI, D, wgT_d, b_d["wgT"]), (wu_d, NI, D, wuT_d, b_d["wuT"]),
                (wd_d, D, NI, wdT_d, b_d["wdT"]))):
            ncc = cols // 1024
            for i in range(rows // 128 * ncc):
                r, cb = divmod(i, ncc)
                passB.append((wsrc, r, cb, dst, bd, bslot))

        def emit_passB1(j):
            wsrc, r, cb, dst, bd, bslot = passB[j]
            bb = j % 2
            ro = b_p1r[bb]
            p.add("gpsimd", lambda e, r=r, cb=cb, bb=bb, wsrc=wsrc: e.dma_start(
                out=xst[bb][:, :], in_=wsrc[r * 128:(r + 1) * 128, cb * 1024:(cb + 1) * 1024]),
                writes=[b_xst[bb]], dma=True)
            p.add("vector", lambda e, bb=bb, bslot=bslot: e.tensor_scalar(
                nn32[:, :], xst[bb][:, :], scw3[bslot], -0.5, op0=ALU.mult, op1=ALU.is_lt),
                reads=[b_xst[bb], b_scw3[bslot]], writes=[b_nn])
            p.add("vector", lambda e, bb=bb, bslot=bslot: e.tensor_scalar(
                ee32[:, :], xst[bb][:, :], scw3[bslot], 0.5, op0=ALU.mult, op1=ALU.is_gt),
                reads=[b_xst[bb], b_scw3[bslot]], writes=[b_ee])
            p.add("vector", lambda e, bb=bb: e.tensor_tensor(
                p1b[:, bb * 1024:(bb + 1) * 1024], nn32[:, :], ee32[:, :],
                op=ALU.subtract),
                reads=[b_nn, b_ee], writes=[ro])

        trB_guard = b_psTh[1].handover()
        for h in range(NH):
            if h > 0:
                emit_wqst(h)
                for oc in range(2):
                    for ts_ in range(8):
                        emit_qgrp(h, oc, ts_)
                for oc in range(2):
                    for ts_ in range(8):
                        emit_kgrp(h, oc, ts_)
            if h == NH - 1:
                arA_g = b_arA.handover()
                for isl_ in range(8):
                    p.add("sync", lambda e, isl_=isl_: e.dma_start(
                        out=arA[:, :].rearrange("pp (dc i) -> pp dc i", i=NI)[:, :, isl_ * 512:(isl_ + 1) * 512],
                        in_=wgT_d.rearrange("(dc pp) i -> pp dc i", pp=128)[:, :, isl_ * 512:(isl_ + 1) * 512]),
                        reads=[b_d["wgT"]], writes=[b_wg8[isl_]],
                        extra=arA_g, dma=True)
            if h > 0:
                emit_vmemset(h)
                for tk in range(NTC):
                    emit_vchunk(h, tk)
            def emit_qk(bi, c):
                g = c % 3
                for d2 in range(2):
                    p.add("tensor", lambda e, bi=bi, c=c, g=g, d2=d2: e.matmul(
                        bank(g), kT[:, d2 * T + c * 128: d2 * T + (c + 1) * 128],
                        qT[:, d2 * T + bi * 512: d2 * T + (bi + 1) * 512],
                        start=(d2 == 0), stop=(d2 == 1)),
                        reads=[b_kT, b_qT], writes=[b_bank[g]])
                p.add("scalar", lambda e, g=g, c=c: e.activation(
                    pT[c % 3][:, :], bank(g), AF.Exp, bias=0.0, scale=ISCL),
                    reads=[b_bank[g]], writes=[b_pT[c % 3]])

            def emit_pv(c):
                for sub in range(4):
                    p.add("tensor", lambda e, c=c, sub=sub: e.matmul(
                        pv_ps(sub, 0, 257), pT[c % 3][:, sub * 128:(sub + 1) * 128],
                        vA[:, c * 257:(c + 1) * 257],
                        start=(c == 0), stop=(c == NTC - 1)),
                        reads=[b_pT[c % 3], b_vA], writes=[b_bank[3 + sub]],
                        extra=(trB_guard if h == 0 and c == 0 and sub == 3 else ()))

            def finalize_recips():
                for sub in range(4):
                    p.add("vector", lambda e, sub=sub: e.reciprocal(
                        repl[:, sub:sub + 1], pv_ps(sub, 256, 257)),
                        reads=[b_bank[3 + sub]], writes=[b_repl])

            def finalize_atto(bi):
                for sub in range(4):
                    p.add("scalar", lambda e, sub=sub: e.activation(
                        att_o[:, sub * 256:(sub + 1) * 256], pv_ps(sub, 0, 256),
                        AF.Identity, scale=repl[:, sub:sub + 1]),
                        reads=[b_bank[3 + sub], b_repl], writes=[b_atto],
                        extra=(b_wkvT.handover()
                               if h == 0 and bi == 0 and sub == 0 else ()))

            def trans_tasks(bi):
                pb = 0 if (h == NH - 1 and bi == 7) else bi % 2
                for half in range(2):
                    for sub in range(4):
                        yield lambda sub=sub, half=half: p.add(
                            "tensor", lambda e, sub=sub, half=half: e.transpose(
                                psT0[:, (half * 4 + sub) * 128:(half * 4 + sub + 1) * 128],
                                att_o[:, sub * 256 + half * 128: sub * 256 + half * 128 + 128],
                                ident[:]),
                            reads=[b_atto, b_ident], writes=[b_psTh[0]])

                def cpdma(pb=pb, bi=bi):
                    p.add("vector", lambda e, pb=pb: e.tensor_copy(
                        aoTh[pb][:, :], psT0[:, 0:1024]),
                        reads=[b_psTh[0]], writes=[b_aoTh[pb]])
                    p.add("sync", lambda e, h=h, bi=bi, pb=pb: e.dma_start(
                        out=aoT_d[h * 256:(h + 1) * 256, bi * 512:(bi + 1) * 512]
                        .rearrange("(ch pp) t -> pp ch t", pp=128),
                        in_=aoTh[pb][:, :].rearrange("pp (ch t) -> pp ch t", t=512)),
                        reads=[b_aoTh[pb]], writes=[b_d["aoT"]], dma=True)
                yield cpdma
                gi = h * 8 + bi
                for j in range(gi * 3 // 2, min((gi + 1) * 3 // 2, len(passB))):
                    for dc in range(8):
                        yield lambda j=j, dc=dc: _passB2_tr(j, dc)
                    yield lambda j=j: _passB2_fin(j)

            def _passB2_tr(j, dc):
                wsrc, r, cb, dst, bd, bslot = passB[j]
                bb = j % 2
                ro = b_p1r[bb]
                p.add("tensor", lambda e, dc=dc, bb=bb: e.transpose(
                    psT0[:, dc * 128:(dc + 1) * 128],
                    p1b[:, bb * 1024 + dc * 128: bb * 1024 + (dc + 1) * 128], ident[:]),
                    reads=[ro, b_ident], writes=[b_psTh[0]])

            def _passB2_fin(j):
                wsrc, r, cb, dst, bd, bslot = passB[j]
                bb = j % 2
                rsb = b_p1r[2 + bb]
                p.add("vector", lambda e, bb=bb: e.tensor_copy(
                    p1b[:, (2 + bb) * 1024:(3 + bb) * 1024], psT0[:, 0:1024]),
                    reads=[b_psTh[0]], writes=[rsb])
                p.add("gpsimd", lambda e, r=r, cb=cb, dst=dst, bb=bb: e.dma_start(
                    out=dst.rearrange("(cc pp) o -> pp cc o", pp=128)[:, cb * 8:(cb + 1) * 8, r * 128:(r + 1) * 128],
                    in_=p1b[:, (2 + bb) * 1024:(3 + bb) * 1024].rearrange("pp (cc t) -> pp cc t", t=128)),
                    reads=[rsb], writes=[bd], dma=True)

            for bi in range(8):
                tq = list(trans_tasks(bi - 1)) if bi >= 1 else []
                ti = 0
                for c in range(NTC):
                    emit_qk(bi, c)
                    if c == 1 and bi >= 1:
                        finalize_atto(bi - 1)
                    if c >= 4:
                        for _ in range(2):
                            if ti < len(tq):
                                tq[ti]()
                                ti += 1
                    if c >= 2:
                        emit_pv(c - 2)
                while ti < len(tq):
                    tq[ti]()
                    ti += 1
                emit_pv(NTC - 2)
                emit_pv(NTC - 1)
                finalize_recips()
                if bi == 7:
                    finalize_atto(7)
                gi0 = h * 8 + bi
                for j in range(gi0 * 3 // 2, min((gi0 + 1) * 3 // 2, len(passB))):
                    emit_passB1(j)
            for fn in trans_tasks(7):
                fn()
        tok_wu = None
        b_wu8h = [[Buf(), Buf()] for _ in range(8)]
        xnq_guard = b_kvT.handover() + b_aoTh[1].handover()
        guard_wu = (b_qT.handover() + b_kT.handover() + b_vA.handover() +
                    b_wqst[0].handover() + b_wqst[1].handover() +
                    b_pT[0].handover() + b_pT[1].handover() + b_pT[2].handover() +
                    b_kvT.handover() + b_aoTh[0].handover() + b_aoTh[1].handover() +
                    b_atto.handover())
        def emit_wu_loads():
            tok = None
            for isl in range(8):
                tok = p.add("scalar" if isl % 2 == 0 else "sync",
                            lambda e, isl=isl: e.dma_start(
                    out=arB[:, 0:32768].rearrange("pp (dc i) -> pp dc i", i=NI)[:, :, isl * 512:(isl + 1) * 512],
                    in_=wuT_d.rearrange("(dc pp) i -> pp dc i", pp=128)[:, :, isl * 512:(isl + 1) * 512]),
                    reads=[b_d["wuT"]], writes=[b_wu8h[isl][0]],
                    extra=guard_wu, dma=True)
            return tok
        h32_guard = (b_xst[0].handover() + b_xst[1].handover() +
                     b_xn32[0].handover() + b_xn32[1].handover() +
                     b_wstA[0].handover() + b_wstA[1].handover() +
                     b_wstA[2].handover())

        bank6_guard = b_bank[6].handover()
        _p3b1 = [False]

        def emit_trans_group(tc, g):
            tb = g % 2
            ex = ()
            if tb == 1 and not _p3b1[0]:
                _p3b1[0] = True
                ex = bank6_guard
            for dc in range(8):
                ic = g * 8 + dc
                p.add("tensor", lambda e, ic=ic, dc=dc, tb=tb: e.transpose(
                    BKap(tb)[:, dc * 128:(dc + 1) * 128],
                    hq_neg[:, ic * 128:(ic + 1) * 128], ident[:]),
                    reads=[b_hqneg[g], b_ident], writes=[b_psTh[tb]],
                    extra=(ex if dc == 0 else ()))
            p.add("scalar", lambda e, g=g, tb=tb: e.activation(
                p1b[:, g * 1024:(g + 1) * 1024], BKap(tb)[:, :],
                AF.Identity, bias=0.0, scale=1.0),
                reads=[b_psTh[tb]], writes=[b_p1r[g]])

        def emit_hq_dma(tc):
            p.add("sync", lambda e, tc=tc: e.dma_start(
                out=hqT_d.rearrange("(ic pp) t -> pp ic t", pp=128)[:, :, tc * 128:(tc + 1) * 128],
                in_=p1b[:, :].rearrange("pp (ic t) -> pp ic t", t=128)),
                reads=[b_p1r[0], b_p1r[1], b_p1r[2], b_p1r[3]],
                writes=[b_d["hqT"]], dma=True)

        def xnq_load(tc):
            bb = tc % 2
            p.add("sync", lambda e, tc=tc, bb=bb: e.dma_start(
                out=xnq_st[bb][:, :].rearrange("pp (dc t) -> pp dc t", t=128),
                in_=xnqT_d.rearrange("(dc pp) t -> pp dc t", pp=128)[:, :, tc * 128:(tc + 1) * 128]),
                reads=[b_d["xnqT"]], writes=[b_xnqst[bb]],
                extra=(xnq_guard if tc < 2 else ()), dma=True)

        p.add("vector", lambda e: e.tensor_scalar(
            sgall[:, :], rsx[:, :], bvec[:, 0:1], None, op0=ALU.mult),
            reads=[b_rsx, b_bvec], writes=[b_sgall])
        p.add("vector", lambda e: e.tensor_scalar(
            suall[:, :], rsx[:, :], bvec[:, 1:2], None, op0=ALU.mult),
            reads=[b_rsx, b_bvec], writes=[b_suall])
        xnq_load(0)
        xnq_load(1)
        tok_wu = emit_wu_loads()
        for b_ in b_hqneg:
            b_.w = tok_wu
        for tc in range(NTC):
            bb = tc % 2
            h32 = h32p[bb]
            b_hs = b_h32[bb]
            st, b_st = stp[bb], b_stp[bb]
            sxv, b_sxv = sxv2[bb], b_sxv2[bb]
            for isl in range(8):
                gg, gu = isl % 2, 2 + isl % 2
                for dc in range(NDC):
                    p.add("tensor", lambda e, isl=isl, dc=dc, gg=gg, bb=bb: e.matmul(
                        bank(gg), xnq_st[bb][:, dc * 128:(dc + 1) * 128],
                        wgqT_sl(dc, isl * 512, 512),
                        start=(dc == 0), stop=(dc == NDC - 1)),
                        reads=[b_xnqst[bb], b_wg8[isl]], writes=[b_bank[gg]])
                for dc in range(NDC):
                    p.add("tensor", lambda e, isl=isl, dc=dc, gu=gu, bb=bb: e.matmul(
                        bank(gu), xnq_st[bb][:, dc * 128:(dc + 1) * 128],
                        wuqT_sl(dc, isl * 512, 512),
                        start=(dc == 0), stop=(dc == NDC - 1)),
                        reads=[b_xnqst[bb], b_wu8h[isl][0]],
                        writes=[b_bank[gu]])
                p.add("scalar", lambda e, isl=isl, gu=gu, h32=h32, tc=tc: e.activation(
                    h32[:, isl * 512:(isl + 1) * 512], bank(gu), AF.Identity,
                    scale=suall[:, tc:tc + 1]),
                    reads=[b_bank[gu], b_suall], writes=[b_hs[isl]],
                    extra=(h32_guard if tc < 2 and isl == 0 else ()))
                p.add("scalar", lambda e, gg=gg, isl=isl, tc=tc: e.activation(
                    sil32[isl % 4][:, :], bank(gg), AF.Silu, scale=sgall[:, tc:tc + 1]),
                    reads=[b_bank[gg], b_sgall], writes=[b_sil[isl % 4]])
                p.add("gpsimd", lambda e, isl=isl, h32=h32: e.tensor_tensor(
                    h32[:, isl * 512:(isl + 1) * 512], sil32[isl % 4][:, :],
                    h32[:, isl * 512:(isl + 1) * 512], op=ALU.mult),
                    reads=[b_sil[isl % 4]], writes=[b_hs[isl]])
                p.add("vector", lambda e, isl=isl, h32=h32, bb=bb: e.tensor_reduce(
                    acc32[:, bb * 8 + isl: bb * 8 + isl + 1],
                    h32[:, isl * 512:(isl + 1) * 512], axis=AX.X, op=ALU.max,
                    apply_absolute_value=True),
                    reads=[b_hs[isl]], writes=[b_acc])
            p.add("vector", lambda e, st=st, bb=bb: e.tensor_reduce(
                st[0], acc32[:, bb * 8: bb * 8 + 8], axis=AX.X, op=ALU.max),
                reads=[b_acc], writes=[b_st[0]])
            p.add("vector", lambda e, tc=tc, st=st: e.tensor_scalar(
                rsh[:, tc:tc + 1], st[0], 1e-5, 1.0 / 127.0, op0=ALU.max, op1=ALU.mult),
                reads=[b_st[0]], writes=[b_rsh])
            if tc + 2 < NTC:
                xnq_load(tc + 2)
            if tc >= 1:
                for g_ in range(4):
                    emit_trans_group(tc - 1, g_)
                emit_hq_dma(tc - 1)
            actscale(st[0], b_st[0], st, b_st, sxv, b_sxv)
            for sl in range(4):
                round5_2(h32[:, sl * 1024:(sl + 1) * 1024],
                         [b_hs[2 * sl], b_hs[2 * sl + 1]],
                         hq_neg[:, sl * 1024:(sl + 1) * 1024], b_hqneg[sl], sxv, b_sxv)
        for g in range(4):
            emit_trans_group(NTC - 1, g)
        emit_hq_dma(NTC - 1)

        b_wd2 = [[Buf() for _ in range(4)], [Buf() for _ in range(4)]]
        b_wo2 = [Buf(), Buf()]
        guard_wd = (b_xnqst[0].handover() + b_xnqst[1].handover() +
                    [d for b_ in b_hqneg for d in b_.handover()] +
                    [d for pr in b_wu8h for bb_ in pr for d in bb_.handover()])
        guard_wo = (b_arA.handover() +
                    [d for bb_ in b_wg8 for d in bb_.handover()])
        arA_guard = guard_wo
        h32_guard2 = [d for par in range(2) for bb_ in b_h32[par] for d in bb_.handover()]
        qq_guard = b_qq.handover()

        def emit_x_load(tc, bb, eng):
            p.add(eng, lambda e, tc=tc, bb=bb: e.dma_start(
                out=x4st[bb][:, :], in_=x_d[tc * 128:(tc + 1) * 128, :]),
                writes=[b_x4[bb]],
                extra=((h32_guard2 + b_xn32[bb].handover()) if tc < 2 else ()),
                dma=True)

        def emit_hqao_load(tc, bb):
            p.add("sync", lambda e, tc=tc, bb=bb: e.dma_start(
                out=hq_st[bb][:, :].rearrange("pp (ic t) -> pp ic t", t=128),
                in_=hqT_d.rearrange("(ic pp) t -> pp ic t", pp=128)[:, :, tc * 128:(tc + 1) * 128]),
                reads=[b_d["hqT"]], writes=[b_hqst[bb]],
                extra=(arA_guard if tc < 2 else ()), dma=True)
            p.add("sync", lambda e, tc=tc, bb=bb: e.dma_start(
                out=ao_st[bb][:, :].rearrange("pp (oc t) -> pp oc t", t=128),
                in_=aoT_d.rearrange("(oc pp) t -> pp oc t", pp=128)[:, :, tc * 128:(tc + 1) * 128]),
                reads=[b_d["aoT"]], writes=[b_aost[bb]],
                extra=(arA_guard if tc < 2 else ()), dma=True)

        for n_ in range(2):
            p.add("scalar", lambda e, n_=n_: e.dma_start(
                out=arA[:, 0:16384].rearrange("pp (oc n) -> pp oc n", n=D)[:, :, n_ * 512:(n_ + 1) * 512],
                in_=woT_d.rearrange("(oc pp) n -> pp oc n", pp=128)[:, :, n_ * 512:(n_ + 1) * 512]),
                reads=[b_d["woT"]], writes=[b_wo2[n_]],
                extra=guard_wo, dma=True)
        for n_ in range(2):
            for q_ in range(4):
                p.add("gpsimd" if n_ == 0 else "scalar",
                      lambda e, n_=n_, q_=q_: e.dma_start(
                    out=arB[:, 0:32768].rearrange("pp (ic n) -> pp ic n", n=D)[:, q_ * 8:(q_ + 1) * 8, n_ * 512:(n_ + 1) * 512],
                    in_=wdT_d.rearrange("(ic pp) n -> pp ic n", pp=128)[:, q_ * 8:(q_ + 1) * 8, n_ * 512:(n_ + 1) * 512]),
                    reads=[b_d["wdT"]], writes=[b_wd2[n_][q_]],
                    extra=guard_wd, dma=True)
        p.add("vector", lambda e: e.tensor_scalar(
            sdall[:, :], rsh[:, :], bvec[:, 2:3], None, op0=ALU.mult),
            reads=[b_rsh, b_bvec], writes=[b_sdall], extra=b_accA.handover())
        bkk = lambda i: bank(i) if i < 7 else psT0[:, 0:1024].bitcast(F32)
        b_bkl = lambda i: [b_bank[i]] if i < 7 else [b_psTh[0]]
        trB_guard2 = b_psTh[1].handover()
        dfin = None
        for tc in range(NTC):
            bb = tc % 2
            dbk = (0, 1) if bb == 0 else (4, 5)
            wbk = (2, 3) if bb == 0 else (6, 7)
            emit_hqao_load(tc, bb)
            emit_x_load(tc, bb, "sync" if tc < 2 else "scalar")
            for n_ in range(2):
                for oc in range(16):
                    p.add("tensor", lambda e, oc=oc, n_=n_, bb=bb, wbk=wbk: e.matmul(
                        bkk(wbk[n_]), ao_st[bb][:, oc * 128:(oc + 1) * 128],
                        woT_sl(oc, n_ * 512, 512),
                        start=(oc == 0), stop=(oc == 15)),
                        reads=[b_aost[bb], b_wo2[n_]], writes=b_bkl(wbk[n_]),
                        extra=(trB_guard2 if tc == 1 and n_ == 0 and oc == 0 else ()))
                p.add("vector", lambda e, n_=n_, bb=bb, wbk=wbk: e.scalar_tensor_tensor(
                    out32[bb][:, n_ * 512:(n_ + 1) * 512], bkk(wbk[n_]), 1.0,
                    x4st[bb][:, n_ * 512:(n_ + 1) * 512],
                    op0=ALU.mult, op1=ALU.add),
                    reads=b_bkl(wbk[n_]) + [b_x4[bb]], writes=[b_xst[bb]],
                    extra=(h32_guard2 if tc < 2 and n_ == 0 else ()))
            for n_ in range(2):
                for ic in range(32):
                    p.add("tensor", lambda e, ic=ic, n_=n_, bb=bb, dbk=dbk: e.matmul(
                        bank(dbk[n_]), hq_st[bb][:, ic * 128:(ic + 1) * 128],
                        wdqT_sl(ic, n_ * 512, 512),
                        start=(ic == 0), stop=(ic == 31)),
                        reads=[b_hqst[bb], b_wd2[n_][ic // 8]],
                        writes=[b_bank[dbk[n_]]])
            p.add("scalar", lambda e, tc=tc, bb=bb, dbk=dbk: e.activation(
                t1_32[:, :], psF[:, dbk[0] * 512:(dbk[0] + 2) * 512],
                AF.Identity, scale=sdall[:, tc:tc + 1]),
                reads=[b_bank[dbk[0]], b_bank[dbk[1]], b_sdall], writes=[b_t1],
                extra=((qq_guard + h32_guard2) if tc == 0 else ()))
            p.add("vector", lambda e, bb=bb: e.tensor_tensor(
                out32[bb][:, :], out32[bb][:, :], t1_32[:, :], op=ALU.add),
                reads=[b_xst[bb], b_t1], writes=[b_xst[bb]])
            dfin = p.add("scalar", lambda e, tc=tc, bb=bb: e.dma_start(
                out=y_d[tc * 128:(tc + 1) * 128, :], in_=out32[bb][:, :]),
                reads=[b_xst[bb]], dma=True)
        p.add_final_wait(dfin)
        if debug:
            p.add_final_wait(p.add("sync", lambda e: e.dma_start(
                out=sc_dbg, in_=smf[:, 0:128]),
                reads=[b_rsx, b_rsh, b_bvec], dma=True))

        blk = ctx.enter_context(nc.Block())
        p.emit(nc, blk, ctx)
    return nc


def kernel(L0_attn_x, L0_attn_Wq, L0_attn_Wkv_down, L0_attn_Wk_up,
           L0_attn_Wv_up, L0_attn_Wo, L0_ffn_w_gate, L0_ffn_w_up,
           L0_ffn_w_down):
    nc = build_nc()
    ident = np.eye(128, dtype=ml_dtypes.bfloat16)
    x = np.ascontiguousarray(np.asarray(L0_attn_x, dtype=np.float32))
    common = {
        "Wq": np.ascontiguousarray(np.asarray(L0_attn_Wq, dtype=np.float32)),
        "Wkv": np.ascontiguousarray(np.asarray(L0_attn_Wkv_down, dtype=np.float32)),
        "Wku": np.ascontiguousarray(np.asarray(L0_attn_Wk_up, dtype=np.float32)),
        "Wvu": np.ascontiguousarray(np.asarray(L0_attn_Wv_up, dtype=np.float32)),
        "Wo": np.ascontiguousarray(np.asarray(L0_attn_Wo, dtype=np.float32)),
        "Wg": np.ascontiguousarray(np.asarray(L0_ffn_w_gate, dtype=np.float32)),
        "Wu": np.ascontiguousarray(np.asarray(L0_ffn_w_up, dtype=np.float32)),
        "Wd": np.ascontiguousarray(np.asarray(L0_ffn_w_down, dtype=np.float32)),
        "ident": ident,
    }
    in_maps = [dict(common, x=x[c]) for c in range(8)]
    res = run_bass_kernel_spmd(nc, in_maps, core_ids=list(range(8)))
    return np.stack([np.asarray(r["y"], dtype=np.float32) for r in res.results],
                    axis=0)


# version 42
# speedup vs baseline: 1.3643x; 1.0027x over previous
import numpy as np
import ml_dtypes
import contextlib
import concourse.bass as bass
import concourse.mybir as mybir
from concourse.bass_utils import run_bass_kernel_spmd

F32 = mybir.dt.float32
BF16 = mybir.dt.bfloat16
I32 = mybir.dt.int32
AF = mybir.ActivationFunctionType
ALU = mybir.AluOpType
AX = mybir.AxisListType

T = 4096
D = 1024
NH = 8
DO = 2048
LR = 64
NI = 4096
NTC = T // 128
NDC = D // 128
CM = 12582912.0
ISCL = 0.0625

ENGINES = ("sync", "tensor", "vector", "scalar", "gpsimd")


class Buf:
    __slots__ = ("w", "r")

    def __init__(self):
        self.w = None
        self.r = []

    def handover(self):
        return ([self.w] if self.w is not None else []) + list(self.r)


class Prog:
    def __init__(self):
        self.ops = []
        self.final_waits = []

    def add(self, eng, fn, reads=(), writes=(), extra=(), dma=False):
        deps = []
        for b in writes:
            if b.w is not None:
                deps.append(b.w)
            deps.extend(b.r)
        for b in reads:
            if b.w is not None:
                deps.append(b.w)
        deps.extend(d for d in extra if d is not None)
        i = len(self.ops)
        assert all(d < i for d in deps)
        self.ops.append({"eng": eng, "fn": fn, "deps": sorted(set(deps)),
                         "dma": dma})
        for b in writes:
            b.w = i
            b.r = []
        for b in reads:
            b.r.append(i)
        return i

    def add_final_wait(self, dep):
        self.final_waits.append(dep)

    NSLOT = 12

    def emit(self, nc, block, ctx):
        ops = self.ops
        NS = self.NSLOT
        ekey = [None] * len(ops)
        qidx = {}
        for i, op in enumerate(ops):
            if op["dma"]:
                q = "q_" + op["eng"]
                j = qidx.get(q, 0)
                qidx[q] = j + 1
                ekey[i] = (q, j % NS)
            else:
                ekey[i] = op["eng"]
        for i, op in enumerate(ops):
            op["deps"] = [d for d in op["deps"]
                          if not (ekey[d] == "tensor" and ekey[i] == "tensor")]
        needed = set(i for i, op in enumerate(ops) if op["dma"])
        for op in ops:
            needed.update(op["deps"])
        needed.update(self.final_waits)
        sems = {}
        for e in ENGINES:
            sems[e] = ctx.enter_context(nc.semaphore("s_" + e))
            for s in range(NS):
                k = ("q_" + e, s)
                sems[k] = ctx.enter_context(nc.semaphore(f"sq_{e}_{s}"))
        val = [0] * len(ops)
        cnt = {k: 0 for k in sems}
        for i, op in enumerate(ops):
            if i in needed:
                cnt[ekey[i]] += 16 if op["dma"] else 1
                val[i] = cnt[ekey[i]]
        per_eng = {e: [] for e in ENGINES}
        for i, op in enumerate(ops):
            per_eng[op["eng"]].append(i)

        last_val = dict(cnt)

        def emit_engine(e, eng):
            maxw = {}

            def w(k, v):
                if v > 0 and maxw.get(k, 0) < v:
                    eng.wait_ge(sems[k], v)
                    maxw[k] = v

            for i in per_eng[e]:
                op = ops[i]
                if op["dma"]:
                    w(ekey[i], val[i] - 16)
                for d in op["deps"]:
                    assert val[d] > 0, (i, d)
                    w(ekey[d], val[d])
                inst = op["fn"](eng)
                if i in needed:
                    inst.then_inc(sems[ekey[i]], 16 if op["dma"] else 1)
            for d in self.final_waits:
                w(ekey[d], val[d])
            if e == "sync":
                for k, v in last_val.items():
                    if isinstance(k, tuple):
                        w(k, v)

        @block.sync
        def _(eng):
            emit_engine("sync", eng)

        @block.tensor
        def _(eng):
            emit_engine("tensor", eng)

        @block.vector
        def _(eng):
            emit_engine("vector", eng)

        @block.scalar
        def _(eng):
            emit_engine("scalar", eng)

        @block.gpsimd
        def _(eng):
            emit_engine("gpsimd", eng)


def build_nc(debug=False):
    nc = bass.Bass("TRN2", target_bir_lowering=False,
                   detect_race_conditions=False)

    x_d = nc.dram_tensor("x", [T, D], F32, kind="ExternalInput").ap()
    wq_d = nc.dram_tensor("Wq", [DO, D], F32, kind="ExternalInput").ap()
    wkv_d = nc.dram_tensor("Wkv", [LR, D], F32, kind="ExternalInput").ap()
    wku_d = nc.dram_tensor("Wku", [DO, LR], F32, kind="ExternalInput").ap()
    wvu_d = nc.dram_tensor("Wvu", [DO, LR], F32, kind="ExternalInput").ap()
    wo_d = nc.dram_tensor("Wo", [D, DO], F32, kind="ExternalInput").ap()
    wg_d = nc.dram_tensor("Wg", [NI, D], F32, kind="ExternalInput").ap()
    wu_d = nc.dram_tensor("Wu", [NI, D], F32, kind="ExternalInput").ap()
    wd_d = nc.dram_tensor("Wd", [D, NI], F32, kind="ExternalInput").ap()
    id_d = nc.dram_tensor("ident", [128, 128], BF16, kind="ExternalInput").ap()
    y_d = nc.dram_tensor("y", [T, D], F32, kind="ExternalOutput").ap()

    sk = "ExternalOutput" if debug else "Internal"
    xnqT_d = nc.dram_tensor("xnqT_d", [D, T], BF16, kind=sk).ap()
    aoT_d = nc.dram_tensor("aoT_d", [DO, T], BF16, kind=sk).ap()
    hqT_d = nc.dram_tensor("hqT_d", [NI, T], BF16, kind=sk).ap()
    wqT_d = nc.dram_tensor("wqT_d", [D, DO], BF16, kind=sk).ap()
    woT_d = nc.dram_tensor("woT_d", [DO, D], BF16, kind=sk).ap()
    wgT_d = nc.dram_tensor("wgT_d", [D, NI], BF16, kind=sk).ap()
    wuT_d = nc.dram_tensor("wuT_d", [D, NI], BF16, kind=sk).ap()
    wdT_d = nc.dram_tensor("wdT_d", [NI, D], BF16, kind=sk).ap()
    if debug:
        xnT_dbg = nc.dram_tensor("xnT_dbg", [D, T], BF16, kind="ExternalOutput").ap()
        kvT_dbg = nc.dram_tensor("kvT_dbg", [LR, T], BF16, kind="ExternalOutput").ap()
        sc_dbg = nc.dram_tensor("sc_dbg", [128, 128], F32, kind="ExternalOutput").ap()

    p = Prog()
    with contextlib.ExitStack() as ctx:
        sbuf = lambda n, s, dt: ctx.enter_context(nc.sbuf_tensor(n, s, dt))
        ident = sbuf("identsb", [128, 128], BF16)
        arA = sbuf("arA", [128, 32768], BF16)
        arB = sbuf("arB", [128, 38912], BF16)
        arC = sbuf("arC", [128, 10240], F32)
        rsc = sbuf("rsc", [128, 3072], BF16)
        t32 = sbuf("t32", [128, 1024], I32)
        smf = sbuf("smf", [128, 272], F32)
        ones_t = sbuf("ones_t", [128, 128], F32)
        wkuT = sbuf("wkuT", [LR, DO], BF16)
        wvuT = sbuf("wvuT", [LR, DO], BF16)
        p1b = sbuf("p1b", [128, 4096], BF16)
        psF = ctx.enter_context(nc.psum_tensor("psF", [128, 3584], F32))
        psT0 = ctx.enter_context(nc.psum_tensor("psT0", [128, 1024], BF16))
        bank = lambda i: psF[:, i * 512:(i + 1) * 512]
        trB = psF[:, 3072:3584].bitcast(BF16)
        BKap = lambda t: psT0[:, 0:1024] if t == 0 else trB
        _tb = [0]

        def nb():
            t = _tb[0]
            _tb[0] ^= 1
            return t
        pv_ps = lambda sub, a, b: psF[:, 1536 + sub * 512 + a: 1536 + sub * 512 + b]

        b_ident, b_ones, b_eps = Buf(), Buf(), Buf()
        b_arA = Buf()
        b_qT, b_kT, b_vA = Buf(), Buf(), Buf()
        b_wqst = [Buf(), Buf()]
        b_pT = [Buf(), Buf(), Buf()]
        b_atto = Buf()
        b_aoTh = [Buf(), Buf()]
        b_wuqT = Buf()
        b_wg8 = [Buf() for _ in range(8)]
        b_xnqst = [Buf(), Buf()]
        b_hqneg = [Buf() for _ in range(4)]
        b_woT = Buf()
        b_hqst = [Buf(), Buf()]
        b_aost = [Buf(), Buf()]
        b_xst = [Buf(), Buf()]
        b_xn32 = [Buf(), Buf()]
        b_h32 = [[Buf() for _ in range(8)] for _ in range(2)]
        b_nn, b_ee, b_qq = Buf(), Buf(), Buf()
        b_sil = [Buf(), Buf(), Buf(), Buf()]
        b_t1 = Buf()
        b_x4 = [Buf(), Buf()]
        b_t32 = Buf()
        b_stp = [[Buf() for _ in range(4)] for _ in range(2)]
        b_wstA = [Buf(), Buf(), Buf()]
        b_sxv2 = [Buf(), Buf()]
        b_sgv, b_suv, b_sdv, b_mu = Buf(), Buf(), Buf(), Buf()
        b_scw3 = [Buf(), Buf(), Buf()]
        b_rsx, b_rsh, b_bvec, b_repl, b_acc = Buf(), Buf(), Buf(), Buf(), Buf()
        b_kvT, b_wkvT, b_wkuT, b_wvuT = Buf(), Buf(), Buf(), Buf()
        b_p1r = [Buf(), Buf(), Buf(), Buf()]
        b_bank = [Buf() for _ in range(7)]
        b_psTh = [Buf(), Buf()]
        b_d = {n: Buf() for n in
               ("xnqT", "aoT", "hqT", "wqT", "woT", "wgT", "wuT", "wdT")}

        rsx = smf[:, 0:32]
        rsh = smf[:, 32:64]
        bvec = smf[:, 64:68]
        repl = smf[:, 68:72]
        stp = [[smf[:, o + i: o + i + 1] for i in range(4)] for o in (72, 76)]
        epsv = smf[:, 80:81]
        sxv2 = [smf[:, 81 + i: 82 + i] for i in range(2)]
        sgv = smf[:, 83:84]
        suv = smf[:, 84:85]
        sdv = smf[:, 85:86]
        scw3 = [smf[:, 86 + i: 87 + i] for i in range(3)]
        mu11 = smf[:, 89:90]
        acc32 = smf[:, 96:112]
        sgall = smf[:, 208:240]
        suall = smf[:, 240:272]
        sdall = acc32
        sdall = smf[:, 112:144]
        b_sdall = Buf()
        b_sgall, b_suall = Buf(), Buf()
        accA_sl = lambda s: smf[:, 112 + s: 113 + s]
        b_accA = Buf()
        wstA = [arC[:, 4096 + b * 1024: 4096 + (b + 1) * 1024] for b in (0, 1, 2)]

        xnT_sl = lambda dc, t0, n: arA[:, dc * T + t0: dc * T + t0 + n]
        wgqT_sl = lambda dc, i0, n: arA[:, dc * NI + i0: dc * NI + i0 + n]
        woT_sl = lambda oc, n0, n: arA[:, oc * D + n0: oc * D + n0 + n]
        hq_st = [arA[:, 16384 + b * 4096: 16384 + (b + 1) * 4096] for b in (0, 1)]
        ao_st = [arA[:, 24576 + b * 2048: 24576 + (b + 1) * 2048] for b in (0, 1)]

        qT = arB[:, 0:8192]
        kT = arB[:, 8192:16384]
        vA = arB[:, 16384:24608]
        wqst = [arB[:, 24608 + b * 2048: 24608 + (b + 1) * 2048] for b in (0, 1)]
        pT = [arB[:, 28704 + b * 512: 28704 + (b + 1) * 512] for b in range(3)]
        att_o = arB[:, 30240:31264]
        wkvT = arB[:, 30240:30752]
        aoTh = [arB[:, 31264 + b * 1024: 31264 + (b + 1) * 1024] for b in (0, 1)]
        kvT = arB[0:LR, 33312:37408]
        wuqT_sl = lambda dc, i0, n: arB[:, dc * NI + i0: dc * NI + i0 + n]
        wdqT_sl = lambda ic, n0, n: arB[:, ic * D + n0: ic * D + n0 + n]
        xnq_st = [arB[:, 32768 + b * 1024: 32768 + (b + 1) * 1024] for b in (0, 1)]
        hq_neg = arB[:, 34816:38912]

        xst = [arC[:, b * 1024:(b + 1) * 1024] for b in (0, 1)]
        xn32p = [arC[:, 2048 + b * 1024: 2048 + (b + 1) * 1024] for b in (0, 1)]
        h32p = [arC[:, b * 4096:(b + 1) * 4096] for b in (0, 1)]
        x4st = [arC[:, 2048 + b * 1024: 2048 + (b + 1) * 1024] for b in (0, 1)]
        out32 = [arC[:, b * 1024:(b + 1) * 1024] for b in (0, 1)]
        nn32 = rsc[:, 0:1024]
        ee32 = rsc[:, 1024:2048]
        qq32 = rsc[:, 2048:3072]
        t1_32 = arC[:, 4096:5120]
        sil32 = [arC[:, 8192 + b * 512: 8192 + (b + 1) * 512] for b in range(4)]

        p.add("sync", lambda e: e.dma_start(out=ident[:], in_=id_d),
              writes=[b_ident], dma=True)
        p.add("vector", lambda e: e.memset(epsv, 1e-5), writes=[b_eps])
        p.add("vector", lambda e: e.memset(ones_t[:, :], 1.0), writes=[b_ones])
        p.add("gpsimd", lambda e: e.dma_start(out=p1b[0:LR, 0:D], in_=wkv_d),
              writes=[b_p1r[0], b_p1r[1], b_p1r[2], b_p1r[3]], dma=True)
        for dc in range(NDC):
            p.add("tensor", lambda e, dc=dc: e.transpose(
                psT0[:, dc * 128: dc * 128 + LR],
                p1b[0:LR, dc * 128:(dc + 1) * 128], ident[0:LR, 0:LR]),
                reads=[b_p1r[dc // 2], b_ident], writes=[b_psTh[0]])
        for dc in range(NDC):
            p.add("vector", lambda e, dc=dc: e.tensor_copy(
                wkvT[:, dc * LR:(dc + 1) * LR], psT0[:, dc * 128: dc * 128 + LR]),
                reads=[b_psTh[0]], writes=[b_wkvT])

        lodB = [arB[:, 8192 + b * 2048: 8192 + (b + 1) * 2048] for b in (0, 1, 2)]
        stgB = [arB[:, 16384 + b * 1024: 16384 + (b + 1) * 1024] for b in range(4)]
        _sg = [0]
        b_ldB = [Buf(), Buf(), Buf()]
        b_stB = [Buf(), Buf(), Buf(), Buf()]
        prep_tiles = []
        for r in range(DO // 128):
            prep_tiles.append(("up", wku_d, wkuT, b_wkuT, r, 0))
        for r in range(DO // 128):
            prep_tiles.append(("up", wvu_d, wvuT, b_wvuT, r, 0))
        for r in range(DO // 128):
            prep_tiles.append(("pl", wq_d, wqT_d, b_d["wqT"], r, 1024))
        for r in range(D // 128):
            prep_tiles.append(("pl", wo_d, woT_d, b_d["woT"], r, 2048))
        _pc = [0]
        _pd = [0]

        def emit_prep_dma():
            if _pd[0] >= len(prep_tiles):
                return
            i = _pd[0]
            _pd[0] += 1
            kind, srct, dst, b_dst, r, cols = prep_tiles[i]
            par = i % 3
            n = LR if kind == "up" else cols
            p.add("gpsimd", lambda e, r=r, srct=srct, par=par, n=n: e.dma_start(
                out=lodB[par][:, 0:n], in_=srct[r * 128:(r + 1) * 128, :]),
                writes=[b_ldB[par]], dma=True)

        def emit_prep_comp():
            if _pc[0] >= _pd[0] or _pc[0] >= len(prep_tiles):
                return
            i = _pc[0]
            _pc[0] += 1
            kind, srct, dst, b_dst, r, cols = prep_tiles[i]
            par = i % 3
            if kind == "up":
                tb = nb()
                p.add("tensor", lambda e, par=par, tb=tb: e.transpose(
                    BKap(tb)[0:LR, 0:128], lodB[par][:, 0:LR],
                    ident[:]),
                    reads=[b_ldB[par], b_ident], writes=[b_psTh[tb]])
                p.add("vector", lambda e, r=r, dst=dst, tb=tb: e.tensor_copy(
                    dst[:, r * 128:(r + 1) * 128],
                    BKap(tb)[0:LR, 0:128]),
                    reads=[b_psTh[tb]], writes=[b_dst])
                return
            for g in range(cols // 1024):
                tb = nb()
                for dc in range(8):
                    cc = g * 8 + dc
                    p.add("tensor", lambda e, cc=cc, dc=dc, par=par, tb=tb: e.transpose(
                        BKap(tb)[:, dc * 128:(dc + 1) * 128],
                        lodB[par][:, cc * 128:(cc + 1) * 128], ident[:]),
                        reads=[b_ldB[par], b_ident], writes=[b_psTh[tb]])
                sg = _sg[0] % 4
                _sg[0] += 1
                p.add("vector", lambda e, sg=sg, tb=tb: e.tensor_copy(
                    stgB[sg][:, :], BKap(tb)[:, :]),
                    reads=[b_psTh[tb]], writes=[b_stB[sg]])
                p.add("sync", lambda e, r=r, g=g, dst=dst, sg=sg: e.dma_start(
                    out=dst.rearrange("(cc pp) o -> pp cc o", pp=128)[:, g * 8:(g + 1) * 8, r * 128:(r + 1) * 128],
                    in_=stgB[sg][:, :].rearrange("pp (cc t) -> pp cc t", t=128)),
                    reads=[b_stB[sg]], writes=[b_dst], dma=True)

        def emit_wqst(h):
            wb = h % 2
            p.add("scalar" if h > 0 else "sync", lambda e, h=h, wb=wb: e.dma_start(
                out=wqst[wb][:, :].rearrange("pp (dc o) -> pp dc o", o=256),
                in_=wqT_d.rearrange("(dc pp) o -> pp dc o", pp=128)[:, :, h * 256:(h + 1) * 256]),
                reads=[b_d["wqT"]], writes=[b_wqst[wb]], dma=True)

        def emit_qgrp(h, oc, ts_):
            wb = h % 2
            i6 = oc * 8 + ts_
            g = (1 + i6 % 5) if h == 0 else i6 % 6
            vec = (i6 % 2 == 0) if h == 0 else True
            for dc in range(NDC):
                p.add("tensor", lambda e, oc=oc, ts_=ts_, dc=dc, g=g, wb=wb: e.matmul(
                    bank(g),
                    wqst[wb][:, dc * 256 + oc * 128: dc * 256 + (oc + 1) * 128],
                    xnT_sl(dc, ts_ * 512, 512),
                    start=(dc == 0), stop=(dc == NDC - 1)),
                    reads=[b_wqst[wb], b_arA], writes=[b_bank[g]])
            p.add("vector" if vec else "scalar",
                  (lambda e, oc=oc, ts_=ts_, g=g: e.tensor_copy(
                      qT[:, oc * T + ts_ * 512: oc * T + (ts_ + 1) * 512], bank(g)))
                  if vec else
                  (lambda e, oc=oc, ts_=ts_, g=g: e.activation(
                      qT[:, oc * T + ts_ * 512: oc * T + (ts_ + 1) * 512], bank(g),
                      AF.Identity, bias=0.0, scale=1.0)),
                  reads=[b_bank[g]], writes=[b_qT])

        def emit_kgrp(h, oc, ts_, first=False):
            i6 = oc * 8 + ts_
            g = (1 + i6 % 5) if h == 0 else i6 % 6
            vec = (i6 % 2 == 0)
            p.add("tensor", lambda e, oc=oc, ts_=ts_, g=g, h=h: e.matmul(
                bank(g), wkuT[:, h * 256 + oc * 128: h * 256 + (oc + 1) * 128],
                kvT[:, ts_ * 512:(ts_ + 1) * 512], start=True, stop=True),
                reads=[b_wkuT, b_kvT], writes=[b_bank[g]])
            p.add("vector" if vec else "scalar",
                  (lambda e, oc=oc, ts_=ts_, g=g: e.tensor_copy(
                      kT[:, oc * T + ts_ * 512: oc * T + (ts_ + 1) * 512], bank(g)))
                  if vec else
                  (lambda e, oc=oc, ts_=ts_, g=g: e.activation(
                      kT[:, oc * T + ts_ * 512: oc * T + (ts_ + 1) * 512], bank(g),
                      AF.Identity, bias=0.0, scale=1.0)),
                  reads=[b_bank[g]], writes=[b_kT],
                  extra=((b_ldB[0].handover() + b_ldB[1].handover() +
                          b_ldB[2].handover())
                         if (first or (h == 0 and oc == 1 and ts_ <= 3))
                         else ()))

        def emit_vmemset(h):
            p.add("vector", lambda e: e.memset(vA[:, :], 1.0), writes=[b_vA],
                  extra=((b_stB[0].handover() + b_stB[1].handover() +
                          b_stB[2].handover() + b_stB[3].handover())
                         if h == 0 else ()))

        def emit_vchunk(h, tk):
            g = (1 + tk % 5) if h == 0 else tk % 6
            vec = (tk % 2 == 0)
            p.add("tensor", lambda e, tk=tk, g=g, h=h: e.matmul(
                bank(g)[:, 0:256], kvT[:, tk * 128:(tk + 1) * 128],
                wvuT[:, h * 256:(h + 1) * 256], start=True, stop=True),
                reads=[b_kvT, b_wvuT], writes=[b_bank[g]])
            p.add("vector" if vec else "scalar",
                  (lambda e, tk=tk, g=g: e.tensor_copy(
                      vA[:, tk * 257: tk * 257 + 256], bank(g)[:, 0:256]))
                  if vec else
                  (lambda e, tk=tk, g=g: e.activation(
                      vA[:, tk * 257: tk * 257 + 256], bank(g)[:, 0:256],
                      AF.Identity, bias=0.0, scale=1.0)),
                  reads=[b_bank[g]], writes=[b_vA])

        def round5(src_ap, b_src, dst_ap, b_dst, sx, b_sx):
            p.add("vector", lambda e: e.tensor_scalar(
                t32[:, :].bitcast(F32), src_ap, sx, CM, op0=ALU.mult, op1=ALU.add),
                reads=[b_src, b_sx], writes=[b_t32])
            p.add("vector", lambda e: e.tensor_scalar(
                nn32[:, :], t32[:, :].bitcast(F32), -CM, None, op0=ALU.add),
                reads=[b_t32], writes=[b_nn])
            p.add("vector", lambda e: e.scalar_tensor_tensor(
                ee32[:, :], src_ap, sx, nn32[:, :], op0=ALU.mult, op1=ALU.subtract),
                reads=[b_src, b_sx, b_nn], writes=[b_ee])
            p.add("vector", lambda e: e.scalar_tensor_tensor(
                qq32[:, :], ee32[:, :], 0.5, nn32[:, :], op0=ALU.is_gt, op1=ALU.add),
                reads=[b_ee, b_nn], writes=[b_qq])
            p.add("vector", lambda e: e.scalar_tensor_tensor(
                dst_ap, ee32[:, :], -0.5, qq32[:, :], op0=ALU.is_lt, op1=ALU.subtract),
                reads=[b_ee, b_qq], writes=[b_dst])

        def round5_2(src_ap, b_srcs, dst_ap, b_dst, sx, b_sx, eng12="vector"):
            p.add(eng12, lambda e: e.tensor_scalar(
                t32[:, :].bitcast(F32), src_ap, sx, CM, op0=ALU.mult, op1=ALU.add),
                reads=b_srcs + [b_sx], writes=[b_t32])
            p.add(eng12, lambda e: e.tensor_scalar(
                nn32[:, :], t32[:, :].bitcast(F32), -CM, None, op0=ALU.add),
                reads=[b_t32], writes=[b_nn])
            p.add("vector", lambda e: e.scalar_tensor_tensor(
                ee32[:, :], src_ap, sx, nn32[:, :], op0=ALU.mult, op1=ALU.subtract),
                reads=b_srcs + [b_sx, b_nn], writes=[b_ee])
            p.add("vector", lambda e: e.scalar_tensor_tensor(
                qq32[:, :], ee32[:, :], 0.5, nn32[:, :], op0=ALU.is_gt, op1=ALU.add),
                reads=[b_ee, b_nn], writes=[b_qq])
            p.add("vector", lambda e: e.scalar_tensor_tensor(
                dst_ap, ee32[:, :], -0.5, qq32[:, :], op0=ALU.is_lt, op1=ALU.subtract),
                reads=[b_ee, b_qq], writes=[b_dst])

        def actscale(src_small, b_src, st, b_st, sx, b_sx):
            p.add("vector", lambda e: e.tensor_scalar(
                st[1], src_small, 1e-5, None, op0=ALU.max),
                reads=[b_src], writes=[b_st[1]])
            p.add("vector", lambda e: e.reciprocal(st[1], st[1]),
                  reads=[b_st[1]], writes=[b_st[1]])
            p.add("vector", lambda e: e.tensor_scalar(
                sx, st[1], 127.0, None, op0=ALU.mult),
                reads=[b_st[1]], writes=[b_sx])

        passA = []
        for wsrc, rows, cols, base in ((wg_d, NI, D, 0), (wu_d, NI, D, 32),
                                       (wd_d, D, NI, 64)):
            ncc = cols // 1024
            for i in range(rows // 128 * ncc):
                r, cb = divmod(i, ncc)
                passA.append((wsrc, r, cb, base + i))

        _pa_d = [0]
        _pa_c = [0]

        def emit_passA_dma():
            if _pa_d[0] >= len(passA):
                return
            j = _pa_d[0]
            _pa_d[0] += 1
            wsrc, r, cb, slot = passA[j]
            ab = j % 3
            p.add("scalar", lambda e, wsrc=wsrc, r=r, cb=cb, ab=ab: e.dma_start(
                out=wstA[ab][:, :], in_=wsrc[r * 128:(r + 1) * 128, cb * 1024:(cb + 1) * 1024]),
                writes=[b_wstA[ab]], dma=True)

        def emit_passA_abs():
            if _pa_c[0] >= _pa_d[0] or _pa_c[0] >= len(passA):
                return
            j = _pa_c[0]
            _pa_c[0] += 1
            wsrc, r, cb, slot = passA[j]
            ab = j % 3
            p.add("scalar", lambda e, slot=slot, ab=ab: e.activation(
                wstA[ab][:, :], wstA[ab][:, :], AF.Abs, accum_out=accA_sl(slot)),
                reads=[b_wstA[ab]], writes=[b_wstA[ab], b_accA])

        def emit_xnq_out(tc2):
            bq = tc2 % 2
            rq2 = b_p1r[bq]
            rs2 = b_p1r[2 + bq]
            tb = nb()
            for dc in range(NDC):
                p.add("tensor", lambda e, dc=dc, bq=bq, tb=tb: e.transpose(
                    BKap(tb)[:, dc * 128:(dc + 1) * 128],
                    p1b[:, bq * 1024 + dc * 128: bq * 1024 + (dc + 1) * 128],
                    ident[:]),
                    reads=[rq2, b_ident], writes=[b_psTh[tb]])
            p.add("scalar", lambda e, bq=bq, tb=tb: e.activation(
                p1b[:, (2 + bq) * 1024:(3 + bq) * 1024], BKap(tb)[:, :],
                AF.Identity, bias=0.0, scale=1.0),
                reads=[b_psTh[tb]], writes=[rs2])
            p.add("sync", lambda e, tc2=tc2, bq=bq: e.dma_start(
                out=xnqT_d.rearrange("(dc pp) t -> pp dc t", pp=128)[:, :, tc2 * 128:(tc2 + 1) * 128],
                in_=p1b[:, (2 + bq) * 1024:(3 + bq) * 1024].rearrange("pp (dc t) -> pp dc t", t=128)),
                reads=[rs2], writes=[b_d["xnqT"]], dma=True)

        def emit_kv(tc2):
            for dc in range(NDC):
                p.add("tensor", lambda e, dc=dc, tc2=tc2: e.matmul(
                    bank(0)[0:LR, 0:128], wkvT[:, dc * LR:(dc + 1) * LR],
                    xnT_sl(dc, tc2 * 128, 128),
                    start=(dc == 0), stop=(dc == NDC - 1)),
                    reads=[b_wkvT, b_arA], writes=[b_bank[0]])
            p.add("scalar", lambda e, tc2=tc2: e.activation(
                kvT[:, tc2 * 128:(tc2 + 1) * 128], bank(0)[0:LR, 0:128],
                AF.Identity, bias=0.0, scale=1.0),
                reads=[b_bank[0]], writes=[b_kvT])

        pre_q = [(oc_, ts__) for ts__ in range(8) for oc_ in range(2)]
        pre_k = [(oc_, ts__) for ts__ in range(8) for oc_ in range(2)]
        pre_v = list(range(NTC))
        for _ in range(3):
            emit_passA_dma()
        for _ in range(3):
            emit_prep_dma()
        for tc in range(NTC):
            bb = tc % 2
            xn32 = xn32p[bb]
            b_xn = b_xn32[bb]
            st, b_st = stp[bb], b_stp[bb]
            sxv, b_sxv = sxv2[bb], b_sxv2[bb]
            rq = b_p1r[bb]
            rs = b_p1r[2 + bb]
            if tc == 20:
                emit_wqst(0)
                emit_vmemset(0)
            if tc >= 21:
                for _ in range(2):
                    if pre_q and 4 * pre_q[0][1] + 3 < tc:
                        oc_, ts__ = pre_q.pop(0)
                        emit_qgrp(0, oc_, ts__)
                for _ in range(2):
                    if pre_k and 4 * pre_k[0][1] + 4 < tc:
                        oc_, ts__ = pre_k.pop(0)
                        emit_kgrp(0, oc_, ts__, first=(oc_ == 0 and ts__ == 0))
                for _ in range(4):
                    if pre_v and pre_v[0] + 1 < tc:
                        emit_vchunk(0, pre_v.pop(0))
            p.add("sync", lambda e, tc=tc, bb=bb: e.dma_start(
                out=xst[bb][:, :], in_=x_d[tc * 128:(tc + 1) * 128, :]),
                writes=[b_xst[bb]], dma=True)
            p.add("vector", lambda e, bb=bb, st=st: e.reduce_sum(
                st[0], xst[bb][:, :], axis=AX.X),
                reads=[b_xst[bb]], writes=[b_st[0]])
            p.add("scalar", lambda e, bb=bb, xn32=xn32, st=st: e.activation(
                xn32[:, :], xst[bb][:, :], AF.Square, accum_out=st[1]),
                reads=[b_xst[bb]], writes=[b_xn, b_st[1]])
            p.add("vector", lambda e, st=st: e.tensor_scalar(
                st[0], st[0], 1.0 / D, None, op0=ALU.mult),
                reads=[b_st[0]], writes=[b_st[0]])
            p.add("vector", lambda e, st=st: e.tensor_scalar(
                st[1], st[1], 1.0 / D, None, op0=ALU.mult),
                reads=[b_st[1]], writes=[b_st[1]])
            p.add("vector", lambda e, st=st: e.scalar_tensor_tensor(
                st[2], st[0], st[0], st[1], op0=ALU.mult, op1=ALU.subtract),
                reads=[b_st[0], b_st[1]], writes=[b_st[2]])
            p.add("scalar", lambda e, st=st: e.activation(
                st[3], st[2], AF.Sqrt, bias=epsv, scale=-1.0),
                reads=[b_st[2], b_eps], writes=[b_st[3]])
            p.add("vector", lambda e, st=st: e.reciprocal(st[3], st[3]),
                  reads=[b_st[3]], writes=[b_st[3]])
            p.add("vector", lambda e, st=st: e.tensor_scalar(
                st[2], st[3], st[0], -1.0, op0=ALU.mult, op1=ALU.mult),
                reads=[b_st[3], b_st[0]], writes=[b_st[2]])
            p.add("scalar", lambda e, bb=bb, st=st: e.activation(
                p1b[:, (2 + bb) * 1024:(3 + bb) * 1024], xst[bb][:, :],
                AF.Identity, bias=st[2], scale=st[3]),
                reads=[b_xst[bb], b_st[2], b_st[3]], writes=[rs])
            p.add("scalar", lambda e, bb=bb, xn32=xn32, st=st: e.activation(
                xn32[:, :], xst[bb][:, :], AF.Identity, bias=st[2], scale=st[3]),
                reads=[b_xst[bb], b_st[2], b_st[3]], writes=[b_xn])
            for _ in range(3):
                emit_passA_abs()
            for _ in range(3):
                emit_prep_comp()
            if tc >= 1:
                emit_kv(tc - 1)
            tb = nb()
            for dc in range(NDC):
                p.add("tensor", lambda e, dc=dc, bb=bb, tb=tb: e.transpose(
                    BKap(tb)[:, dc * 128:(dc + 1) * 128],
                    p1b[:, (2 + bb) * 1024 + dc * 128: (2 + bb) * 1024 + (dc + 1) * 128],
                    ident[:]),
                    reads=[rs, b_ident], writes=[b_psTh[tb]])
            p.add("vector", lambda e, tc=tc, tb=tb: e.tensor_copy(
                arA[:, :].rearrange("pp (dc t) -> pp dc t", t=T)[:, :, tc * 128:(tc + 1) * 128],
                BKap(tb)[:, :].rearrange("pp (dc t) -> pp dc t", t=128)),
                reads=[b_psTh[tb]], writes=[b_arA])
            if tc >= 1:
                emit_xnq_out(tc - 1)
            p.add("vector", lambda e, xn32=xn32, st=st: e.tensor_reduce(
                st[0], xn32[:, :], axis=AX.X, op=ALU.max, apply_absolute_value=True),
                reads=[b_xn], writes=[b_st[0]])
            p.add("vector", lambda e, tc=tc, st=st: e.tensor_scalar(
                rsx[:, tc:tc + 1], st[0], 1e-5, 1.0 / 127.0, op0=ALU.max, op1=ALU.mult),
                reads=[b_st[0]], writes=[b_rsx])
            actscale(st[0], b_st[0], st, b_st, sxv, b_sxv)
            round5(xn32[:, :], b_xn, p1b[:, bb * 1024:(bb + 1) * 1024], rq, sxv, b_sxv)
            for _ in range(3):
                emit_passA_dma()
            for _ in range(3):
                emit_prep_dma()
        emit_xnq_out(NTC - 1)
        emit_kv(NTC - 1)
        while _pa_c[0] < len(passA):
            emit_passA_abs()
            emit_passA_dma()
        while _pc[0] < len(prep_tiles):
            emit_prep_comp()
            emit_prep_dma()
        for oc_, ts__ in pre_q:
            emit_qgrp(0, oc_, ts__)
        for oc_, ts__ in pre_k:
            emit_kgrp(0, oc_, ts__, first=(oc_ == 0 and ts__ == 0))
        for tk_ in pre_v:
            emit_vchunk(0, tk_)
        if debug:
            p.add_final_wait(p.add("sync", lambda e: e.dma_start(
                out=xnT_dbg.rearrange("(dc pp) t -> pp dc t", pp=128),
                in_=arA[:, :].rearrange("pp (dc t) -> pp dc t", t=T)),
                reads=[b_arA], dma=True))
            p.add_final_wait(p.add("sync", lambda e: e.dma_start(
                out=kvT_dbg, in_=kvT[:, :]), reads=[b_kvT], dma=True))

        for bslot, (rows, cols, base) in enumerate(((NI, D, 0), (NI, D, 32), (D, NI, 64))):
            st = stp[0]
            p.add("vector", lambda e, base=base, st=st: e.reduce_sum(
                st[0], smf[:, 112 + base: 112 + base + 32], axis=AX.X),
                reads=[b_accA], writes=[b_stp[0][0]])
            p.add("tensor", lambda e, st=st: e.matmul(
                bank(5)[0:1, 0:1], ones_t[:, 0:1], st[0], start=True, stop=True),
                reads=[b_ones, b_stp[0][0]], writes=[b_bank[5]])
            p.add("vector", lambda e, rows=rows, cols=cols: e.tensor_scalar(
                mu11[0:1, :], bank(5)[0:1, 0:1], 1.0 / (rows * cols), 1e-5,
                op0=ALU.mult, op1=ALU.max),
                reads=[b_bank[5]], writes=[b_mu])
            p.add("tensor", lambda e: e.matmul(
                bank(5)[:, 0:1], ones_t[0:1, :], mu11[0:1, :], start=True, stop=True),
                reads=[b_ones, b_mu], writes=[b_bank[5]])
            p.add("vector", lambda e, bslot=bslot: e.tensor_copy(
                bvec[:, bslot:bslot + 1], bank(5)[:, 0:1]),
                reads=[b_bank[5]], writes=[b_bvec])
            p.add("vector", lambda e, bslot=bslot: e.reciprocal(
                scw3[bslot], bvec[:, bslot:bslot + 1]),
                reads=[b_bvec], writes=[b_scw3[bslot]])

        passB = []
        for bslot, (wsrc, rows, cols, dst, bd) in enumerate((
                (wg_d, NI, D, wgT_d, b_d["wgT"]), (wu_d, NI, D, wuT_d, b_d["wuT"]),
                (wd_d, D, NI, wdT_d, b_d["wdT"]))):
            ncc = cols // 1024
            for i in range(rows // 128 * ncc):
                r, cb = divmod(i, ncc)
                passB.append((wsrc, r, cb, dst, bd, bslot))

        def emit_passB1(j):
            wsrc, r, cb, dst, bd, bslot = passB[j]
            bb = j % 2
            ro = b_p1r[bb]
            p.add("gpsimd", lambda e, r=r, cb=cb, bb=bb, wsrc=wsrc: e.dma_start(
                out=xst[bb][:, :], in_=wsrc[r * 128:(r + 1) * 128, cb * 1024:(cb + 1) * 1024]),
                writes=[b_xst[bb]], dma=True)
            p.add("vector", lambda e, bb=bb, bslot=bslot: e.tensor_scalar(
                nn32[:, :], xst[bb][:, :], scw3[bslot], -0.5, op0=ALU.mult, op1=ALU.is_lt),
                reads=[b_xst[bb], b_scw3[bslot]], writes=[b_nn])
            p.add("vector", lambda e, bb=bb, bslot=bslot: e.tensor_scalar(
                ee32[:, :], xst[bb][:, :], scw3[bslot], 0.5, op0=ALU.mult, op1=ALU.is_gt),
                reads=[b_xst[bb], b_scw3[bslot]], writes=[b_ee])
            p.add("vector", lambda e, bb=bb: e.tensor_tensor(
                p1b[:, bb * 1024:(bb + 1) * 1024], nn32[:, :], ee32[:, :],
                op=ALU.subtract),
                reads=[b_nn, b_ee], writes=[ro])

        trB_guard = b_psTh[1].handover()
        for h in range(NH):
            if h > 0:
                emit_wqst(h)
                for oc in range(2):
                    for ts_ in range(8):
                        emit_qgrp(h, oc, ts_)
                for oc in range(2):
                    for ts_ in range(8):
                        emit_kgrp(h, oc, ts_)
            if h == NH - 1:
                arA_g = b_arA.handover()
                for isl_ in range(8):
                    p.add("sync", lambda e, isl_=isl_: e.dma_start(
                        out=arA[:, :].rearrange("pp (dc i) -> pp dc i", i=NI)[:, :, isl_ * 512:(isl_ + 1) * 512],
                        in_=wgT_d.rearrange("(dc pp) i -> pp dc i", pp=128)[:, :, isl_ * 512:(isl_ + 1) * 512]),
                        reads=[b_d["wgT"]], writes=[b_wg8[isl_]],
                        extra=arA_g, dma=True)
            if h > 0:
                emit_vmemset(h)
                for tk in range(NTC):
                    emit_vchunk(h, tk)
            def emit_qk(bi, c):
                g = c % 3
                for d2 in range(2):
                    p.add("tensor", lambda e, bi=bi, c=c, g=g, d2=d2: e.matmul(
                        bank(g), kT[:, d2 * T + c * 128: d2 * T + (c + 1) * 128],
                        qT[:, d2 * T + bi * 512: d2 * T + (bi + 1) * 512],
                        start=(d2 == 0), stop=(d2 == 1)),
                        reads=[b_kT, b_qT], writes=[b_bank[g]])
                p.add("scalar", lambda e, g=g, c=c: e.activation(
                    pT[c % 3][:, :], bank(g), AF.Exp, bias=0.0, scale=ISCL),
                    reads=[b_bank[g]], writes=[b_pT[c % 3]])

            def emit_pv(c):
                for sub in range(4):
                    p.add("tensor", lambda e, c=c, sub=sub: e.matmul(
                        pv_ps(sub, 0, 257), pT[c % 3][:, sub * 128:(sub + 1) * 128],
                        vA[:, c * 257:(c + 1) * 257],
                        start=(c == 0), stop=(c == NTC - 1)),
                        reads=[b_pT[c % 3], b_vA], writes=[b_bank[3 + sub]],
                        extra=(trB_guard if h == 0 and c == 0 and sub == 3 else ()))

            def finalize_recips():
                for sub in range(4):
                    p.add("vector", lambda e, sub=sub: e.reciprocal(
                        repl[:, sub:sub + 1], pv_ps(sub, 256, 257)),
                        reads=[b_bank[3 + sub]], writes=[b_repl])

            def finalize_atto(bi):
                for sub in range(4):
                    p.add("scalar", lambda e, sub=sub: e.activation(
                        att_o[:, sub * 256:(sub + 1) * 256], pv_ps(sub, 0, 256),
                        AF.Identity, scale=repl[:, sub:sub + 1]),
                        reads=[b_bank[3 + sub], b_repl], writes=[b_atto],
                        extra=(b_wkvT.handover()
                               if h == 0 and bi == 0 and sub == 0 else ()))

            def trans_tasks(bi):
                pb = 0 if (h == NH - 1 and bi == 7) else bi % 2
                for half in range(2):
                    for sub in range(4):
                        yield lambda sub=sub, half=half: p.add(
                            "tensor", lambda e, sub=sub, half=half: e.transpose(
                                psT0[:, (half * 4 + sub) * 128:(half * 4 + sub + 1) * 128],
                                att_o[:, sub * 256 + half * 128: sub * 256 + half * 128 + 128],
                                ident[:]),
                            reads=[b_atto, b_ident], writes=[b_psTh[0]])

                def cpdma(pb=pb, bi=bi):
                    p.add("vector", lambda e, pb=pb: e.tensor_copy(
                        aoTh[pb][:, :], psT0[:, 0:1024]),
                        reads=[b_psTh[0]], writes=[b_aoTh[pb]])
                    p.add("sync", lambda e, h=h, bi=bi, pb=pb: e.dma_start(
                        out=aoT_d[h * 256:(h + 1) * 256, bi * 512:(bi + 1) * 512]
                        .rearrange("(ch pp) t -> pp ch t", pp=128),
                        in_=aoTh[pb][:, :].rearrange("pp (ch t) -> pp ch t", t=512)),
                        reads=[b_aoTh[pb]], writes=[b_d["aoT"]], dma=True)
                yield cpdma
                gi = h * 8 + bi
                for j in range(gi * 3 // 2, min((gi + 1) * 3 // 2, len(passB))):
                    for dc in range(8):
                        yield lambda j=j, dc=dc: _passB2_tr(j, dc)
                    yield lambda j=j: _passB2_fin(j)

            def _passB2_tr(j, dc):
                wsrc, r, cb, dst, bd, bslot = passB[j]
                bb = j % 2
                ro = b_p1r[bb]
                p.add("tensor", lambda e, dc=dc, bb=bb: e.transpose(
                    psT0[:, dc * 128:(dc + 1) * 128],
                    p1b[:, bb * 1024 + dc * 128: bb * 1024 + (dc + 1) * 128], ident[:]),
                    reads=[ro, b_ident], writes=[b_psTh[0]])

            def _passB2_fin(j):
                wsrc, r, cb, dst, bd, bslot = passB[j]
                bb = j % 2
                rsb = b_p1r[2 + bb]
                p.add("vector", lambda e, bb=bb: e.tensor_copy(
                    p1b[:, (2 + bb) * 1024:(3 + bb) * 1024], psT0[:, 0:1024]),
                    reads=[b_psTh[0]], writes=[rsb])
                p.add("gpsimd", lambda e, r=r, cb=cb, dst=dst, bb=bb: e.dma_start(
                    out=dst.rearrange("(cc pp) o -> pp cc o", pp=128)[:, cb * 8:(cb + 1) * 8, r * 128:(r + 1) * 128],
                    in_=p1b[:, (2 + bb) * 1024:(3 + bb) * 1024].rearrange("pp (cc t) -> pp cc t", t=128)),
                    reads=[rsb], writes=[bd], dma=True)

            for bi in range(8):
                tq = list(trans_tasks(bi - 1)) if bi >= 1 else []
                ti = 0
                for c in range(NTC):
                    emit_qk(bi, c)
                    if c == 1 and bi >= 1:
                        finalize_atto(bi - 1)
                    if c >= 4:
                        for _ in range(2):
                            if ti < len(tq):
                                tq[ti]()
                                ti += 1
                    if c >= 2:
                        emit_pv(c - 2)
                while ti < len(tq):
                    tq[ti]()
                    ti += 1
                emit_pv(NTC - 2)
                emit_pv(NTC - 1)
                finalize_recips()
                if bi == 7:
                    finalize_atto(7)
                gi0 = h * 8 + bi
                for j in range(gi0 * 3 // 2, min((gi0 + 1) * 3 // 2, len(passB))):
                    emit_passB1(j)
            for fn in trans_tasks(7):
                fn()
        tok_wu = None
        b_wu8h = [[Buf(), Buf()] for _ in range(8)]
        xnq_guard = b_kvT.handover() + b_aoTh[1].handover()
        guard_wu = (b_qT.handover() + b_kT.handover() + b_vA.handover() +
                    b_wqst[0].handover() + b_wqst[1].handover() +
                    b_pT[0].handover() + b_pT[1].handover() + b_pT[2].handover() +
                    b_kvT.handover() + b_aoTh[0].handover() + b_aoTh[1].handover() +
                    b_atto.handover())
        def emit_wu_loads():
            tok = None
            for isl in range(8):
                tok = p.add("scalar" if isl < 4 else "sync",
                            lambda e, isl=isl: e.dma_start(
                    out=arB[:, 0:32768].rearrange("pp (dc i) -> pp dc i", i=NI)[:, :, isl * 512:(isl + 1) * 512],
                    in_=wuT_d.rearrange("(dc pp) i -> pp dc i", pp=128)[:, :, isl * 512:(isl + 1) * 512]),
                    reads=[b_d["wuT"]], writes=[b_wu8h[isl][0]],
                    extra=guard_wu, dma=True)
            return tok
        h32_guard = (b_xst[0].handover() + b_xst[1].handover() +
                     b_xn32[0].handover() + b_xn32[1].handover() +
                     b_wstA[0].handover() + b_wstA[1].handover() +
                     b_wstA[2].handover())

        bank6_guard = b_bank[6].handover()
        _p3b1 = [False]

        def emit_trans_group(tc, g):
            tb = g % 2
            ex = ()
            if tb == 1 and not _p3b1[0]:
                _p3b1[0] = True
                ex = bank6_guard
            for dc in range(8):
                ic = g * 8 + dc
                p.add("tensor", lambda e, ic=ic, dc=dc, tb=tb: e.transpose(
                    BKap(tb)[:, dc * 128:(dc + 1) * 128],
                    hq_neg[:, ic * 128:(ic + 1) * 128], ident[:]),
                    reads=[b_hqneg[g], b_ident], writes=[b_psTh[tb]],
                    extra=(ex if dc == 0 else ()))
            p.add("scalar", lambda e, g=g, tb=tb: e.activation(
                p1b[:, g * 1024:(g + 1) * 1024], BKap(tb)[:, :],
                AF.Identity, bias=0.0, scale=1.0),
                reads=[b_psTh[tb]], writes=[b_p1r[g]])

        def emit_hq_dma(tc):
            p.add("sync", lambda e, tc=tc: e.dma_start(
                out=hqT_d.rearrange("(ic pp) t -> pp ic t", pp=128)[:, :, tc * 128:(tc + 1) * 128],
                in_=p1b[:, :].rearrange("pp (ic t) -> pp ic t", t=128)),
                reads=[b_p1r[0], b_p1r[1], b_p1r[2], b_p1r[3]],
                writes=[b_d["hqT"]], dma=True)

        def xnq_load(tc):
            bb = tc % 2
            p.add("sync", lambda e, tc=tc, bb=bb: e.dma_start(
                out=xnq_st[bb][:, :].rearrange("pp (dc t) -> pp dc t", t=128),
                in_=xnqT_d.rearrange("(dc pp) t -> pp dc t", pp=128)[:, :, tc * 128:(tc + 1) * 128]),
                reads=[b_d["xnqT"]], writes=[b_xnqst[bb]],
                extra=(xnq_guard if tc < 2 else ()), dma=True)

        p.add("vector", lambda e: e.tensor_scalar(
            sgall[:, :], rsx[:, :], bvec[:, 0:1], None, op0=ALU.mult),
            reads=[b_rsx, b_bvec], writes=[b_sgall])
        p.add("vector", lambda e: e.tensor_scalar(
            suall[:, :], rsx[:, :], bvec[:, 1:2], None, op0=ALU.mult),
            reads=[b_rsx, b_bvec], writes=[b_suall])
        xnq_load(0)
        xnq_load(1)
        tok_wu = emit_wu_loads()
        for b_ in b_hqneg:
            b_.w = tok_wu
        for tc in range(NTC):
            bb = tc % 2
            h32 = h32p[bb]
            b_hs = b_h32[bb]
            st, b_st = stp[bb], b_stp[bb]
            sxv, b_sxv = sxv2[bb], b_sxv2[bb]
            for isl in range(8):
                gg, gu = isl % 2, 2 + isl % 2
                for dc in range(NDC):
                    p.add("tensor", lambda e, isl=isl, dc=dc, gg=gg, bb=bb: e.matmul(
                        bank(gg), xnq_st[bb][:, dc * 128:(dc + 1) * 128],
                        wgqT_sl(dc, isl * 512, 512),
                        start=(dc == 0), stop=(dc == NDC - 1)),
                        reads=[b_xnqst[bb], b_wg8[isl]], writes=[b_bank[gg]])
                for dc in range(NDC):
                    p.add("tensor", lambda e, isl=isl, dc=dc, gu=gu, bb=bb: e.matmul(
                        bank(gu), xnq_st[bb][:, dc * 128:(dc + 1) * 128],
                        wuqT_sl(dc, isl * 512, 512),
                        start=(dc == 0), stop=(dc == NDC - 1)),
                        reads=[b_xnqst[bb], b_wu8h[isl][0]],
                        writes=[b_bank[gu]])
                p.add("scalar", lambda e, isl=isl, gu=gu, h32=h32, tc=tc: e.activation(
                    h32[:, isl * 512:(isl + 1) * 512], bank(gu), AF.Identity,
                    scale=suall[:, tc:tc + 1]),
                    reads=[b_bank[gu], b_suall], writes=[b_hs[isl]],
                    extra=(h32_guard if tc < 2 and isl == 0 else ()))
                p.add("scalar", lambda e, gg=gg, isl=isl, tc=tc: e.activation(
                    sil32[isl % 4][:, :], bank(gg), AF.Silu, scale=sgall[:, tc:tc + 1]),
                    reads=[b_bank[gg], b_sgall], writes=[b_sil[isl % 4]])
                p.add("gpsimd", lambda e, isl=isl, h32=h32: e.tensor_tensor(
                    h32[:, isl * 512:(isl + 1) * 512], sil32[isl % 4][:, :],
                    h32[:, isl * 512:(isl + 1) * 512], op=ALU.mult),
                    reads=[b_sil[isl % 4]], writes=[b_hs[isl]])
                p.add("vector", lambda e, isl=isl, h32=h32, bb=bb: e.tensor_reduce(
                    acc32[:, bb * 8 + isl: bb * 8 + isl + 1],
                    h32[:, isl * 512:(isl + 1) * 512], axis=AX.X, op=ALU.max,
                    apply_absolute_value=True),
                    reads=[b_hs[isl]], writes=[b_acc])
            p.add("vector", lambda e, st=st, bb=bb: e.tensor_reduce(
                st[0], acc32[:, bb * 8: bb * 8 + 8], axis=AX.X, op=ALU.max),
                reads=[b_acc], writes=[b_st[0]])
            p.add("vector", lambda e, tc=tc, st=st: e.tensor_scalar(
                rsh[:, tc:tc + 1], st[0], 1e-5, 1.0 / 127.0, op0=ALU.max, op1=ALU.mult),
                reads=[b_st[0]], writes=[b_rsh])
            if tc + 2 < NTC:
                xnq_load(tc + 2)
            if tc >= 1:
                for g_ in range(4):
                    emit_trans_group(tc - 1, g_)
                emit_hq_dma(tc - 1)
            actscale(st[0], b_st[0], st, b_st, sxv, b_sxv)
            for sl in range(4):
                round5_2(h32[:, sl * 1024:(sl + 1) * 1024],
                         [b_hs[2 * sl], b_hs[2 * sl + 1]],
                         hq_neg[:, sl * 1024:(sl + 1) * 1024], b_hqneg[sl], sxv, b_sxv)
        for g in range(4):
            emit_trans_group(NTC - 1, g)
        emit_hq_dma(NTC - 1)

        b_wd2 = [[Buf() for _ in range(4)], [Buf() for _ in range(4)]]
        b_wo2 = [Buf(), Buf()]
        guard_wd = (b_xnqst[0].handover() + b_xnqst[1].handover() +
                    [d for b_ in b_hqneg for d in b_.handover()] +
                    [d for pr in b_wu8h for bb_ in pr for d in bb_.handover()])
        guard_wo = (b_arA.handover() +
                    [d for bb_ in b_wg8 for d in bb_.handover()])
        arA_guard = guard_wo
        h32_guard2 = [d for par in range(2) for bb_ in b_h32[par] for d in bb_.handover()]
        qq_guard = b_qq.handover()

        def emit_x_load(tc, bb, eng):
            p.add(eng, lambda e, tc=tc, bb=bb: e.dma_start(
                out=x4st[bb][:, :], in_=x_d[tc * 128:(tc + 1) * 128, :]),
                writes=[b_x4[bb]],
                extra=((h32_guard2 + b_xn32[bb].handover()) if tc < 2 else ()),
                dma=True)

        def emit_hqao_load(tc, bb):
            p.add("sync", lambda e, tc=tc, bb=bb: e.dma_start(
                out=hq_st[bb][:, :].rearrange("pp (ic t) -> pp ic t", t=128),
                in_=hqT_d.rearrange("(ic pp) t -> pp ic t", pp=128)[:, :, tc * 128:(tc + 1) * 128]),
                reads=[b_d["hqT"]], writes=[b_hqst[bb]],
                extra=(arA_guard if tc < 2 else ()), dma=True)
            p.add("sync", lambda e, tc=tc, bb=bb: e.dma_start(
                out=ao_st[bb][:, :].rearrange("pp (oc t) -> pp oc t", t=128),
                in_=aoT_d.rearrange("(oc pp) t -> pp oc t", pp=128)[:, :, tc * 128:(tc + 1) * 128]),
                reads=[b_d["aoT"]], writes=[b_aost[bb]],
                extra=(arA_guard if tc < 2 else ()), dma=True)

        for n_ in range(2):
            p.add("scalar", lambda e, n_=n_: e.dma_start(
                out=arA[:, 0:16384].rearrange("pp (oc n) -> pp oc n", n=D)[:, :, n_ * 512:(n_ + 1) * 512],
                in_=woT_d.rearrange("(oc pp) n -> pp oc n", pp=128)[:, :, n_ * 512:(n_ + 1) * 512]),
                reads=[b_d["woT"]], writes=[b_wo2[n_]],
                extra=guard_wo, dma=True)
        for n_ in range(2):
            for q_ in range(4):
                p.add("gpsimd" if n_ == 0 else "scalar",
                      lambda e, n_=n_, q_=q_: e.dma_start(
                    out=arB[:, 0:32768].rearrange("pp (ic n) -> pp ic n", n=D)[:, q_ * 8:(q_ + 1) * 8, n_ * 512:(n_ + 1) * 512],
                    in_=wdT_d.rearrange("(ic pp) n -> pp ic n", pp=128)[:, q_ * 8:(q_ + 1) * 8, n_ * 512:(n_ + 1) * 512]),
                    reads=[b_d["wdT"]], writes=[b_wd2[n_][q_]],
                    extra=guard_wd, dma=True)
        p.add("vector", lambda e: e.tensor_scalar(
            sdall[:, :], rsh[:, :], bvec[:, 2:3], None, op0=ALU.mult),
            reads=[b_rsh, b_bvec], writes=[b_sdall], extra=b_accA.handover())
        bkk = lambda i: bank(i) if i < 7 else psT0[:, 0:1024].bitcast(F32)
        b_bkl = lambda i: [b_bank[i]] if i < 7 else [b_psTh[0]]
        trB_guard2 = b_psTh[1].handover()
        dfin = None
        for tc in range(NTC):
            bb = tc % 2
            dbk = (0, 1) if bb == 0 else (4, 5)
            wbk = (2, 3) if bb == 0 else (6, 7)
            emit_hqao_load(tc, bb)
            emit_x_load(tc, bb, "sync" if tc < 2 else "scalar")
            for n_ in range(2):
                for oc in range(16):
                    p.add("tensor", lambda e, oc=oc, n_=n_, bb=bb, wbk=wbk: e.matmul(
                        bkk(wbk[n_]), ao_st[bb][:, oc * 128:(oc + 1) * 128],
                        woT_sl(oc, n_ * 512, 512),
                        start=(oc == 0), stop=(oc == 15)),
                        reads=[b_aost[bb], b_wo2[n_]], writes=b_bkl(wbk[n_]),
                        extra=(trB_guard2 if tc == 1 and n_ == 0 and oc == 0 else ()))
                p.add("vector", lambda e, n_=n_, bb=bb, wbk=wbk: e.scalar_tensor_tensor(
                    out32[bb][:, n_ * 512:(n_ + 1) * 512], bkk(wbk[n_]), 1.0,
                    x4st[bb][:, n_ * 512:(n_ + 1) * 512],
                    op0=ALU.mult, op1=ALU.add),
                    reads=b_bkl(wbk[n_]) + [b_x4[bb]], writes=[b_xst[bb]],
                    extra=(h32_guard2 if tc < 2 and n_ == 0 else ()))
            for n_ in range(2):
                for ic in range(32):
                    p.add("tensor", lambda e, ic=ic, n_=n_, bb=bb, dbk=dbk: e.matmul(
                        bank(dbk[n_]), hq_st[bb][:, ic * 128:(ic + 1) * 128],
                        wdqT_sl(ic, n_ * 512, 512),
                        start=(ic == 0), stop=(ic == 31)),
                        reads=[b_hqst[bb], b_wd2[n_][ic // 8]],
                        writes=[b_bank[dbk[n_]]])
            p.add("scalar", lambda e, tc=tc, bb=bb, dbk=dbk: e.activation(
                t1_32[:, :], psF[:, dbk[0] * 512:(dbk[0] + 2) * 512],
                AF.Identity, scale=sdall[:, tc:tc + 1]),
                reads=[b_bank[dbk[0]], b_bank[dbk[1]], b_sdall], writes=[b_t1],
                extra=((qq_guard + h32_guard2) if tc == 0 else ()))
            p.add("vector", lambda e, bb=bb: e.tensor_tensor(
                out32[bb][:, :], out32[bb][:, :], t1_32[:, :], op=ALU.add),
                reads=[b_xst[bb], b_t1], writes=[b_xst[bb]])
            dfin = p.add("scalar", lambda e, tc=tc, bb=bb: e.dma_start(
                out=y_d[tc * 128:(tc + 1) * 128, :], in_=out32[bb][:, :]),
                reads=[b_xst[bb]], dma=True)
        p.add_final_wait(dfin)
        if debug:
            p.add_final_wait(p.add("sync", lambda e: e.dma_start(
                out=sc_dbg, in_=smf[:, 0:128]),
                reads=[b_rsx, b_rsh, b_bvec], dma=True))

        blk = ctx.enter_context(nc.Block())
        p.emit(nc, blk, ctx)
    return nc


def kernel(L0_attn_x, L0_attn_Wq, L0_attn_Wkv_down, L0_attn_Wk_up,
           L0_attn_Wv_up, L0_attn_Wo, L0_ffn_w_gate, L0_ffn_w_up,
           L0_ffn_w_down):
    nc = build_nc()
    ident = np.eye(128, dtype=ml_dtypes.bfloat16)
    x = np.ascontiguousarray(np.asarray(L0_attn_x, dtype=np.float32))
    common = {
        "Wq": np.ascontiguousarray(np.asarray(L0_attn_Wq, dtype=np.float32)),
        "Wkv": np.ascontiguousarray(np.asarray(L0_attn_Wkv_down, dtype=np.float32)),
        "Wku": np.ascontiguousarray(np.asarray(L0_attn_Wk_up, dtype=np.float32)),
        "Wvu": np.ascontiguousarray(np.asarray(L0_attn_Wv_up, dtype=np.float32)),
        "Wo": np.ascontiguousarray(np.asarray(L0_attn_Wo, dtype=np.float32)),
        "Wg": np.ascontiguousarray(np.asarray(L0_ffn_w_gate, dtype=np.float32)),
        "Wu": np.ascontiguousarray(np.asarray(L0_ffn_w_up, dtype=np.float32)),
        "Wd": np.ascontiguousarray(np.asarray(L0_ffn_w_down, dtype=np.float32)),
        "ident": ident,
    }
    in_maps = [dict(common, x=x[c]) for c in range(8)]
    res = run_bass_kernel_spmd(nc, in_maps, core_ids=list(range(8)))
    return np.stack([np.asarray(r["y"], dtype=np.float32) for r in res.results],
                    axis=0)


# version 43
# speedup vs baseline: 1.3676x; 1.0024x over previous
import numpy as np
import ml_dtypes
import contextlib
import concourse.bass as bass
import concourse.mybir as mybir
from concourse.bass_utils import run_bass_kernel_spmd

F32 = mybir.dt.float32
BF16 = mybir.dt.bfloat16
I32 = mybir.dt.int32
AF = mybir.ActivationFunctionType
ALU = mybir.AluOpType
AX = mybir.AxisListType

T = 4096
D = 1024
NH = 8
DO = 2048
LR = 64
NI = 4096
NTC = T // 128
NDC = D // 128
CM = 12582912.0
ISCL = 0.0625

ENGINES = ("sync", "tensor", "vector", "scalar", "gpsimd")


class Buf:
    __slots__ = ("w", "r")

    def __init__(self):
        self.w = None
        self.r = []

    def handover(self):
        return ([self.w] if self.w is not None else []) + list(self.r)


class Prog:
    def __init__(self):
        self.ops = []
        self.final_waits = []

    def add(self, eng, fn, reads=(), writes=(), extra=(), dma=False):
        deps = []
        for b in writes:
            if b.w is not None:
                deps.append(b.w)
            deps.extend(b.r)
        for b in reads:
            if b.w is not None:
                deps.append(b.w)
        deps.extend(d for d in extra if d is not None)
        i = len(self.ops)
        assert all(d < i for d in deps)
        self.ops.append({"eng": eng, "fn": fn, "deps": sorted(set(deps)),
                         "dma": dma})
        for b in writes:
            b.w = i
            b.r = []
        for b in reads:
            b.r.append(i)
        return i

    def add_final_wait(self, dep):
        self.final_waits.append(dep)

    NSLOT = 12

    def emit(self, nc, block, ctx):
        ops = self.ops
        NS = self.NSLOT
        ekey = [None] * len(ops)
        qidx = {}
        for i, op in enumerate(ops):
            if op["dma"]:
                q = "q_" + op["eng"]
                j = qidx.get(q, 0)
                qidx[q] = j + 1
                ekey[i] = (q, j % NS)
            else:
                ekey[i] = op["eng"]
        for i, op in enumerate(ops):
            op["deps"] = [d for d in op["deps"]
                          if not (ekey[d] == "tensor" and ekey[i] == "tensor")]
        needed = set(i for i, op in enumerate(ops) if op["dma"])
        for op in ops:
            needed.update(op["deps"])
        needed.update(self.final_waits)
        sems = {}
        for e in ENGINES:
            sems[e] = ctx.enter_context(nc.semaphore("s_" + e))
            for s in range(NS):
                k = ("q_" + e, s)
                sems[k] = ctx.enter_context(nc.semaphore(f"sq_{e}_{s}"))
        val = [0] * len(ops)
        cnt = {k: 0 for k in sems}
        for i, op in enumerate(ops):
            if i in needed:
                cnt[ekey[i]] += 16 if op["dma"] else 1
                val[i] = cnt[ekey[i]]
        per_eng = {e: [] for e in ENGINES}
        for i, op in enumerate(ops):
            per_eng[op["eng"]].append(i)

        last_val = dict(cnt)

        def emit_engine(e, eng):
            maxw = {}

            def w(k, v):
                if v > 0 and maxw.get(k, 0) < v:
                    eng.wait_ge(sems[k], v)
                    maxw[k] = v

            for i in per_eng[e]:
                op = ops[i]
                if op["dma"]:
                    w(ekey[i], val[i] - 16)
                for d in op["deps"]:
                    assert val[d] > 0, (i, d)
                    w(ekey[d], val[d])
                inst = op["fn"](eng)
                if i in needed:
                    inst.then_inc(sems[ekey[i]], 16 if op["dma"] else 1)
            for d in self.final_waits:
                w(ekey[d], val[d])
            if e == "sync":
                for k, v in last_val.items():
                    if isinstance(k, tuple):
                        w(k, v)

        @block.sync
        def _(eng):
            emit_engine("sync", eng)

        @block.tensor
        def _(eng):
            emit_engine("tensor", eng)

        @block.vector
        def _(eng):
            emit_engine("vector", eng)

        @block.scalar
        def _(eng):
            emit_engine("scalar", eng)

        @block.gpsimd
        def _(eng):
            emit_engine("gpsimd", eng)


def build_nc(debug=False):
    nc = bass.Bass("TRN2", target_bir_lowering=False,
                   detect_race_conditions=False)

    x_d = nc.dram_tensor("x", [T, D], F32, kind="ExternalInput").ap()
    wq_d = nc.dram_tensor("Wq", [DO, D], F32, kind="ExternalInput").ap()
    wkv_d = nc.dram_tensor("Wkv", [LR, D], F32, kind="ExternalInput").ap()
    wku_d = nc.dram_tensor("Wku", [DO, LR], F32, kind="ExternalInput").ap()
    wvu_d = nc.dram_tensor("Wvu", [DO, LR], F32, kind="ExternalInput").ap()
    wo_d = nc.dram_tensor("Wo", [D, DO], F32, kind="ExternalInput").ap()
    wg_d = nc.dram_tensor("Wg", [NI, D], F32, kind="ExternalInput").ap()
    wu_d = nc.dram_tensor("Wu", [NI, D], F32, kind="ExternalInput").ap()
    wd_d = nc.dram_tensor("Wd", [D, NI], F32, kind="ExternalInput").ap()
    id_d = nc.dram_tensor("ident", [128, 128], BF16, kind="ExternalInput").ap()
    y_d = nc.dram_tensor("y", [T, D], F32, kind="ExternalOutput").ap()

    sk = "ExternalOutput" if debug else "Internal"
    xnqT_d = nc.dram_tensor("xnqT_d", [D, T], BF16, kind=sk).ap()
    aoT_d = nc.dram_tensor("aoT_d", [DO, T], BF16, kind=sk).ap()
    hqT_d = nc.dram_tensor("hqT_d", [NI, T], BF16, kind=sk).ap()
    wqT_d = nc.dram_tensor("wqT_d", [D, DO], BF16, kind=sk).ap()
    woT_d = nc.dram_tensor("woT_d", [DO, D], BF16, kind=sk).ap()
    wgT_d = nc.dram_tensor("wgT_d", [D, NI], BF16, kind=sk).ap()
    wuT_d = nc.dram_tensor("wuT_d", [D, NI], BF16, kind=sk).ap()
    wdT_d = nc.dram_tensor("wdT_d", [NI, D], BF16, kind=sk).ap()
    if debug:
        xnT_dbg = nc.dram_tensor("xnT_dbg", [D, T], BF16, kind="ExternalOutput").ap()
        kvT_dbg = nc.dram_tensor("kvT_dbg", [LR, T], BF16, kind="ExternalOutput").ap()
        sc_dbg = nc.dram_tensor("sc_dbg", [128, 128], F32, kind="ExternalOutput").ap()

    p = Prog()
    with contextlib.ExitStack() as ctx:
        sbuf = lambda n, s, dt: ctx.enter_context(nc.sbuf_tensor(n, s, dt))
        ident = sbuf("identsb", [128, 128], BF16)
        arA = sbuf("arA", [128, 32768], BF16)
        arB = sbuf("arB", [128, 38912], BF16)
        arC = sbuf("arC", [128, 10240], F32)
        rsc = sbuf("rsc", [128, 3072], BF16)
        t32 = sbuf("t32", [128, 1024], I32)
        smf = sbuf("smf", [128, 272], F32)
        ones_t = sbuf("ones_t", [128, 128], F32)
        wkuT = sbuf("wkuT", [LR, DO], BF16)
        wvuT = sbuf("wvuT", [LR, DO], BF16)
        p1b = sbuf("p1b", [128, 4096], BF16)
        psF = ctx.enter_context(nc.psum_tensor("psF", [128, 3584], F32))
        psT0 = ctx.enter_context(nc.psum_tensor("psT0", [128, 1024], BF16))
        bank = lambda i: psF[:, i * 512:(i + 1) * 512]
        trB = psF[:, 3072:3584].bitcast(BF16)
        BKap = lambda t: psT0[:, 0:1024] if t == 0 else trB
        _tb = [0]

        def nb():
            t = _tb[0]
            _tb[0] ^= 1
            return t
        pv_ps = lambda sub, a, b: psF[:, 1536 + sub * 512 + a: 1536 + sub * 512 + b]

        b_ident, b_ones, b_eps = Buf(), Buf(), Buf()
        b_arA = Buf()
        b_qT, b_kT, b_vA = Buf(), Buf(), Buf()
        b_wqst = [Buf(), Buf()]
        b_pT = [Buf(), Buf(), Buf()]
        b_atto = Buf()
        b_aoTh = [Buf(), Buf()]
        b_wuqT = Buf()
        b_wg8 = [Buf() for _ in range(8)]
        b_xnqst = [Buf(), Buf()]
        b_hqneg = [Buf() for _ in range(4)]
        b_woT = Buf()
        b_hqst = [Buf(), Buf()]
        b_aost = [Buf(), Buf()]
        b_xst = [Buf(), Buf()]
        b_xn32 = [Buf(), Buf()]
        b_h32 = [[Buf() for _ in range(8)] for _ in range(2)]
        b_nn, b_ee, b_qq = Buf(), Buf(), Buf()
        b_sil = [Buf(), Buf(), Buf(), Buf()]
        b_t1 = Buf()
        b_x4 = [Buf(), Buf()]
        b_t32 = Buf()
        b_stp = [[Buf() for _ in range(4)] for _ in range(2)]
        b_wstA = [Buf(), Buf(), Buf()]
        b_sxv2 = [Buf(), Buf()]
        b_sgv, b_suv, b_sdv, b_mu = Buf(), Buf(), Buf(), Buf()
        b_scw3 = [Buf(), Buf(), Buf()]
        b_rsx, b_rsh, b_bvec, b_repl, b_acc = Buf(), Buf(), Buf(), Buf(), Buf()
        b_kvT, b_wkvT, b_wkuT, b_wvuT = Buf(), Buf(), Buf(), Buf()
        b_p1r = [Buf(), Buf(), Buf(), Buf()]
        b_bank = [Buf() for _ in range(7)]
        b_psTh = [Buf(), Buf()]
        b_d = {n: Buf() for n in
               ("xnqT", "aoT", "hqT", "wqT", "woT", "wgT", "wuT", "wdT")}

        rsx = smf[:, 0:32]
        rsh = smf[:, 32:64]
        bvec = smf[:, 64:68]
        repl = smf[:, 68:72]
        stp = [[smf[:, o + i: o + i + 1] for i in range(4)] for o in (72, 76)]
        epsv = smf[:, 80:81]
        sxv2 = [smf[:, 81 + i: 82 + i] for i in range(2)]
        sgv = smf[:, 83:84]
        suv = smf[:, 84:85]
        sdv = smf[:, 85:86]
        scw3 = [smf[:, 86 + i: 87 + i] for i in range(3)]
        mu11 = smf[:, 89:90]
        acc32 = smf[:, 96:112]
        sgall = smf[:, 208:240]
        suall = smf[:, 240:272]
        sdall = acc32
        sdall = smf[:, 112:144]
        b_sdall = Buf()
        b_sgall, b_suall = Buf(), Buf()
        accA_sl = lambda s: smf[:, 112 + s: 113 + s]
        b_accA = Buf()
        wstA = [arC[:, 4096 + b * 1024: 4096 + (b + 1) * 1024] for b in (0, 1, 2)]

        xnT_sl = lambda dc, t0, n: arA[:, dc * T + t0: dc * T + t0 + n]
        wgqT_sl = lambda dc, i0, n: arA[:, dc * NI + i0: dc * NI + i0 + n]
        woT_sl = lambda oc, n0, n: arA[:, oc * D + n0: oc * D + n0 + n]
        hq_st = [arA[:, 16384 + b * 4096: 16384 + (b + 1) * 4096] for b in (0, 1)]
        ao_st = [arA[:, 24576 + b * 2048: 24576 + (b + 1) * 2048] for b in (0, 1)]

        qT = arB[:, 0:8192]
        kT = arB[:, 8192:16384]
        vA = arB[:, 16384:24608]
        wqst = [arB[:, 24608 + b * 2048: 24608 + (b + 1) * 2048] for b in (0, 1)]
        pT = [arB[:, 28704 + b * 512: 28704 + (b + 1) * 512] for b in range(3)]
        att_o = arB[:, 30240:31264]
        wkvT = arB[:, 30240:30752]
        aoTh = [arB[:, 31264 + b * 1024: 31264 + (b + 1) * 1024] for b in (0, 1)]
        kvT = arB[0:LR, 33312:37408]
        wuqT_sl = lambda dc, i0, n: arB[:, dc * NI + i0: dc * NI + i0 + n]
        wdqT_sl = lambda ic, n0, n: arB[:, ic * D + n0: ic * D + n0 + n]
        xnq_st = [arB[:, 32768 + b * 1024: 32768 + (b + 1) * 1024] for b in (0, 1)]
        hq_neg = arB[:, 34816:38912]

        xst = [arC[:, b * 1024:(b + 1) * 1024] for b in (0, 1)]
        xn32p = [arC[:, 2048 + b * 1024: 2048 + (b + 1) * 1024] for b in (0, 1)]
        h32p = [arC[:, b * 4096:(b + 1) * 4096] for b in (0, 1)]
        x4st = [arC[:, 2048 + b * 1024: 2048 + (b + 1) * 1024] for b in (0, 1)]
        out32 = [arC[:, b * 1024:(b + 1) * 1024] for b in (0, 1)]
        nn32 = rsc[:, 0:1024]
        ee32 = rsc[:, 1024:2048]
        qq32 = rsc[:, 2048:3072]
        t1_32 = arC[:, 4096:5120]
        sil32 = [arC[:, 8192 + b * 512: 8192 + (b + 1) * 512] for b in range(4)]

        p.add("sync", lambda e: e.dma_start(out=ident[:], in_=id_d),
              writes=[b_ident], dma=True)
        p.add("vector", lambda e: e.memset(epsv, 1e-5), writes=[b_eps])
        p.add("vector", lambda e: e.memset(ones_t[:, :], 1.0), writes=[b_ones])
        p.add("gpsimd", lambda e: e.dma_start(out=p1b[0:LR, 0:D], in_=wkv_d),
              writes=[b_p1r[0], b_p1r[1], b_p1r[2], b_p1r[3]], dma=True)
        for dc in range(NDC):
            p.add("tensor", lambda e, dc=dc: e.transpose(
                psT0[:, dc * 128: dc * 128 + LR],
                p1b[0:LR, dc * 128:(dc + 1) * 128], ident[0:LR, 0:LR]),
                reads=[b_p1r[dc // 2], b_ident], writes=[b_psTh[0]])
        for dc in range(NDC):
            p.add("vector", lambda e, dc=dc: e.tensor_copy(
                wkvT[:, dc * LR:(dc + 1) * LR], psT0[:, dc * 128: dc * 128 + LR]),
                reads=[b_psTh[0]], writes=[b_wkvT])

        lodB = [arB[:, 8192 + b * 2048: 8192 + (b + 1) * 2048] for b in (0, 1, 2)]
        stgB = [arB[:, 16384 + b * 1024: 16384 + (b + 1) * 1024] for b in range(4)]
        _sg = [0]
        b_ldB = [Buf(), Buf(), Buf()]
        b_stB = [Buf(), Buf(), Buf(), Buf()]
        prep_tiles = []
        for r in range(DO // 128):
            prep_tiles.append(("up", wku_d, wkuT, b_wkuT, r, 0))
        for r in range(DO // 128):
            prep_tiles.append(("up", wvu_d, wvuT, b_wvuT, r, 0))
        for r in range(DO // 128):
            prep_tiles.append(("pl", wq_d, wqT_d, b_d["wqT"], r, 1024))
        for r in range(D // 128):
            prep_tiles.append(("pl", wo_d, woT_d, b_d["woT"], r, 2048))
        _pc = [0]
        _pd = [0]

        def emit_prep_dma():
            if _pd[0] >= len(prep_tiles):
                return
            i = _pd[0]
            _pd[0] += 1
            kind, srct, dst, b_dst, r, cols = prep_tiles[i]
            par = i % 3
            n = LR if kind == "up" else cols
            p.add("gpsimd", lambda e, r=r, srct=srct, par=par, n=n: e.dma_start(
                out=lodB[par][:, 0:n], in_=srct[r * 128:(r + 1) * 128, :]),
                writes=[b_ldB[par]], dma=True)

        def emit_prep_comp():
            if _pc[0] >= _pd[0] or _pc[0] >= len(prep_tiles):
                return
            i = _pc[0]
            _pc[0] += 1
            kind, srct, dst, b_dst, r, cols = prep_tiles[i]
            par = i % 3
            if kind == "up":
                tb = nb()
                p.add("tensor", lambda e, par=par, tb=tb: e.transpose(
                    BKap(tb)[0:LR, 0:128], lodB[par][:, 0:LR],
                    ident[:]),
                    reads=[b_ldB[par], b_ident], writes=[b_psTh[tb]])
                p.add("vector", lambda e, r=r, dst=dst, tb=tb: e.tensor_copy(
                    dst[:, r * 128:(r + 1) * 128],
                    BKap(tb)[0:LR, 0:128]),
                    reads=[b_psTh[tb]], writes=[b_dst])
                return
            for g in range(cols // 1024):
                tb = nb()
                for dc in range(8):
                    cc = g * 8 + dc
                    p.add("tensor", lambda e, cc=cc, dc=dc, par=par, tb=tb: e.transpose(
                        BKap(tb)[:, dc * 128:(dc + 1) * 128],
                        lodB[par][:, cc * 128:(cc + 1) * 128], ident[:]),
                        reads=[b_ldB[par], b_ident], writes=[b_psTh[tb]])
                sg = _sg[0] % 4
                _sg[0] += 1
                p.add("vector", lambda e, sg=sg, tb=tb: e.tensor_copy(
                    stgB[sg][:, :], BKap(tb)[:, :]),
                    reads=[b_psTh[tb]], writes=[b_stB[sg]])
                p.add("sync", lambda e, r=r, g=g, dst=dst, sg=sg: e.dma_start(
                    out=dst.rearrange("(cc pp) o -> pp cc o", pp=128)[:, g * 8:(g + 1) * 8, r * 128:(r + 1) * 128],
                    in_=stgB[sg][:, :].rearrange("pp (cc t) -> pp cc t", t=128)),
                    reads=[b_stB[sg]], writes=[b_dst], dma=True)

        def emit_wqst(h):
            wb = h % 2
            p.add("scalar" if h > 0 else "sync", lambda e, h=h, wb=wb: e.dma_start(
                out=wqst[wb][:, :].rearrange("pp (dc o) -> pp dc o", o=256),
                in_=wqT_d.rearrange("(dc pp) o -> pp dc o", pp=128)[:, :, h * 256:(h + 1) * 256]),
                reads=[b_d["wqT"]], writes=[b_wqst[wb]], dma=True)

        def emit_qgrp(h, oc, ts_):
            wb = h % 2
            i6 = oc * 8 + ts_
            g = (1 + i6 % 5) if h == 0 else i6 % 6
            vec = (i6 % 2 == 0) if h == 0 else True
            for dc in range(NDC):
                p.add("tensor", lambda e, oc=oc, ts_=ts_, dc=dc, g=g, wb=wb: e.matmul(
                    bank(g),
                    wqst[wb][:, dc * 256 + oc * 128: dc * 256 + (oc + 1) * 128],
                    xnT_sl(dc, ts_ * 512, 512),
                    start=(dc == 0), stop=(dc == NDC - 1)),
                    reads=[b_wqst[wb], b_arA], writes=[b_bank[g]])
            p.add("vector" if vec else "scalar",
                  (lambda e, oc=oc, ts_=ts_, g=g: e.tensor_copy(
                      qT[:, oc * T + ts_ * 512: oc * T + (ts_ + 1) * 512], bank(g)))
                  if vec else
                  (lambda e, oc=oc, ts_=ts_, g=g: e.activation(
                      qT[:, oc * T + ts_ * 512: oc * T + (ts_ + 1) * 512], bank(g),
                      AF.Identity, bias=0.0, scale=1.0)),
                  reads=[b_bank[g]], writes=[b_qT])

        def emit_kgrp(h, oc, ts_, first=False):
            i6 = oc * 8 + ts_
            g = (1 + i6 % 5) if h == 0 else i6 % 6
            vec = (i6 % 2 == 0)
            p.add("tensor", lambda e, oc=oc, ts_=ts_, g=g, h=h: e.matmul(
                bank(g), wkuT[:, h * 256 + oc * 128: h * 256 + (oc + 1) * 128],
                kvT[:, ts_ * 512:(ts_ + 1) * 512], start=True, stop=True),
                reads=[b_wkuT, b_kvT], writes=[b_bank[g]])
            p.add("vector" if vec else "scalar",
                  (lambda e, oc=oc, ts_=ts_, g=g: e.tensor_copy(
                      kT[:, oc * T + ts_ * 512: oc * T + (ts_ + 1) * 512], bank(g)))
                  if vec else
                  (lambda e, oc=oc, ts_=ts_, g=g: e.activation(
                      kT[:, oc * T + ts_ * 512: oc * T + (ts_ + 1) * 512], bank(g),
                      AF.Identity, bias=0.0, scale=1.0)),
                  reads=[b_bank[g]], writes=[b_kT],
                  extra=((b_ldB[0].handover() + b_ldB[1].handover() +
                          b_ldB[2].handover())
                         if (first or (h == 0 and oc == 1 and ts_ <= 3))
                         else ()))

        def emit_vmemset(h):
            p.add("vector", lambda e: e.memset(vA[:, :], 1.0), writes=[b_vA],
                  extra=((b_stB[0].handover() + b_stB[1].handover() +
                          b_stB[2].handover() + b_stB[3].handover())
                         if h == 0 else ()))

        def emit_vchunk(h, tk):
            g = (1 + tk % 5) if h == 0 else tk % 6
            vec = (tk % 2 == 0)
            p.add("tensor", lambda e, tk=tk, g=g, h=h: e.matmul(
                bank(g)[:, 0:256], kvT[:, tk * 128:(tk + 1) * 128],
                wvuT[:, h * 256:(h + 1) * 256], start=True, stop=True),
                reads=[b_kvT, b_wvuT], writes=[b_bank[g]])
            p.add("vector" if vec else "scalar",
                  (lambda e, tk=tk, g=g: e.tensor_copy(
                      vA[:, tk * 257: tk * 257 + 256], bank(g)[:, 0:256]))
                  if vec else
                  (lambda e, tk=tk, g=g: e.activation(
                      vA[:, tk * 257: tk * 257 + 256], bank(g)[:, 0:256],
                      AF.Identity, bias=0.0, scale=1.0)),
                  reads=[b_bank[g]], writes=[b_vA])

        def round5(src_ap, b_src, dst_ap, b_dst, sx, b_sx):
            p.add("vector", lambda e: e.tensor_scalar(
                t32[:, :].bitcast(F32), src_ap, sx, CM, op0=ALU.mult, op1=ALU.add),
                reads=[b_src, b_sx], writes=[b_t32])
            p.add("vector", lambda e: e.tensor_scalar(
                nn32[:, :], t32[:, :].bitcast(F32), -CM, None, op0=ALU.add),
                reads=[b_t32], writes=[b_nn])
            p.add("vector", lambda e: e.scalar_tensor_tensor(
                ee32[:, :], src_ap, sx, nn32[:, :], op0=ALU.mult, op1=ALU.subtract),
                reads=[b_src, b_sx, b_nn], writes=[b_ee])
            p.add("vector", lambda e: e.scalar_tensor_tensor(
                qq32[:, :], ee32[:, :], 0.5, nn32[:, :], op0=ALU.is_gt, op1=ALU.add),
                reads=[b_ee, b_nn], writes=[b_qq])
            p.add("vector", lambda e: e.scalar_tensor_tensor(
                dst_ap, ee32[:, :], -0.5, qq32[:, :], op0=ALU.is_lt, op1=ALU.subtract),
                reads=[b_ee, b_qq], writes=[b_dst])

        def round5_2(src_ap, b_srcs, dst_ap, b_dst, sx, b_sx, eng12="vector"):
            p.add(eng12, lambda e: e.tensor_scalar(
                t32[:, :].bitcast(F32), src_ap, sx, CM, op0=ALU.mult, op1=ALU.add),
                reads=b_srcs + [b_sx], writes=[b_t32])
            p.add(eng12, lambda e: e.tensor_scalar(
                nn32[:, :], t32[:, :].bitcast(F32), -CM, None, op0=ALU.add),
                reads=[b_t32], writes=[b_nn])
            p.add("vector", lambda e: e.scalar_tensor_tensor(
                ee32[:, :], src_ap, sx, nn32[:, :], op0=ALU.mult, op1=ALU.subtract),
                reads=b_srcs + [b_sx, b_nn], writes=[b_ee])
            p.add("vector", lambda e: e.scalar_tensor_tensor(
                qq32[:, :], ee32[:, :], 0.5, nn32[:, :], op0=ALU.is_gt, op1=ALU.add),
                reads=[b_ee, b_nn], writes=[b_qq])
            p.add("vector", lambda e: e.scalar_tensor_tensor(
                dst_ap, ee32[:, :], -0.5, qq32[:, :], op0=ALU.is_lt, op1=ALU.subtract),
                reads=[b_ee, b_qq], writes=[b_dst])

        def actscale(src_small, b_src, st, b_st, sx, b_sx):
            p.add("vector", lambda e: e.tensor_scalar(
                st[1], src_small, 1e-5, None, op0=ALU.max),
                reads=[b_src], writes=[b_st[1]])
            p.add("vector", lambda e: e.reciprocal(st[1], st[1]),
                  reads=[b_st[1]], writes=[b_st[1]])
            p.add("vector", lambda e: e.tensor_scalar(
                sx, st[1], 127.0, None, op0=ALU.mult),
                reads=[b_st[1]], writes=[b_sx])

        passA = []
        for wsrc, rows, cols, base in ((wg_d, NI, D, 0), (wu_d, NI, D, 32),
                                       (wd_d, D, NI, 64)):
            ncc = cols // 1024
            for i in range(rows // 128 * ncc):
                r, cb = divmod(i, ncc)
                passA.append((wsrc, r, cb, base + i))

        _pa_d = [0]
        _pa_c = [0]

        def emit_passA_dma():
            if _pa_d[0] >= len(passA):
                return
            j = _pa_d[0]
            _pa_d[0] += 1
            wsrc, r, cb, slot = passA[j]
            ab = j % 3
            p.add("scalar", lambda e, wsrc=wsrc, r=r, cb=cb, ab=ab: e.dma_start(
                out=wstA[ab][:, :], in_=wsrc[r * 128:(r + 1) * 128, cb * 1024:(cb + 1) * 1024]),
                writes=[b_wstA[ab]], dma=True)

        def emit_passA_abs():
            if _pa_c[0] >= _pa_d[0] or _pa_c[0] >= len(passA):
                return
            j = _pa_c[0]
            _pa_c[0] += 1
            wsrc, r, cb, slot = passA[j]
            ab = j % 3
            p.add("scalar", lambda e, slot=slot, ab=ab: e.activation(
                wstA[ab][:, :], wstA[ab][:, :], AF.Abs, accum_out=accA_sl(slot)),
                reads=[b_wstA[ab]], writes=[b_wstA[ab], b_accA])

        def emit_xnq_out(tc2):
            bq = tc2 % 2
            rq2 = b_p1r[bq]
            rs2 = b_p1r[2 + bq]
            tb = nb()
            for dc in range(NDC):
                p.add("tensor", lambda e, dc=dc, bq=bq, tb=tb: e.transpose(
                    BKap(tb)[:, dc * 128:(dc + 1) * 128],
                    p1b[:, bq * 1024 + dc * 128: bq * 1024 + (dc + 1) * 128],
                    ident[:]),
                    reads=[rq2, b_ident], writes=[b_psTh[tb]])
            p.add("scalar", lambda e, bq=bq, tb=tb: e.activation(
                p1b[:, (2 + bq) * 1024:(3 + bq) * 1024], BKap(tb)[:, :],
                AF.Identity, bias=0.0, scale=1.0),
                reads=[b_psTh[tb]], writes=[rs2])
            p.add("sync", lambda e, tc2=tc2, bq=bq: e.dma_start(
                out=xnqT_d.rearrange("(dc pp) t -> pp dc t", pp=128)[:, :, tc2 * 128:(tc2 + 1) * 128],
                in_=p1b[:, (2 + bq) * 1024:(3 + bq) * 1024].rearrange("pp (dc t) -> pp dc t", t=128)),
                reads=[rs2], writes=[b_d["xnqT"]], dma=True)

        def emit_kv(tc2):
            for dc in range(NDC):
                p.add("tensor", lambda e, dc=dc, tc2=tc2: e.matmul(
                    bank(0)[0:LR, 0:128], wkvT[:, dc * LR:(dc + 1) * LR],
                    xnT_sl(dc, tc2 * 128, 128),
                    start=(dc == 0), stop=(dc == NDC - 1)),
                    reads=[b_wkvT, b_arA], writes=[b_bank[0]])
            p.add("scalar", lambda e, tc2=tc2: e.activation(
                kvT[:, tc2 * 128:(tc2 + 1) * 128], bank(0)[0:LR, 0:128],
                AF.Identity, bias=0.0, scale=1.0),
                reads=[b_bank[0]], writes=[b_kvT])

        pre_q = [(oc_, ts__) for ts__ in range(8) for oc_ in range(2)]
        pre_k = [(oc_, ts__) for ts__ in range(8) for oc_ in range(2)]
        pre_v = list(range(NTC))
        for _ in range(3):
            emit_passA_dma()
        for _ in range(3):
            emit_prep_dma()
        for tc in range(NTC):
            bb = tc % 2
            xn32 = xn32p[bb]
            b_xn = b_xn32[bb]
            st, b_st = stp[bb], b_stp[bb]
            sxv, b_sxv = sxv2[bb], b_sxv2[bb]
            rq = b_p1r[bb]
            rs = b_p1r[2 + bb]
            if tc == 20:
                emit_wqst(0)
                emit_vmemset(0)
            if tc >= 21:
                for _ in range(2):
                    if pre_q and 4 * pre_q[0][1] + 3 < tc:
                        oc_, ts__ = pre_q.pop(0)
                        emit_qgrp(0, oc_, ts__)
                for _ in range(2):
                    if pre_k and 4 * pre_k[0][1] + 4 < tc:
                        oc_, ts__ = pre_k.pop(0)
                        emit_kgrp(0, oc_, ts__, first=(oc_ == 0 and ts__ == 0))
                for _ in range(4):
                    if pre_v and pre_v[0] + 1 < tc:
                        emit_vchunk(0, pre_v.pop(0))
            p.add("sync", lambda e, tc=tc, bb=bb: e.dma_start(
                out=xst[bb][:, :], in_=x_d[tc * 128:(tc + 1) * 128, :]),
                writes=[b_xst[bb]], dma=True)
            p.add("vector", lambda e, bb=bb, st=st: e.reduce_sum(
                st[0], xst[bb][:, :], axis=AX.X),
                reads=[b_xst[bb]], writes=[b_st[0]])
            p.add("scalar", lambda e, bb=bb, xn32=xn32, st=st: e.activation(
                xn32[:, :], xst[bb][:, :], AF.Square, accum_out=st[1]),
                reads=[b_xst[bb]], writes=[b_xn, b_st[1]])
            p.add("vector", lambda e, st=st: e.tensor_scalar(
                st[0], st[0], 1.0 / D, None, op0=ALU.mult),
                reads=[b_st[0]], writes=[b_st[0]])
            p.add("vector", lambda e, st=st: e.tensor_scalar(
                st[1], st[1], 1.0 / D, None, op0=ALU.mult),
                reads=[b_st[1]], writes=[b_st[1]])
            p.add("vector", lambda e, st=st: e.scalar_tensor_tensor(
                st[2], st[0], st[0], st[1], op0=ALU.mult, op1=ALU.subtract),
                reads=[b_st[0], b_st[1]], writes=[b_st[2]])
            p.add("scalar", lambda e, st=st: e.activation(
                st[3], st[2], AF.Sqrt, bias=epsv, scale=-1.0),
                reads=[b_st[2], b_eps], writes=[b_st[3]])
            p.add("vector", lambda e, st=st: e.reciprocal(st[3], st[3]),
                  reads=[b_st[3]], writes=[b_st[3]])
            p.add("vector", lambda e, st=st: e.tensor_scalar(
                st[2], st[3], st[0], -1.0, op0=ALU.mult, op1=ALU.mult),
                reads=[b_st[3], b_st[0]], writes=[b_st[2]])
            p.add("scalar", lambda e, bb=bb, st=st: e.activation(
                p1b[:, (2 + bb) * 1024:(3 + bb) * 1024], xst[bb][:, :],
                AF.Identity, bias=st[2], scale=st[3]),
                reads=[b_xst[bb], b_st[2], b_st[3]], writes=[rs])
            p.add("scalar", lambda e, bb=bb, xn32=xn32, st=st: e.activation(
                xn32[:, :], xst[bb][:, :], AF.Identity, bias=st[2], scale=st[3]),
                reads=[b_xst[bb], b_st[2], b_st[3]], writes=[b_xn])
            for _ in range(3):
                emit_passA_abs()
            for _ in range(3):
                emit_prep_comp()
            if tc >= 1:
                emit_kv(tc - 1)
            tb = nb()
            for dc in range(NDC):
                p.add("tensor", lambda e, dc=dc, bb=bb, tb=tb: e.transpose(
                    BKap(tb)[:, dc * 128:(dc + 1) * 128],
                    p1b[:, (2 + bb) * 1024 + dc * 128: (2 + bb) * 1024 + (dc + 1) * 128],
                    ident[:]),
                    reads=[rs, b_ident], writes=[b_psTh[tb]])
            p.add("vector", lambda e, tc=tc, tb=tb: e.tensor_copy(
                arA[:, :].rearrange("pp (dc t) -> pp dc t", t=T)[:, :, tc * 128:(tc + 1) * 128],
                BKap(tb)[:, :].rearrange("pp (dc t) -> pp dc t", t=128)),
                reads=[b_psTh[tb]], writes=[b_arA])
            if tc >= 1:
                emit_xnq_out(tc - 1)
            p.add("vector", lambda e, xn32=xn32, st=st: e.tensor_reduce(
                st[0], xn32[:, :], axis=AX.X, op=ALU.max, apply_absolute_value=True),
                reads=[b_xn], writes=[b_st[0]])
            p.add("vector", lambda e, tc=tc, st=st: e.tensor_scalar(
                rsx[:, tc:tc + 1], st[0], 1e-5, 1.0 / 127.0, op0=ALU.max, op1=ALU.mult),
                reads=[b_st[0]], writes=[b_rsx])
            actscale(st[0], b_st[0], st, b_st, sxv, b_sxv)
            round5(xn32[:, :], b_xn, p1b[:, bb * 1024:(bb + 1) * 1024], rq, sxv, b_sxv)
            for _ in range(3):
                emit_passA_dma()
            for _ in range(3):
                emit_prep_dma()
        emit_xnq_out(NTC - 1)
        emit_kv(NTC - 1)
        while _pa_c[0] < len(passA):
            emit_passA_abs()
            emit_passA_dma()
        while _pc[0] < len(prep_tiles):
            emit_prep_comp()
            emit_prep_dma()
        for oc_, ts__ in pre_q:
            emit_qgrp(0, oc_, ts__)
        for oc_, ts__ in pre_k:
            emit_kgrp(0, oc_, ts__, first=(oc_ == 0 and ts__ == 0))
        for tk_ in pre_v:
            emit_vchunk(0, tk_)
        if debug:
            p.add_final_wait(p.add("sync", lambda e: e.dma_start(
                out=xnT_dbg.rearrange("(dc pp) t -> pp dc t", pp=128),
                in_=arA[:, :].rearrange("pp (dc t) -> pp dc t", t=T)),
                reads=[b_arA], dma=True))
            p.add_final_wait(p.add("sync", lambda e: e.dma_start(
                out=kvT_dbg, in_=kvT[:, :]), reads=[b_kvT], dma=True))

        for bslot, (rows, cols, base) in enumerate(((NI, D, 0), (NI, D, 32), (D, NI, 64))):
            st = stp[0]
            p.add("vector", lambda e, base=base, st=st: e.reduce_sum(
                st[0], smf[:, 112 + base: 112 + base + 32], axis=AX.X),
                reads=[b_accA], writes=[b_stp[0][0]])
            p.add("tensor", lambda e, st=st: e.matmul(
                bank(5)[0:1, 0:1], ones_t[:, 0:1], st[0], start=True, stop=True),
                reads=[b_ones, b_stp[0][0]], writes=[b_bank[5]])
            p.add("vector", lambda e, rows=rows, cols=cols: e.tensor_scalar(
                mu11[0:1, :], bank(5)[0:1, 0:1], 1.0 / (rows * cols), 1e-5,
                op0=ALU.mult, op1=ALU.max),
                reads=[b_bank[5]], writes=[b_mu])
            p.add("tensor", lambda e: e.matmul(
                bank(5)[:, 0:1], ones_t[0:1, :], mu11[0:1, :], start=True, stop=True),
                reads=[b_ones, b_mu], writes=[b_bank[5]])
            p.add("vector", lambda e, bslot=bslot: e.tensor_copy(
                bvec[:, bslot:bslot + 1], bank(5)[:, 0:1]),
                reads=[b_bank[5]], writes=[b_bvec])
            p.add("vector", lambda e, bslot=bslot: e.reciprocal(
                scw3[bslot], bvec[:, bslot:bslot + 1]),
                reads=[b_bvec], writes=[b_scw3[bslot]])

        passB = []
        for bslot, (wsrc, rows, cols, dst, bd) in enumerate((
                (wg_d, NI, D, wgT_d, b_d["wgT"]), (wu_d, NI, D, wuT_d, b_d["wuT"]),
                (wd_d, D, NI, wdT_d, b_d["wdT"]))):
            ncc = cols // 1024
            for i in range(rows // 128 * ncc):
                r, cb = divmod(i, ncc)
                passB.append((wsrc, r, cb, dst, bd, bslot))

        def emit_passB1(j):
            wsrc, r, cb, dst, bd, bslot = passB[j]
            bb = j % 2
            ro = b_p1r[bb]
            p.add("gpsimd", lambda e, r=r, cb=cb, bb=bb, wsrc=wsrc: e.dma_start(
                out=xst[bb][:, :], in_=wsrc[r * 128:(r + 1) * 128, cb * 1024:(cb + 1) * 1024]),
                writes=[b_xst[bb]], dma=True)
            p.add("vector", lambda e, bb=bb, bslot=bslot: e.tensor_scalar(
                nn32[:, :], xst[bb][:, :], scw3[bslot], -0.5, op0=ALU.mult, op1=ALU.is_lt),
                reads=[b_xst[bb], b_scw3[bslot]], writes=[b_nn])
            p.add("vector", lambda e, bb=bb, bslot=bslot: e.tensor_scalar(
                ee32[:, :], xst[bb][:, :], scw3[bslot], 0.5, op0=ALU.mult, op1=ALU.is_gt),
                reads=[b_xst[bb], b_scw3[bslot]], writes=[b_ee])
            p.add("vector", lambda e, bb=bb: e.tensor_tensor(
                p1b[:, bb * 1024:(bb + 1) * 1024], nn32[:, :], ee32[:, :],
                op=ALU.subtract),
                reads=[b_nn, b_ee], writes=[ro])

        trB_guard = b_psTh[1].handover()
        for h in range(NH):
            if h > 0:
                emit_wqst(h)
                for oc in range(2):
                    for ts_ in range(8):
                        emit_qgrp(h, oc, ts_)
                for oc in range(2):
                    for ts_ in range(8):
                        emit_kgrp(h, oc, ts_)
            if h == NH - 1:
                arA_g = b_arA.handover()
                for isl_ in range(8):
                    p.add("sync", lambda e, isl_=isl_: e.dma_start(
                        out=arA[:, :].rearrange("pp (dc i) -> pp dc i", i=NI)[:, :, isl_ * 512:(isl_ + 1) * 512],
                        in_=wgT_d.rearrange("(dc pp) i -> pp dc i", pp=128)[:, :, isl_ * 512:(isl_ + 1) * 512]),
                        reads=[b_d["wgT"]], writes=[b_wg8[isl_]],
                        extra=arA_g, dma=True)
            if h > 0:
                emit_vmemset(h)
                for tk in range(NTC):
                    emit_vchunk(h, tk)
            def emit_qk(bi, c):
                g = c % 3
                for d2 in range(2):
                    p.add("tensor", lambda e, bi=bi, c=c, g=g, d2=d2: e.matmul(
                        bank(g), kT[:, d2 * T + c * 128: d2 * T + (c + 1) * 128],
                        qT[:, d2 * T + bi * 512: d2 * T + (bi + 1) * 512],
                        start=(d2 == 0), stop=(d2 == 1)),
                        reads=[b_kT, b_qT], writes=[b_bank[g]])
                p.add("scalar", lambda e, g=g, c=c: e.activation(
                    pT[c % 3][:, :], bank(g), AF.Exp, bias=0.0, scale=ISCL),
                    reads=[b_bank[g]], writes=[b_pT[c % 3]])

            def emit_pv(c):
                for sub in range(4):
                    p.add("tensor", lambda e, c=c, sub=sub: e.matmul(
                        pv_ps(sub, 0, 257), pT[c % 3][:, sub * 128:(sub + 1) * 128],
                        vA[:, c * 257:(c + 1) * 257],
                        start=(c == 0), stop=(c == NTC - 1)),
                        reads=[b_pT[c % 3], b_vA], writes=[b_bank[3 + sub]],
                        extra=(trB_guard if h == 0 and c == 0 and sub == 3 else ()))

            def finalize_recips():
                for sub in range(4):
                    p.add("vector", lambda e, sub=sub: e.reciprocal(
                        repl[:, sub:sub + 1], pv_ps(sub, 256, 257)),
                        reads=[b_bank[3 + sub]], writes=[b_repl])

            def finalize_atto(bi):
                for sub in range(4):
                    p.add("scalar", lambda e, sub=sub: e.activation(
                        att_o[:, sub * 256:(sub + 1) * 256], pv_ps(sub, 0, 256),
                        AF.Identity, scale=repl[:, sub:sub + 1]),
                        reads=[b_bank[3 + sub], b_repl], writes=[b_atto],
                        extra=(b_wkvT.handover()
                               if h == 0 and bi == 0 and sub == 0 else ()))

            def trans_tasks(bi):
                pb = 0 if (h == NH - 1 and bi == 7) else bi % 2
                for half in range(2):
                    for sub in range(4):
                        yield lambda sub=sub, half=half: p.add(
                            "tensor", lambda e, sub=sub, half=half: e.transpose(
                                psT0[:, (half * 4 + sub) * 128:(half * 4 + sub + 1) * 128],
                                att_o[:, sub * 256 + half * 128: sub * 256 + half * 128 + 128],
                                ident[:]),
                            reads=[b_atto, b_ident], writes=[b_psTh[0]])

                def cpdma(pb=pb, bi=bi):
                    p.add("vector", lambda e, pb=pb: e.tensor_copy(
                        aoTh[pb][:, :], psT0[:, 0:1024]),
                        reads=[b_psTh[0]], writes=[b_aoTh[pb]])
                    p.add("sync", lambda e, h=h, bi=bi, pb=pb: e.dma_start(
                        out=aoT_d[h * 256:(h + 1) * 256, bi * 512:(bi + 1) * 512]
                        .rearrange("(ch pp) t -> pp ch t", pp=128),
                        in_=aoTh[pb][:, :].rearrange("pp (ch t) -> pp ch t", t=512)),
                        reads=[b_aoTh[pb]], writes=[b_d["aoT"]], dma=True)
                yield cpdma
                gi = h * 8 + bi
                for j in range(gi * 3 // 2, min((gi + 1) * 3 // 2, len(passB))):
                    for dc in range(8):
                        yield lambda j=j, dc=dc: _passB2_tr(j, dc)
                    yield lambda j=j: _passB2_fin(j)

            def _passB2_tr(j, dc):
                wsrc, r, cb, dst, bd, bslot = passB[j]
                bb = j % 2
                ro = b_p1r[bb]
                p.add("tensor", lambda e, dc=dc, bb=bb: e.transpose(
                    psT0[:, dc * 128:(dc + 1) * 128],
                    p1b[:, bb * 1024 + dc * 128: bb * 1024 + (dc + 1) * 128], ident[:]),
                    reads=[ro, b_ident], writes=[b_psTh[0]])

            def _passB2_fin(j):
                wsrc, r, cb, dst, bd, bslot = passB[j]
                bb = j % 2
                rsb = b_p1r[2 + bb]
                p.add("vector", lambda e, bb=bb: e.tensor_copy(
                    p1b[:, (2 + bb) * 1024:(3 + bb) * 1024], psT0[:, 0:1024]),
                    reads=[b_psTh[0]], writes=[rsb])
                p.add("gpsimd", lambda e, r=r, cb=cb, dst=dst, bb=bb: e.dma_start(
                    out=dst.rearrange("(cc pp) o -> pp cc o", pp=128)[:, cb * 8:(cb + 1) * 8, r * 128:(r + 1) * 128],
                    in_=p1b[:, (2 + bb) * 1024:(3 + bb) * 1024].rearrange("pp (cc t) -> pp cc t", t=128)),
                    reads=[rsb], writes=[bd], dma=True)

            for bi in range(8):
                tq = list(trans_tasks(bi - 1)) if bi >= 1 else []
                ti = 0
                for c in range(NTC):
                    emit_qk(bi, c)
                    if c == 1 and bi >= 1:
                        finalize_atto(bi - 1)
                    if c >= 4:
                        for _ in range(1):
                            if ti < len(tq):
                                tq[ti]()
                                ti += 1
                    if c >= 2:
                        emit_pv(c - 2)
                while ti < len(tq):
                    tq[ti]()
                    ti += 1
                emit_pv(NTC - 2)
                emit_pv(NTC - 1)
                finalize_recips()
                if bi == 7:
                    finalize_atto(7)
                gi0 = h * 8 + bi
                for j in range(gi0 * 3 // 2, min((gi0 + 1) * 3 // 2, len(passB))):
                    emit_passB1(j)
            for fn in trans_tasks(7):
                fn()
        tok_wu = None
        b_wu8h = [[Buf(), Buf()] for _ in range(8)]
        xnq_guard = b_kvT.handover() + b_aoTh[1].handover()
        guard_wu = (b_qT.handover() + b_kT.handover() + b_vA.handover() +
                    b_wqst[0].handover() + b_wqst[1].handover() +
                    b_pT[0].handover() + b_pT[1].handover() + b_pT[2].handover() +
                    b_kvT.handover() + b_aoTh[0].handover() + b_aoTh[1].handover() +
                    b_atto.handover())
        def emit_wu_loads():
            tok = None
            for isl in range(8):
                tok = p.add("scalar" if isl % 2 == 0 else "sync",
                            lambda e, isl=isl: e.dma_start(
                    out=arB[:, 0:32768].rearrange("pp (dc i) -> pp dc i", i=NI)[:, :, isl * 512:(isl + 1) * 512],
                    in_=wuT_d.rearrange("(dc pp) i -> pp dc i", pp=128)[:, :, isl * 512:(isl + 1) * 512]),
                    reads=[b_d["wuT"]], writes=[b_wu8h[isl][0]],
                    extra=guard_wu, dma=True)
            return tok
        h32_guard = (b_xst[0].handover() + b_xst[1].handover() +
                     b_xn32[0].handover() + b_xn32[1].handover() +
                     b_wstA[0].handover() + b_wstA[1].handover() +
                     b_wstA[2].handover())

        bank6_guard = b_bank[6].handover()
        _p3b1 = [False]

        def emit_trans_group(tc, g):
            tb = g % 2
            ex = ()
            if tb == 1 and not _p3b1[0]:
                _p3b1[0] = True
                ex = bank6_guard
            for dc in range(8):
                ic = g * 8 + dc
                p.add("tensor", lambda e, ic=ic, dc=dc, tb=tb: e.transpose(
                    BKap(tb)[:, dc * 128:(dc + 1) * 128],
                    hq_neg[:, ic * 128:(ic + 1) * 128], ident[:]),
                    reads=[b_hqneg[g], b_ident], writes=[b_psTh[tb]],
                    extra=(ex if dc == 0 else ()))
            p.add("scalar", lambda e, g=g, tb=tb: e.activation(
                p1b[:, g * 1024:(g + 1) * 1024], BKap(tb)[:, :],
                AF.Identity, bias=0.0, scale=1.0),
                reads=[b_psTh[tb]], writes=[b_p1r[g]])

        def emit_hq_dma(tc):
            p.add("sync", lambda e, tc=tc: e.dma_start(
                out=hqT_d.rearrange("(ic pp) t -> pp ic t", pp=128)[:, :, tc * 128:(tc + 1) * 128],
                in_=p1b[:, :].rearrange("pp (ic t) -> pp ic t", t=128)),
                reads=[b_p1r[0], b_p1r[1], b_p1r[2], b_p1r[3]],
                writes=[b_d["hqT"]], dma=True)

        def xnq_load(tc):
            bb = tc % 2
            p.add("sync", lambda e, tc=tc, bb=bb: e.dma_start(
                out=xnq_st[bb][:, :].rearrange("pp (dc t) -> pp dc t", t=128),
                in_=xnqT_d.rearrange("(dc pp) t -> pp dc t", pp=128)[:, :, tc * 128:(tc + 1) * 128]),
                reads=[b_d["xnqT"]], writes=[b_xnqst[bb]],
                extra=(xnq_guard if tc < 2 else ()), dma=True)

        p.add("vector", lambda e: e.tensor_scalar(
            sgall[:, :], rsx[:, :], bvec[:, 0:1], None, op0=ALU.mult),
            reads=[b_rsx, b_bvec], writes=[b_sgall])
        p.add("vector", lambda e: e.tensor_scalar(
            suall[:, :], rsx[:, :], bvec[:, 1:2], None, op0=ALU.mult),
            reads=[b_rsx, b_bvec], writes=[b_suall])
        xnq_load(0)
        xnq_load(1)
        tok_wu = emit_wu_loads()
        for b_ in b_hqneg:
            b_.w = tok_wu
        for tc in range(NTC):
            bb = tc % 2
            h32 = h32p[bb]
            b_hs = b_h32[bb]
            st, b_st = stp[bb], b_stp[bb]
            sxv, b_sxv = sxv2[bb], b_sxv2[bb]
            for isl in range(8):
                gg, gu = isl % 2, 2 + isl % 2
                for dc in range(NDC):
                    p.add("tensor", lambda e, isl=isl, dc=dc, gg=gg, bb=bb: e.matmul(
                        bank(gg), xnq_st[bb][:, dc * 128:(dc + 1) * 128],
                        wgqT_sl(dc, isl * 512, 512),
                        start=(dc == 0), stop=(dc == NDC - 1)),
                        reads=[b_xnqst[bb], b_wg8[isl]], writes=[b_bank[gg]])
                for dc in range(NDC):
                    p.add("tensor", lambda e, isl=isl, dc=dc, gu=gu, bb=bb: e.matmul(
                        bank(gu), xnq_st[bb][:, dc * 128:(dc + 1) * 128],
                        wuqT_sl(dc, isl * 512, 512),
                        start=(dc == 0), stop=(dc == NDC - 1)),
                        reads=[b_xnqst[bb], b_wu8h[isl][0]],
                        writes=[b_bank[gu]])
                p.add("scalar", lambda e, isl=isl, gu=gu, h32=h32, tc=tc: e.activation(
                    h32[:, isl * 512:(isl + 1) * 512], bank(gu), AF.Identity,
                    scale=suall[:, tc:tc + 1]),
                    reads=[b_bank[gu], b_suall], writes=[b_hs[isl]],
                    extra=(h32_guard if tc < 2 and isl == 0 else ()))
                p.add("scalar", lambda e, gg=gg, isl=isl, tc=tc: e.activation(
                    sil32[isl % 4][:, :], bank(gg), AF.Silu, scale=sgall[:, tc:tc + 1]),
                    reads=[b_bank[gg], b_sgall], writes=[b_sil[isl % 4]])
                p.add("gpsimd", lambda e, isl=isl, h32=h32: e.tensor_tensor(
                    h32[:, isl * 512:(isl + 1) * 512], sil32[isl % 4][:, :],
                    h32[:, isl * 512:(isl + 1) * 512], op=ALU.mult),
                    reads=[b_sil[isl % 4]], writes=[b_hs[isl]])
                p.add("vector", lambda e, isl=isl, h32=h32, bb=bb: e.tensor_reduce(
                    acc32[:, bb * 8 + isl: bb * 8 + isl + 1],
                    h32[:, isl * 512:(isl + 1) * 512], axis=AX.X, op=ALU.max,
                    apply_absolute_value=True),
                    reads=[b_hs[isl]], writes=[b_acc])
            p.add("vector", lambda e, st=st, bb=bb: e.tensor_reduce(
                st[0], acc32[:, bb * 8: bb * 8 + 8], axis=AX.X, op=ALU.max),
                reads=[b_acc], writes=[b_st[0]])
            p.add("vector", lambda e, tc=tc, st=st: e.tensor_scalar(
                rsh[:, tc:tc + 1], st[0], 1e-5, 1.0 / 127.0, op0=ALU.max, op1=ALU.mult),
                reads=[b_st[0]], writes=[b_rsh])
            if tc + 2 < NTC:
                xnq_load(tc + 2)
            if tc >= 1:
                for g_ in range(4):
                    emit_trans_group(tc - 1, g_)
                emit_hq_dma(tc - 1)
            actscale(st[0], b_st[0], st, b_st, sxv, b_sxv)
            for sl in range(4):
                round5_2(h32[:, sl * 1024:(sl + 1) * 1024],
                         [b_hs[2 * sl], b_hs[2 * sl + 1]],
                         hq_neg[:, sl * 1024:(sl + 1) * 1024], b_hqneg[sl], sxv, b_sxv)
        for g in range(4):
            emit_trans_group(NTC - 1, g)
        emit_hq_dma(NTC - 1)

        b_wd2 = [[Buf() for _ in range(4)], [Buf() for _ in range(4)]]
        b_wo2 = [Buf(), Buf()]
        guard_wd = (b_xnqst[0].handover() + b_xnqst[1].handover() +
                    [d for b_ in b_hqneg for d in b_.handover()] +
                    [d for pr in b_wu8h for bb_ in pr for d in bb_.handover()])
        guard_wo = (b_arA.handover() +
                    [d for bb_ in b_wg8 for d in bb_.handover()])
        arA_guard = guard_wo
        h32_guard2 = [d for par in range(2) for bb_ in b_h32[par] for d in bb_.handover()]
        qq_guard = b_qq.handover()

        def emit_x_load(tc, bb, eng):
            p.add(eng, lambda e, tc=tc, bb=bb: e.dma_start(
                out=x4st[bb][:, :], in_=x_d[tc * 128:(tc + 1) * 128, :]),
                writes=[b_x4[bb]],
                extra=((h32_guard2 + b_xn32[bb].handover()) if tc < 2 else ()),
                dma=True)

        def emit_hqao_load(tc, bb):
            p.add("sync", lambda e, tc=tc, bb=bb: e.dma_start(
                out=hq_st[bb][:, :].rearrange("pp (ic t) -> pp ic t", t=128),
                in_=hqT_d.rearrange("(ic pp) t -> pp ic t", pp=128)[:, :, tc * 128:(tc + 1) * 128]),
                reads=[b_d["hqT"]], writes=[b_hqst[bb]],
                extra=(arA_guard if tc < 2 else ()), dma=True)
            p.add("sync", lambda e, tc=tc, bb=bb: e.dma_start(
                out=ao_st[bb][:, :].rearrange("pp (oc t) -> pp oc t", t=128),
                in_=aoT_d.rearrange("(oc pp) t -> pp oc t", pp=128)[:, :, tc * 128:(tc + 1) * 128]),
                reads=[b_d["aoT"]], writes=[b_aost[bb]],
                extra=(arA_guard if tc < 2 else ()), dma=True)

        for n_ in range(2):
            p.add("scalar", lambda e, n_=n_: e.dma_start(
                out=arA[:, 0:16384].rearrange("pp (oc n) -> pp oc n", n=D)[:, :, n_ * 512:(n_ + 1) * 512],
                in_=woT_d.rearrange("(oc pp) n -> pp oc n", pp=128)[:, :, n_ * 512:(n_ + 1) * 512]),
                reads=[b_d["woT"]], writes=[b_wo2[n_]],
                extra=guard_wo, dma=True)
        for n_ in range(2):
            for q_ in range(4):
                p.add("gpsimd" if n_ == 0 else "scalar",
                      lambda e, n_=n_, q_=q_: e.dma_start(
                    out=arB[:, 0:32768].rearrange("pp (ic n) -> pp ic n", n=D)[:, q_ * 8:(q_ + 1) * 8, n_ * 512:(n_ + 1) * 512],
                    in_=wdT_d.rearrange("(ic pp) n -> pp ic n", pp=128)[:, q_ * 8:(q_ + 1) * 8, n_ * 512:(n_ + 1) * 512]),
                    reads=[b_d["wdT"]], writes=[b_wd2[n_][q_]],
                    extra=guard_wd, dma=True)
        p.add("vector", lambda e: e.tensor_scalar(
            sdall[:, :], rsh[:, :], bvec[:, 2:3], None, op0=ALU.mult),
            reads=[b_rsh, b_bvec], writes=[b_sdall], extra=b_accA.handover())
        bkk = lambda i: bank(i) if i < 7 else psT0[:, 0:1024].bitcast(F32)
        b_bkl = lambda i: [b_bank[i]] if i < 7 else [b_psTh[0]]
        trB_guard2 = b_psTh[1].handover()
        dfin = None
        for tc in range(NTC):
            bb = tc % 2
            dbk = (0, 1) if bb == 0 else (4, 5)
            wbk = (2, 3) if bb == 0 else (6, 7)
            emit_hqao_load(tc, bb)
            emit_x_load(tc, bb, "sync" if tc < 2 else "scalar")
            for n_ in range(2):
                for oc in range(16):
                    p.add("tensor", lambda e, oc=oc, n_=n_, bb=bb, wbk=wbk: e.matmul(
                        bkk(wbk[n_]), ao_st[bb][:, oc * 128:(oc + 1) * 128],
                        woT_sl(oc, n_ * 512, 512),
                        start=(oc == 0), stop=(oc == 15)),
                        reads=[b_aost[bb], b_wo2[n_]], writes=b_bkl(wbk[n_]),
                        extra=(trB_guard2 if tc == 1 and n_ == 0 and oc == 0 else ()))
                p.add("vector", lambda e, n_=n_, bb=bb, wbk=wbk: e.scalar_tensor_tensor(
                    out32[bb][:, n_ * 512:(n_ + 1) * 512], bkk(wbk[n_]), 1.0,
                    x4st[bb][:, n_ * 512:(n_ + 1) * 512],
                    op0=ALU.mult, op1=ALU.add),
                    reads=b_bkl(wbk[n_]) + [b_x4[bb]], writes=[b_xst[bb]],
                    extra=(h32_guard2 if tc < 2 and n_ == 0 else ()))
            for n_ in range(2):
                for ic in range(32):
                    p.add("tensor", lambda e, ic=ic, n_=n_, bb=bb, dbk=dbk: e.matmul(
                        bank(dbk[n_]), hq_st[bb][:, ic * 128:(ic + 1) * 128],
                        wdqT_sl(ic, n_ * 512, 512),
                        start=(ic == 0), stop=(ic == 31)),
                        reads=[b_hqst[bb], b_wd2[n_][ic // 8]],
                        writes=[b_bank[dbk[n_]]])
            p.add("scalar", lambda e, tc=tc, bb=bb, dbk=dbk: e.activation(
                t1_32[:, :], psF[:, dbk[0] * 512:(dbk[0] + 2) * 512],
                AF.Identity, scale=sdall[:, tc:tc + 1]),
                reads=[b_bank[dbk[0]], b_bank[dbk[1]], b_sdall], writes=[b_t1],
                extra=((qq_guard + h32_guard2) if tc == 0 else ()))
            p.add("vector", lambda e, bb=bb: e.tensor_tensor(
                out32[bb][:, :], out32[bb][:, :], t1_32[:, :], op=ALU.add),
                reads=[b_xst[bb], b_t1], writes=[b_xst[bb]])
            dfin = p.add("scalar", lambda e, tc=tc, bb=bb: e.dma_start(
                out=y_d[tc * 128:(tc + 1) * 128, :], in_=out32[bb][:, :]),
                reads=[b_xst[bb]], dma=True)
        p.add_final_wait(dfin)
        if debug:
            p.add_final_wait(p.add("sync", lambda e: e.dma_start(
                out=sc_dbg, in_=smf[:, 0:128]),
                reads=[b_rsx, b_rsh, b_bvec], dma=True))

        blk = ctx.enter_context(nc.Block())
        p.emit(nc, blk, ctx)
    return nc


def kernel(L0_attn_x, L0_attn_Wq, L0_attn_Wkv_down, L0_attn_Wk_up,
           L0_attn_Wv_up, L0_attn_Wo, L0_ffn_w_gate, L0_ffn_w_up,
           L0_ffn_w_down):
    nc = build_nc()
    ident = np.eye(128, dtype=ml_dtypes.bfloat16)
    x = np.ascontiguousarray(np.asarray(L0_attn_x, dtype=np.float32))
    common = {
        "Wq": np.ascontiguousarray(np.asarray(L0_attn_Wq, dtype=np.float32)),
        "Wkv": np.ascontiguousarray(np.asarray(L0_attn_Wkv_down, dtype=np.float32)),
        "Wku": np.ascontiguousarray(np.asarray(L0_attn_Wk_up, dtype=np.float32)),
        "Wvu": np.ascontiguousarray(np.asarray(L0_attn_Wv_up, dtype=np.float32)),
        "Wo": np.ascontiguousarray(np.asarray(L0_attn_Wo, dtype=np.float32)),
        "Wg": np.ascontiguousarray(np.asarray(L0_ffn_w_gate, dtype=np.float32)),
        "Wu": np.ascontiguousarray(np.asarray(L0_ffn_w_up, dtype=np.float32)),
        "Wd": np.ascontiguousarray(np.asarray(L0_ffn_w_down, dtype=np.float32)),
        "ident": ident,
    }
    in_maps = [dict(common, x=x[c]) for c in range(8)]
    res = run_bass_kernel_spmd(nc, in_maps, core_ids=list(range(8)))
    return np.stack([np.asarray(r["y"], dtype=np.float32) for r in res.results],
                    axis=0)
